# version 06
# baseline (speedup 1.0000x reference)
import contextlib
import numpy as np
import ml_dtypes
import concourse.bass as bass
import concourse.mybir as mybir
from concourse.bass_utils import run_bass_kernel_spmd

F32 = mybir.dt.float32
BF16 = mybir.dt.bfloat16
I32 = mybir.dt.int32
AF = mybir.ActivationFunctionType
ALU = mybir.AluOpType
BFNP = ml_dtypes.bfloat16

S = 2048
D = 2048
H = 8
DQK = 128
DV = 256
FF = 8192
NCH = 16
NRB = 16
EPS1 = 1e-5
EPS2 = 1e-6
CAPV = 15.0
LOG_ISQ = float(np.log(1.0 / np.sqrt(float(DQK))))

ENGS = ("sync", "gpsimd", "scalar", "vector", "tensor")


class Bld:
    def __init__(self, nc):
        self.nc = nc
        self.ops = {e: [] for e in ENGS}
        self.cnt = {e: 0 for e in ENGS}
        self.tail_inc = {e: False for e in ENGS}
        self.sems = {}

    def alloc_sems(self, es):
        for e in ENGS:
            self.sems[e] = es.enter_context(self.nc.semaphore("sem_" + e))
        for e in ("sync", "gpsimd"):
            self.sems[e + "_m"] = es.enter_context(self.nc.semaphore("sem_" + e + "_m"))
            self.cnt[e + "_m"] = 0
        self._ns = 0

    def dsem(self, stack):
        self._ns += 1
        tok = [stack.enter_context(self.nc.semaphore(f"ds{self._ns}")), 0]
        if not hasattr(self, "all_toks"):
            self.all_toks = []
        self.all_toks.append(tok)
        return tok

    def dma(self, q, tok, fn):
        tok[1] += 16

        def thunk(eng, fn=fn, sem=tok[0], tgt=tok[1]):
            fn(eng).then_inc(sem, 16)

        self.ops[q].append(thunk)
        self.tail_inc[q] = False
        return tok[1]

    def wait_tok(self, e, tok, val=None):
        if val is None:
            val = tok[1]
        if val <= 0:
            return
        self.ops[e].append(
            lambda eng, sem=tok[0], val=val: eng.wait_ge(sem, val))

    def do(self, e, fn):
        step = 16 if e in ("sync", "gpsimd") else 1
        self.cnt[e] += step

        def thunk(eng, fn=fn, e=e, step=step):
            fn(eng).then_inc(self.sems[e], step)

        self.ops[e].append(thunk)
        self.tail_inc[e] = True
        return self.cnt[e]

    def do_(self, e, fn):
        self.ops[e].append(fn)
        self.tail_inc[e] = False

    def wait(self, e, other, val):
        if val <= 0:
            return
        self.ops[e].append(
            lambda eng, other=other, val=val: eng.wait_ge(self.sems[other], val)
        )
        self.tail_inc[e] = False

    def mark(self, e):
        if e not in ("sync", "gpsimd") and self.tail_inc[e]:
            return self.cnt[e]
        if e in ("sync", "gpsimd"):
            self.cnt[e + "_m"] += 1

            def thunk(eng, e=e):
                eng.nop().then_inc(self.sems[e + "_m"], 1)

            self.ops[e].append(thunk)
            return self.cnt[e + "_m"]
        return self.do(e, lambda eng: eng.nop())

    def flush(self, e):
        self.wait(e, e, self.cnt[e])

    def s_settle(self):
        if getattr(self, "last_settle", 0):
            self.wait("scalar", "scalar", self.last_settle)
        self.last_settle = self.do("scalar", lambda eng: eng.activation(
            out=self.settle_dst[:], in_=self.settle_dst[:], func=AF.Abs))
        return self.last_settle

    def barrier(self):
        self.s_settle()
        for tok in getattr(self, "all_toks", []):
            if tok[1] > 0:
                self.wait_tok("sync", tok)
                self.wait_tok("gpsimd", tok)
        marks = {e: self.mark(e) for e in ENGS}
        for e in ENGS:
            for o in ENGS:
                if o != e:
                    sem = o + "_m" if o in ("sync", "gpsimd") else o
                    self.ops[e].append(
                        lambda eng, sem=sem, val=marks[o]: eng.wait_ge(self.sems[sem], val))

    def emit(self, block):
        def run(eng, e):
            for fn in self.ops[e]:
                fn(eng)

        block.sync(lambda eng: run(eng, "sync"))
        block.gpsimd(lambda eng: run(eng, "gpsimd"))
        block.scalar(lambda eng: run(eng, "scalar"))
        block.vector(lambda eng: run(eng, "vector"))
        block.tensor(lambda eng: run(eng, "tensor"))


def build_core(taps=()):
    nc = bass.Bass("TRN2", target_bir_lowering=False, debug=False)
    taps = set(taps)

    I = {}
    I["x"] = nc.dram_tensor("x", [S, D], F32, kind="ExternalInput").ap()
    for nm, shp in [
        ("n1w", [D]), ("wq", [D, 1024]), ("wk", [D, 1024]), ("wv", [D, D]),
        ("wig", [D, H]), ("big", [H]), ("wfg", [D, H]), ("bfg", [H]),
        ("wog", [D, D]), ("mhw", [D]), ("wout", [D, D]), ("n2w", [D]),
        ("wgate", [D, FF]), ("wup", [D, FF]), ("wdown", [FF, D]),
        ("s_up", [FF, 32]), ("b_up", [FF]),
        ("lua", [D, 16]), ("lub", [16, FF]),
        ("s_dn", [D, 128]), ("b_dn", [D]),
        ("lda", [FF, 16]), ("ldb", [16, D]),
    ]:
        I[nm] = nc.dram_tensor(nm, shp, F32, kind="ExternalInput").ap()
    I["q_up"] = nc.dram_tensor("q_up", [FF, D], I32, kind="ExternalInput").ap()
    I["q_dn"] = nc.dram_tensor("q_dn", [D, FF], I32, kind="ExternalInput").ap()
    I["c_ident"] = nc.dram_tensor("c_ident", [128, 128], F32, kind="ExternalInput").ap()
    I["c_dmask"] = nc.dram_tensor("c_dmask", [128, 4 * 512], F32, kind="ExternalInput").ap()
    I["c_ones_row"] = nc.dram_tensor("c_ones_row", [1, 128], F32, kind="ExternalInput").ap()
    I["c_ones_col"] = nc.dram_tensor("c_ones_col", [128, 1], F32, kind="ExternalInput").ap()

    out = nc.dram_tensor("out", [S, D], F32, kind="ExternalOutput").ap()

    scr_hin = nc.dram_tensor("scr_hin", [D, S], BF16, kind="Internal").ap()
    scr_v = nc.dram_tensor("scr_v", [S, D], BF16, kind="Internal").ap()
    scr_x1 = nc.dram_tensor("scr_x1", [S, D], BF16, kind="Internal").ap()
    scr_y = nc.dram_tensor("scr_y", [S, D], F32, kind="Internal").ap()
    scr_gme = nc.dram_tensor("scr_gme", [16, S], F32, kind="Internal").ap()
    scr_wup = nc.dram_tensor("scr_wup", [16, 16, 128, 512], BF16, kind="Internal").ap()
    scr_wdn = nc.dram_tensor("scr_wdn", [4, 64, 128, 512], BF16, kind="Internal").ap()

    tap_outs = {}

    def add_tap(name, shape, dtype=F32):
        tap_outs[name] = nc.dram_tensor("tap_" + name, shape, dtype, kind="ExternalOutput").ap()
        return tap_outs[name]

    b = Bld(nc)
    es = contextlib.ExitStack()
    with es:
        b.alloc_sems(es)

        def sb(name, shape, dtype, stack=es):
            return stack.enter_context(nc.sbuf_tensor(name, shape, dtype))

        def ps(name, shape, dtype, stack):
            return stack.enter_context(nc.psum_tensor(name, shape, dtype))

        identb = sb("identb", [128, 128], BF16)
        ident32 = sb("ident32", [128, 128], F32)
        ones_row = sb("ones_row", [1, 128], F32)
        settle_dst = sb("settle_dst", [128, 512], F32)
        b.settle_dst = settle_dst
        n2w_rep = sb("n2w_repg", [128, D], BF16)
        ones_colb = sb("ones_colb", [128, 1], BF16)
        n1w_pp = sb("n1w_pp", [128, NCH], F32)
        mhw_pp = sb("mhw_pp", [128, NCH], F32)
        bup_pp = sb("bup_pp", [128, 64], F32)
        big_sb = sb("big_sb", [8, 1], F32)
        bfg_sb = sb("bfg_sb", [8, 1], F32)
        b15i = sb("b15i", [8, 1], F32)
        b15f = sb("b15f", [8, 1], F32)
        one8 = sb("one8", [8, 1], F32)
        st_m = sb("st_m", [128, NRB], F32)
        st_rstd = sb("st_rstd", [128, NRB], F32)
        st_c1 = sb("st_c1", [128, NRB], F32)
        sy_m = sb("sy_m", [128, NRB], F32)
        sy_rstd = sb("sy_rstd", [128, NRB], F32)
        KOFF = 0
        QOFF = NCH * S // 2

        scope_ac = contextlib.ExitStack()
        es.enter_context(scope_ac)
        big_a = sb("big_a", [128, NCH * S], BF16, scope_ac)
        Gcol = sb("Gcol", [128, 128], F32, scope_ac)
        hinT = big_a

        vset = b.do("vector", lambda e: e.memset(settle_dst[:], 1.0))
        b.wait("scalar", "vector", vset)
        b.last_settle = 0
        tok_c = b.dsem(es)
        tok_g = b.dsem(es)
        tok_x = [b.dsem(es) for _ in range(4)]
        tok_scrh = b.dsem(es)
        tok_tap = b.dsem(es)
        with contextlib.ExitStack() as p0:
            wigt = sb("wig_sb", [128, NCH * H], BF16, p0)
            wfgt = sb("wfg_sb", [128, NCH * H], BF16, p0)
            gM = sb("gM", [8, S], F32, p0)
            gEn = sb("gEn", [8, S], F32, p0)
            b.dma("sync", tok_c, lambda e: e.dma_start(out=ident32[:], in_=I["c_ident"]))
            b.dma("gpsimd", tok_g, lambda e: e.dma_start(out=identb[:], in_=I["c_ident"]))
            b.dma("sync", tok_c, lambda e: e.dma_start(out=ones_row[:], in_=I["c_ones_row"]))
            b.dma("gpsimd", tok_g, lambda e: e.dma_start(out=ones_colb[:], in_=I["c_ones_col"]))
            for c in range(NCH):
                b.dma("sync", tok_c, lambda e, c=c: e.dma_start(
                    out=n1w_pp[:, c:c + 1], in_=I["n1w"][c * 128:(c + 1) * 128].unsqueeze(1)))
                b.dma("sync", tok_c, lambda e, c=c: e.dma_start(
                    out=mhw_pp[:, c:c + 1], in_=I["mhw"][c * 128:(c + 1) * 128].unsqueeze(1)))
                b.dma("gpsimd", tok_g, lambda e, c=c: e.dma_start(
                    out=wigt[:, c * H:(c + 1) * H], in_=I["wig"][c * 128:(c + 1) * 128, :]))
                b.dma("gpsimd", tok_g, lambda e, c=c: e.dma_start(
                    out=wfgt[:, c * H:(c + 1) * H], in_=I["wfg"][c * 128:(c + 1) * 128, :]))
            for c in range(64):
                b.dma("sync", tok_c, lambda e, c=c: e.dma_start(
                    out=bup_pp[:, c:c + 1], in_=I["b_up"][c * 128:(c + 1) * 128].unsqueeze(1)))
            b.dma("sync", tok_c, lambda e: e.dma_start(out=big_sb[:], in_=I["big"].unsqueeze(1)))
            b.dma("sync", tok_c, lambda e: e.dma_start(out=bfg_sb[:], in_=I["bfg"].unsqueeze(1)))
            b.wait_tok("vector", tok_c)
            b.wait_tok("vector", tok_g)
            b.do_("vector", lambda e: e.tensor_scalar(
                out=b15i[:], in0=big_sb[:], scalar1=float(1 / CAPV), scalar2=None, op0=ALU.mult))
            b.do_("vector", lambda e: e.tensor_scalar(
                out=b15f[:], in0=bfg_sb[:], scalar1=float(1 / CAPV), scalar2=None, op0=ALU.mult))
            b.do("vector", lambda e: e.memset(one8[:], 1.0))
            with contextlib.ExitStack() as p0r:
                n2w_row = sb("n2w_rowg", [1, D], F32, p0r)
                ps_p0 = ps("ps_p0", [128, 512], F32, p0r)
                tok_n2g = b.dsem(es)
                ln2 = b.dma("sync", tok_n2g, lambda e: e.dma_start(
                    out=n2w_row[:], in_=I["n2w"].unsqueeze(0)))
                b.wait_tok("tensor", tok_n2g, ln2)
                b.wait_tok("tensor", tok_c)
                b.wait_tok("tensor", tok_g)
                vrepm = 0
                for q4 in range(4):
                    if q4 > 0:
                        b.wait("tensor", "vector", vrepm)
                    trm = b.do("tensor", lambda e, q4=q4: e.matmul(
                        ps_p0[:], ones_row[:], n2w_row[0:1, q4 * 512:(q4 + 1) * 512],
                        start=True, stop=True))
                    b.wait("vector", "tensor", trm)
                    vrepm = b.do("vector", lambda e, q4=q4: e.tensor_copy(
                        n2w_rep[:, q4 * 512:(q4 + 1) * 512], ps_p0[:]))
            for e in ENGS:
                b.wait_tok(e, tok_c)
                b.wait_tok(e, tok_g)
            b.barrier()

            with contextlib.ExitStack() as pa:
                xbuf = [sb(f"xbuf{i}", [128, D], F32, pa) for i in range(4)]
                xsq = [sb("xsq0", [128, D], F32, pa), sb("xsq1", [128, D], F32, pa)]
                hb = [sb("hb0", [128, D], BF16, pa), sb("hb1", [128, D], BF16, pa)]
                pa_a = sb("pa_a", [128, 512], F32, pa)
                pa_b = sb("pa_b", [128, 512], F32, pa)
                pa_m = sb("pa_m", [128, 512], F32, pa)
                pa_t = sb("pa_t", [128, 512], F32, pa)
                pa_t2 = sb("pa_t2", [128, 512], F32, pa)
                pa_c = sb("pa_c", [128, 512], F32, pa)
                b.do_("vector", lambda e: e.memset(pa_a[:], 1.0))
                vmem = b.do("vector", lambda e: e.memset(pa_b[:], 1.0))
                b.wait("scalar", "vector", vmem)
                b.flush("vector")
                vdone = {}
                sdone = {}
                lxs = {}
                for rb in range(NRB):
                    bi = rb % 2
                    if rb >= 2:
                        b.wait("sync", "vector", vdone[rb - 2])
                        b.wait("sync", "scalar", sdone[rb - 2])
                    lxs[rb] = b.dma("sync", tok_x[bi], lambda e, rb=rb, bi=bi: e.dma_start(
                        out=xbuf[bi][:], in_=I["x"][rb * 128:(rb + 1) * 128, :]))
                    b.wait_tok("vector", tok_x[bi], lxs[rb])
                    vdone[rb] = b.do("vector", lambda e, rb=rb, bi=bi: e.reduce_sum(
                        pa_a[:, rb:rb + 1], xbuf[bi][:], axis=mybir.AxisListType.X))
                    b.wait_tok("scalar", tok_x[bi], lxs[rb])
                    sdone[rb] = b.do("scalar", lambda e, rb=rb, bi=bi: e.activation(
                        out=xsq[bi][:], in_=xbuf[bi][:], func=AF.Square,
                        accum_out=pa_b[:, rb:rb + 1]))
                sm = b.s_settle()
                b.wait("vector", "scalar", sm)
                b.flush("vector")
                v1 = b.do("vector", lambda e: e.tensor_scalar(
                    out=pa_m[:], in0=pa_a[:], scalar1=float(1.0 / D), scalar2=None, op0=ALU.mult))
                b.do_("vector", lambda e: e.tensor_scalar(
                    out=pa_t[:], in0=pa_b[:], scalar1=float(1.0 / D), scalar2=None, op0=ALU.mult))
                b.wait("vector", "vector", v1)
                v2 = b.do("vector", lambda e: e.tensor_mul(pa_t2[:], pa_m[:], pa_m[:]))
                b.wait("vector", "vector", v2)
                v3 = b.do("vector", lambda e: e.tensor_sub(pa_t[:], pa_t[:], pa_t2[:]))
                b.wait("vector", "vector", v3)
                vm = b.do("vector", lambda e: e.tensor_scalar(
                    out=pa_t2[:], in0=pa_t[:], scalar1=EPS1, scalar2=None, op0=ALU.add))
                b.wait("scalar", "vector", vm)
                s1 = b.do("scalar", lambda e: e.activation(out=pa_b[:], in_=pa_t2[:], func=AF.Ln))
                b.wait("scalar", "scalar", s1)
                s2 = b.do("scalar", lambda e: e.activation(
                    out=pa_c[:], in_=pa_b[:], func=AF.Exp, scale=-1.0))
                b.wait("scalar", "scalar", s2)
                b.do_("scalar", lambda e: e.activation(
                    out=pa_b[:], in_=pa_b[:], func=AF.Exp, scale=-0.5))
                sm = b.s_settle()
                b.wait("vector", "scalar", sm)
                v4 = b.do("vector", lambda e: e.tensor_mul(pa_a[:], pa_t[:], pa_c[:]))
                b.wait("vector", "vector", v4)
                vm = b.do("vector", lambda e: e.tensor_scalar(
                    out=pa_a[:], in0=pa_a[:], scalar1=EPS2, scalar2=None, op0=ALU.add))
                b.wait("scalar", "vector", vm)
                s4 = b.do("scalar", lambda e: e.activation(out=pa_c[:], in_=pa_a[:], func=AF.Ln))
                b.wait("scalar", "scalar", s4)
                b.do_("scalar", lambda e: e.activation(
                    out=pa_t2[:], in_=pa_c[:], func=AF.Exp, scale=-0.5))
                sm = b.s_settle()
                b.wait("vector", "scalar", sm)
                v5 = b.do("vector", lambda e: e.tensor_mul(pa_a[:], pa_b[:], pa_t2[:]))
                b.do_("vector", lambda e: e.tensor_copy(st_m[:], pa_m[:, 0:NRB]))
                b.do_("vector", lambda e: e.tensor_copy(st_rstd[:], pa_b[:, 0:NRB]))
                b.wait("vector", "vector", v5)
                vm = b.do("vector", lambda e: e.tensor_copy(st_c1[:], pa_a[:, 0:NRB]))

                with contextlib.ExitStack() as pt:
                    ps_tp = [ps(f"ps_tp{i}", [128, 512], BF16, pt) for i in range(4)]
                    b.wait("vector", "vector", vm)
                    vdone2 = {}
                    tdone = {}
                    vevac = {}
                    for rb in range(NRB):
                        bi = rb % 4
                        if rb < 4:
                            b.wait("sync", "vector", vm)
                            b.wait("sync", "scalar", b.cnt["scalar"])
                        else:
                            b.wait("sync", "vector", vdone2[rb - 4])
                        lx = b.dma("sync", tok_x[bi], lambda e, rb=rb, bi=bi: e.dma_start(
                            out=xbuf[bi][:], in_=I["x"][rb * 128:(rb + 1) * 128, :]))
                        b.wait_tok("vector", tok_x[bi], lx)
                        if rb >= 2:
                            b.wait("vector", "tensor", tdone[(rb - 2) * 4 + 3])
                        hm = b.do("vector", lambda e, rb=rb, bi=bi: e.tensor_scalar(
                            out=hb[rb % 2][:], in0=xbuf[bi][:],
                            scalar1=st_m[:, rb:rb + 1], scalar2=st_c1[:, rb:rb + 1],
                            op0=ALU.subtract, op1=ALU.mult))
                        vdone2[rb] = hm
                        b.wait("tensor", "vector", hm)
                        for g4 in range(4):
                            pb = g4 % 4
                            bidx = rb * 4 + g4
                            if bidx >= 4:
                                b.wait("tensor", "vector", vevac[bidx - 4])
                            for j in range(4):
                                c = g4 * 4 + j
                                fn = lambda e, rb=rb, c=c, pb=pb, j=j: e.transpose(
                                    ps_tp[pb][:, j * 128:(j + 1) * 128],
                                    hb[rb % 2][:, c * 128:(c + 1) * 128], identb[:])
                                if j == 3:
                                    tdone[bidx] = b.do("tensor", fn)
                                else:
                                    b.do_("tensor", fn)
                            b.wait("vector", "tensor", tdone[bidx])
                            for j in range(4):
                                c = g4 * 4 + j
                                fn = lambda e, c=c, pb=pb, j=j, rb=rb: e.tensor_scalar(
                                    out=hinT[:, c * S + rb * 128: c * S + (rb + 1) * 128],
                                    in0=ps_tp[pb][:, j * 128:(j + 1) * 128],
                                    scalar1=n1w_pp[:, c:c + 1], scalar2=None, op0=ALU.mult)
                                if j == 3:
                                    vevac[bidx] = b.do("vector", fn)
                                else:
                                    b.do_("vector", fn)
                        if rb % 4 == 3:
                            g = rb // 4
                            b.wait("sync", "vector", b.cnt["vector"])
                            b.dma("sync", tok_scrh, lambda e, g=g: e.dma_start(
                                out=scr_hin[:, g * 512:(g + 1) * 512].rearrange(
                                    "(c p) s -> p c s", p=128),
                                in_=hinT[:].rearrange("p (c s) -> p c s", s=S)[
                                    :, :, g * 512:(g + 1) * 512]))
                b.barrier()

            with contextlib.ExitStack() as pg:
                gA = sb("gA", [8, S], F32, pg)
                gB = sb("gB", [8, S], F32, pg)
                gC = sb("gC", [8, S], F32, pg)
                gZ = sb("gZ", [8, S], F32, pg)
                with contextlib.ExitStack() as pgp:
                    ps_ig = ps("ps_ig", [8, S], F32, pgp)
                    ps_fg = ps("ps_fg", [8, S], F32, pgp)
                    tg = 0
                    for st in range(4):
                        for c in range(NCH):
                            b.do_("tensor", lambda e, st=st, c=c: e.matmul(
                                ps_ig[:, st * 512:(st + 1) * 512],
                                wigt[:, c * H:(c + 1) * H],
                                hinT[:, c * S + st * 512: c * S + (st + 1) * 512],
                                start=(c == 0), stop=(c == NCH - 1)))
                        for c in range(NCH):
                            fn = lambda e, st=st, c=c: e.matmul(
                                ps_fg[:, st * 512:(st + 1) * 512],
                                wfgt[:, c * H:(c + 1) * H],
                                hinT[:, c * S + st * 512: c * S + (st + 1) * 512],
                                start=(c == 0), stop=(c == NCH - 1))
                            if c == NCH - 1:
                                tg = b.do("tensor", fn)
                            else:
                                b.do_("tensor", fn)
                    b.wait("scalar", "tensor", tg)
                    s1 = b.do("scalar", lambda e: e.activation(
                        out=gA[:], in_=ps_ig[:], func=AF.Tanh, scale=float(1 / CAPV),
                        bias=b15i[:]))
                    sm = b.do("scalar", lambda e: e.activation(
                        out=gB[:], in_=ps_fg[:], func=AF.Tanh, scale=float(1 / CAPV),
                        bias=b15f[:]))
                    b.wait("vector", "scalar", sm)
                    b.do_("vector", lambda e: e.memset(gZ[:], 0.0))
                    b.do_("vector", lambda e: e.tensor_scalar(
                        out=gA[:], in0=gA[:], scalar1=CAPV, scalar2=None, op0=ALU.mult))
                    vm = b.do("vector", lambda e: e.tensor_scalar(
                        out=gB[:], in0=gB[:], scalar1=CAPV, scalar2=None, op0=ALU.mult))
                if "gates" in taps:
                    t = add_tap("gates_if", [8, 2 * S])
                    b.wait("sync", "vector", vm)
                    b.dma("sync", tok_tap, lambda e, t=t: e.dma_start(out=t[:, 0:S], in_=gA[:]))
                    b.dma("sync", tok_tap, lambda e, t=t: e.dma_start(out=t[:, S:2 * S], in_=gB[:]))
                b.wait("scalar", "vector", vm)
                if "gates" in taps:
                    b.wait_tok("scalar", tok_tap)
                s2 = b.do("scalar", lambda e: e.activation(out=gC[:], in_=gB[:], func=AF.Exp, scale=-1.0))
                b.wait("scalar", "scalar", s2)
                sm = b.do("scalar", lambda e: e.activation(out=gB[:], in_=gC[:], func=AF.Ln, bias=one8[:]))
                b.wait("vector", "scalar", sm)
                b.flush("vector")
                v1 = b.do("vector", lambda e: e.tensor_tensor_scan(
                    out=gC[:], data0=gB[:], data1=gZ[:], initial=0.0,
                    op0=ALU.add, op1=ALU.add))
                b.wait("vector", "vector", v1)
                v2 = b.do("vector", lambda e: e.tensor_add(gA[:], gA[:], gC[:]))
                b.wait("vector", "vector", v2)
                v3 = b.do("vector", lambda e: e.tensor_tensor_scan(
                    out=gM[:], data0=gA[:], data1=gZ[:], initial=-1e30,
                    op0=ALU.max, op1=ALU.add))
                b.wait("vector", "vector", v3)
                vm = b.do("vector", lambda e: e.tensor_sub(gB[:], gC[:], gM[:]))
                b.wait("scalar", "vector", vm)
                sm = b.do("scalar", lambda e: e.activation(out=gEn[:], in_=gB[:], func=AF.Exp))
                if "gates" in taps:
                    t = add_tap("gates2", [8, 4 * S])
                    b.wait("sync", "vector", vm)
                    b.wait("sync", "scalar", sm)
                    b.dma("sync", tok_tap, lambda e, t=t: e.dma_start(out=t[:, 0:S], in_=gC[:]))
                    b.dma("sync", tok_tap, lambda e, t=t: e.dma_start(out=t[:, S:2 * S], in_=gA[:]))
                    b.dma("sync", tok_tap, lambda e, t=t: e.dma_start(out=t[:, 2 * S:3 * S], in_=gM[:]))
                    b.dma("sync", tok_tap, lambda e, t=t: e.dma_start(out=t[:, 3 * S:4 * S], in_=gEn[:]))
                b.wait("sync", "vector", vm)
                b.wait("sync", "scalar", sm)
                b.dma("sync", tok_tap, lambda e: e.dma_start(out=scr_gme[0:8, :], in_=gM[:]))
                b.dma("sync", tok_tap, lambda e: e.dma_start(out=scr_gme[8:16, :], in_=gEn[:]))
                with contextlib.ExitStack() as pgc:
                    ps_gc = ps("ps_gc", [128, 32], F32, pgc)
                    b.wait("tensor", "vector", vm)
                    for g4 in range(4):
                        for j in range(4):
                            tc = g4 * 4 + j
                            fn = lambda e, tc=tc, j=j: e.transpose(
                                ps_gc[:, j * 8:(j + 1) * 8], gA[:, tc * 128:(tc + 1) * 128],
                                ident32[0:8, 0:8])
                            if j == 3:
                                tm = b.do("tensor", fn)
                            else:
                                b.do_("tensor", fn)
                        b.wait("vector", "tensor", tm)
                        vmg = b.do("vector", lambda e, g4=g4: e.tensor_scalar(
                            out=Gcol[:, g4 * 32:(g4 + 1) * 32], in0=ps_gc[:],
                            scalar1=LOG_ISQ, scalar2=None, op0=ALU.add))
                        b.wait("tensor", "vector", vmg)
                b.barrier()

        scope_bm = contextlib.ExitStack()
        scope_ac.enter_context(scope_bm)
        big_b = sb("big_b", [128, NCH * S], BF16, scope_bm)
        with contextlib.ExitStack() as pb:
            wkq_cb = [sb("wkq0", [128, NCH * 128], BF16, pb),
                      sb("wkq1", [128, NCH * 128], BF16, pb)]
            wv_ct = sb("wv_ct", [128, NCH * 512], BF16, pb)
            vstage = sb("vstage", [128, NCH * 512], BF16, pb)
            tok_wb = [b.dsem(es), b.dsem(es)]
            tok_wv = b.dsem(es)
            tok_scrv = b.dsem(es)
            with contextlib.ExitStack() as pbp:
                ps_b = [ps(f"ps_b{i}", [128, 512], F32, pbp) for i in range(4)]
                t_used = {}
                v_evac = {}
                lwc = {}
                for idx, (wname, off) in enumerate([("wk", KOFF), ("wq", QOFF)]):
                    for oc in range(8):
                        gi = idx * 8 + oc
                        slot = gi % 2
                        if gi >= 2:
                            b.wait("gpsimd", "tensor", t_used[gi - 2])
                        lwc[gi] = b.dma("gpsimd", tok_wb[slot], lambda e, wname=wname, oc=oc, slot=slot: e.dma_start(
                            out=wkq_cb[slot][:].rearrange("p (c o) -> p c o", o=128),
                            in_=I[wname][:, oc * 128:(oc + 1) * 128].rearrange(
                                "(c p) o -> p c o", p=128)))
                        b.wait_tok("tensor", tok_wb[slot], lwc[gi])
                        for nt in range(4):
                            ti = gi * 4 + nt
                            pbk = nt % 4
                            if ti >= 4:
                                b.wait("tensor", "vector", v_evac[ti - 4])
                            tm = 0
                            for c in range(NCH):
                                fn = lambda e, slot=slot, c=c, pbk=pbk, nt=nt: e.matmul(
                                    ps_b[pbk][:],
                                    wkq_cb[slot][:, c * 128:(c + 1) * 128],
                                    hinT[:, c * S + nt * 512: c * S + (nt + 1) * 512],
                                    start=(c == 0), stop=(c == NCH - 1))
                                if c == NCH - 1:
                                    tm = b.do("tensor", fn)
                                else:
                                    b.do_("tensor", fn)
                            if nt == 3:
                                t_used[gi] = tm
                            b.wait("vector", "tensor", tm)
                            v_evac[ti] = b.do("vector", lambda e, off=off, oc=oc, nt=nt, pbk=pbk: e.tensor_copy(
                                big_b[:, off + oc * S + nt * 512: off + oc * S + (nt + 1) * 512],
                                ps_b[pbk][:]))
                vm_dma = 0
                for dvt in range(4):
                    if dvt > 0:
                        b.wait("gpsimd", "tensor", t_used[("v", dvt - 1)])
                    lv = b.dma("gpsimd", tok_wv, lambda e, dvt=dvt: e.dma_start(
                        out=wv_ct[:].rearrange("p (c o) -> p c o", o=512),
                        in_=I["wv"][:, dvt * 512:(dvt + 1) * 512].rearrange(
                            "(c p) o -> p c o", p=128)))
                    b.wait_tok("tensor", tok_wv, lv)
                    if dvt > 0:
                        b.wait_tok("tensor", tok_scrv, vm_dma)
                    for m in range(NRB):
                        pbk = m % 4
                        if m >= 4:
                            b.wait("tensor", "vector", v_evac[("v", dvt, m - 4)])
                        elif dvt > 0:
                            b.wait("tensor", "vector", v_evac[("v", dvt - 1, NRB - 1)])
                        else:
                            b.wait("tensor", "vector", v_evac[63])
                        tm = 0
                        for c in range(NCH):
                            fn = lambda e, c=c, m=m, pbk=pbk: e.matmul(
                                ps_b[pbk][:],
                                hinT[:, c * S + m * 128: c * S + (m + 1) * 128],
                                wv_ct[:, c * 512:(c + 1) * 512],
                                start=(c == 0), stop=(c == NCH - 1))
                            if c == NCH - 1:
                                tm = b.do("tensor", fn)
                            else:
                                b.do_("tensor", fn)
                        if m == NRB - 1:
                            t_used[("v", dvt)] = tm
                        b.wait("vector", "tensor", tm)
                        v_evac[("v", dvt, m)] = b.do("vector", lambda e, m=m, pbk=pbk: e.tensor_copy(
                            vstage[:, m * 512:(m + 1) * 512], ps_b[pbk][:]))
                    b.wait("sync", "vector", v_evac[("v", dvt, NRB - 1)])
                    vm_dma = b.dma("sync", tok_scrv, lambda e, dvt=dvt: e.dma_start(
                        out=scr_v[:, dvt * 512:(dvt + 1) * 512].rearrange(
                            "(m p) d -> p m d", p=128),
                        in_=vstage[:].rearrange("p (m d) -> p m d", d=512)))
            b.barrier()

        hT = big_a
        with contextlib.ExitStack() as pm:
            V_h = [sb("V_h0", [128, NCH * 256], BF16, pm), sb("V_h1", [128, NCH * 256], BF16, pm)]
            M_rep = sb("M_rep", [128, S], F32, pm)
            Mrow1 = sb("Mrow1", [1, S], F32, pm)
            EnRow1 = sb("EnRow1", [1, 512], F32, pm)
            E_sb = [sb(f"E_sb{i}", [128, 512], F32, pm) for i in range(3)]
            C_sb = [sb(f"C_sb{i}", [128, 512], BF16, pm) for i in range(3)]
            nrep = sb("nrep", [128, 512], F32, pm)
            nabs = sb("nabs", [1, 512], F32, pm)
            nmax = sb("nmax", [1, 512], F32, pm)
            ninv = sb("ninv", [1, 512], F32, pm)
            dmask_sb = sb("dmask_sb", [128, 4 * 512], F32, pm)
            tok_vh = [b.dsem(es), b.dsem(es)]
            tok_row = b.dsem(es)
            tok_er = b.dsem(es)
            tok_dm = b.dsem(es)
            b.dma("sync", tok_dm, lambda e: e.dma_start(out=dmask_sb[:], in_=I["c_dmask"]))
            for e in ("vector",):
                b.wait_tok(e, tok_dm)
            with contextlib.ExitStack() as pmp:
                ps_qk = [ps(f"ps_qk{i}", [128, 512], F32, pmp) for i in range(2)]
                ps_h = [ps(f"ps_h{i}", [128, 512], F32, pmp) for i in range(4)]
                ps_row = ps("ps_row", [1, 512], F32, pmp)
                ps_rep2 = ps("ps_rep2", [128, 512], F32, pmp)
                sE_prev = {}
                vC = {}
                last_vh_use = {}
                s_absmark = 0
                v_hevac = 0
                v_hevac_pair = {}
                trep = 0
                last_rep_read = 0
                ti = 0
                for h in range(H):
                    slot = h % 2
                    if h >= 2:
                        b.wait("sync", "tensor", last_vh_use[h - 2])
                    lv = b.dma("sync", tok_vh[slot], lambda e, h=h, slot=slot: e.dma_start(
                        out=V_h[slot][:].rearrange("p (tc d) -> p tc d", d=256),
                        in_=scr_v[:, h * 256:(h + 1) * 256].rearrange(
                            "(tc p) d -> p tc d", p=128)))
                    if h > 0:
                        b.wait("sync", "tensor", mrep_t)
                    lr = b.dma("sync", tok_row, lambda e, h=h: e.dma_start(
                        out=Mrow1[:], in_=scr_gme[h:h + 1, :]))
                    b.wait_tok("tensor", tok_row, lr)
                    if h > 0:
                        b.wait("tensor", "scalar", sE_prev[h - 1])
                        b.wait("tensor", "vector", last_rep_read)
                    for q4 in range(4):
                        if q4 > 0:
                            b.wait("tensor", "vector", last_rep_read)
                        mrep_t = b.do("tensor", lambda e, q4=q4: e.matmul(
                            ps_rep2[:], ones_row[:], Mrow1[0:1, q4 * 512:(q4 + 1) * 512],
                            start=True, stop=True))
                        b.wait("vector", "tensor", mrep_t)
                        v_mrep_evac = b.do("vector", lambda e, q4=q4: e.tensor_copy(
                            M_rep[:, q4 * 512:(q4 + 1) * 512], ps_rep2[:]))
                        last_rep_read = v_mrep_evac
                    for st in range(4):
                        ntc = 4 * (st + 1)
                        if not (h == 0 and st == 0):
                            b.wait("sync", "vector", v_nmax)
                        ler = b.dma("sync", tok_er, lambda e, h=h, st=st: e.dma_start(
                            out=EnRow1[:], in_=scr_gme[8 + h:9 + h, st * 512:(st + 1) * 512]))
                        hx = st % 2
                        def emit_qk(tc, qx):
                            b.wait_tok("tensor", tok_vh[slot], lv)
                            return b.do("tensor", lambda e, h=h, st=st, tc=tc, qx=qx: e.matmul(
                                ps_qk[qx][:],
                                big_b[:, KOFF + h * S + tc * 128: KOFF + h * S + (tc + 1) * 128],
                                big_b[:, QOFF + h * S + st * 512: QOFF + h * S + (st + 1) * 512],
                                start=True, stop=True))
                        if ti >= 2:
                            b.wait("tensor", "vector", vC[ti - 2])
                        tqk_pend = emit_qk(0, ti % 2)
                        for tc in range(ntc):
                            qx = ti % 2
                            tqk = tqk_pend
                            if ti >= 2:
                                b.wait("scalar", "vector", vC[ti - 2])
                            b.wait("scalar", "vector", v_mrep_evac)
                            sE = b.do("scalar", lambda e, h=h, st=st, tc=tc, qx=qx: e.activation(
                                out=E_sb[qx][:], in_=M_rep[:, st * 512:(st + 1) * 512],
                                func=AF.Exp, scale=-1.0,
                                bias=Gcol[:, tc * 8 + h: tc * 8 + h + 1]))
                            sE_prev[h] = sE
                            b.wait("vector", "scalar", sE)
                            if tc >= 4 * st:
                                k = tc - 4 * st
                                vmm = b.do("vector", lambda e, qx=qx, k=k: e.tensor_mul(
                                    E_sb[qx][:], E_sb[qx][:],
                                    dmask_sb[:, k * 512:(k + 1) * 512]))
                                b.wait("vector", "vector", vmm)
                            b.wait("vector", "tensor", tqk)
                            vC[ti] = b.do("vector", lambda e, qx=qx: e.tensor_mul(
                                C_sb[qx][:], ps_qk[qx][:], E_sb[qx][:]))
                            if tc + 1 < ntc:
                                if ti + 1 >= 2:
                                    b.wait("tensor", "vector", vC[ti - 1])
                                tqk_pend = emit_qk(tc + 1, (ti + 1) % 2)
                            b.wait("tensor", "vector", vC[ti])
                            if tc == 0 and s_absmark:
                                b.wait("tensor", "scalar", s_absmark)
                            b.do_("tensor", lambda e, tc=tc, qx=qx, ntc=ntc: e.matmul(
                                ps_row[:], ones_colb[:], C_sb[qx][:],
                                start=(tc == 0), stop=(tc == ntc - 1),
                                skip_group_check=True))
                            if tc == 0 and v_hevac_pair.get(hx):
                                b.wait("tensor", "vector", v_hevac_pair[hx])
                            for dvc in range(2):
                                fn = lambda e, slot=slot, tc=tc, dvc=dvc, qx=qx, ntc=ntc, hx=hx: e.matmul(
                                    ps_h[2 * hx + dvc][:],
                                    V_h[slot][:, tc * 256 + dvc * 128: tc * 256 + dvc * 128 + 128],
                                    C_sb[qx][:],
                                    start=(tc == 0), stop=(tc == ntc - 1),
                                    skip_group_check=True)
                                if dvc == 1 and tc == ntc - 1:
                                    th_done = b.do("tensor", fn)
                                else:
                                    b.do_("tensor", fn)
                            ti += 1
                        last_vh_use[h] = th_done
                        b.wait("scalar", "tensor", th_done)
                        s_absmark = b.do("scalar", lambda e: e.activation(
                            out=nabs[:], in_=ps_row[:], func=AF.Abs))
                        b.wait("vector", "scalar", s_absmark)
                        b.wait_tok("vector", tok_er, ler)
                        v_nmax = b.do("vector", lambda e: e.tensor_max(
                            nmax[:], nabs[:], EnRow1[0:1, :]))
                        b.wait("scalar", "vector", v_nmax)
                        if trep:
                            b.wait("scalar", "tensor", trep)
                        sl = b.do("scalar", lambda e: e.activation(out=ninv[:], in_=nmax[:], func=AF.Ln))
                        b.wait("scalar", "scalar", sl)
                        sl2 = b.do("scalar", lambda e: e.activation(
                            out=ninv[:], in_=ninv[:], func=AF.Exp, scale=-1.0))
                        sl2 = b.s_settle()
                        b.wait("tensor", "scalar", sl2)
                        b.wait("tensor", "vector", last_rep_read)
                        trep = b.do("tensor", lambda e: e.matmul(
                            ps_rep2[:], ones_row[:], ninv[:], start=True, stop=True))
                        b.wait("vector", "tensor", trep)
                        vrep = b.do("vector", lambda e: e.tensor_copy(nrep[:], ps_rep2[:]))
                        last_rep_read = vrep
                        b.wait("vector", "vector", vrep)
                        for dvc in range(2):
                            r = h * 2 + dvc
                            fn = lambda e, r=r, st=st, dvc=dvc, hx=hx: e.tensor_mul(
                                hT[:, r * S + st * 512: r * S + (st + 1) * 512],
                                ps_h[2 * hx + dvc][:], nrep[:])
                            v_hevac = b.do("vector", fn)
                        v_hevac_pair[hx] = v_hevac
            b.barrier()
        scope_bm.close()

        with contextlib.ExitStack() as pc:
            hintile = sb("hintile", [128, NCH * 512], BF16, pc)
            sig_blk = sb("sig_blk", [128, NCH * 512], BF16, pc)
            houtT = sb("houtT", [128, NCH * 512], BF16, pc)
            wog_cb = [sb("wog0", [128, NCH * 128], BF16, pc),
                      sb("wog1", [128, NCH * 128], BF16, pc)]
            wout_nt = [sb("wout_nt0", [128, NCH * 512], BF16, pc),
                       sb("wout_nt1", [128, NCH * 512], BF16, pc)]
            xc_all = sb("xc_all", [128, 4 * D], BF16, pc)
            x1st = sb("x1st", [128, 4 * D], BF16, pc)
            hsq = sb("hsq", [128, 512], BF16, pc)
            hn1 = sb("hn1", [1, 512], F32, pc)
            rstd_rep = [sb("rsr0", [128, 512], F32, pc), sb("rsr1", [128, 512], F32, pc)]
            tmp1 = [sb("tmp1a", [128, 512], BF16, pc), sb("tmp1b", [128, 512], BF16, pc)]
            xa_t = [sb("xa0", [128, 512], F32, pc), sb("xa1", [128, 512], F32, pc)]
            tok_hin = b.dsem(es)
            tok_wog = [b.dsem(es), b.dsem(es)]
            tok_wot = [b.dsem(es), b.dsem(es)]
            tok_xc = b.dsem(es)
            tok_x1w = b.dsem(es)
            with contextlib.ExitStack() as pcp:
                ps_o = [ps(f"ps_o{i}", [128, 512], F32, pcp) for i in range(4)]
                ps_hn = ps("ps_hn", [1, 512], F32, pcp)
                ps_rr = ps("ps_rr", [128, 512], F32, pcp)
                ps_x1 = [ps("ps_x10", [128, 512], F32, pcp), ps("ps_x11", [128, 512], F32, pcp)]
                t_wog_used = {}
                s_sig = {}
                v_hout = {}
                v_hn1 = 0
                v_rrevac = 0
                t_x1_last = 0
                t_x1w_use = {}
                t_hin_last = {}
                v_x1evac = {}
                v_hout_last = {}
                v_xa_last = {}
                x1w = {}
                t_rr = 0
                lh_pend = b.dma("sync", tok_hin, lambda e: e.dma_start(
                    out=hintile[:].rearrange("p (c s) -> p c s", s=512),
                    in_=scr_hin[:, 0:512].rearrange("(c p) s -> p c s", p=128)))
                for st in range(4):
                    lh = lh_pend
                    if st > 0:
                        b.wait("scalar", "vector", v_hout_last[(st - 1, H - 1)])
                    for oc in range(NCH):
                        gi = st * NCH + oc
                        slot = gi % 2
                        if gi >= 2:
                            b.wait("gpsimd", "tensor", t_wog_used[gi - 2])
                        lw = b.dma("gpsimd", tok_wog[slot], lambda e, oc=oc, slot=slot: e.dma_start(
                            out=wog_cb[slot][:].rearrange("p (c o) -> p c o", o=128),
                            in_=I["wog"][:, oc * 128:(oc + 1) * 128].rearrange(
                                "(c p) o -> p c o", p=128)))
                        b.wait_tok("tensor", tok_wog[slot], lw)
                        b.wait_tok("tensor", tok_hin, lh)
                        if gi >= 4:
                            b.wait("tensor", "scalar", s_sig[gi - 4])
                        tm = 0
                        for c in range(NCH):
                            fn = lambda e, slot=slot, gi=gi, c=c: e.matmul(
                                ps_o[gi % 4][:],
                                wog_cb[slot][:, c * 128:(c + 1) * 128],
                                hintile[:, c * 512:(c + 1) * 512],
                                start=(c == 0), stop=(c == NCH - 1))
                            if c == NCH - 1:
                                tm = b.do("tensor", fn)
                            else:
                                b.do_("tensor", fn)
                        t_wog_used[gi] = tm
                        if oc == NCH - 1:
                            t_hin_last[st] = tm
                            if st < 3:
                                b.wait("sync", "tensor", t_hin_last[st])
                                lh_pend = b.dma("sync", tok_hin, lambda e, st=st: e.dma_start(
                                    out=hintile[:].rearrange("p (c s) -> p c s", s=512),
                                    in_=scr_hin[:, (st + 1) * 512:(st + 2) * 512].rearrange(
                                        "(c p) s -> p c s", p=128)))
                        b.wait("scalar", "tensor", tm)
                        s_sig[gi] = b.do("scalar", lambda e, oc=oc, gi=gi: e.activation(
                            out=sig_blk[:, oc * 512:(oc + 1) * 512], in_=ps_o[gi % 4][:],
                            func=AF.Sigmoid))
                    for hh in range(H):
                        t_hn = 0
                        for k in range(2):
                            r = 2 * hh + k
                            if t_hn:
                                b.wait("vector", "tensor", t_hn)
                            vq = b.do("vector", lambda e, r=r, st=st: e.tensor_mul(
                                hsq[:], hT[:, r * S + st * 512: r * S + (st + 1) * 512],
                                hT[:, r * S + st * 512: r * S + (st + 1) * 512]))
                            b.wait("tensor", "vector", vq)
                            if k == 0 and v_hn1:
                                b.wait("tensor", "vector", v_hn1)
                            fn = lambda e, k=k: e.matmul(
                                ps_hn[:], ones_colb[:], hsq[:],
                                start=(k == 0), stop=(k == 1), skip_group_check=True)
                            t_hn = b.do("tensor", fn)
                        b.wait("vector", "tensor", t_hn)
                        v_hn1 = b.do("vector", lambda e: e.tensor_copy(hn1[:], ps_hn[:]))
                        b.wait("vector", "vector", v_hn1)
                        v_hn2 = b.do("vector", lambda e: e.tensor_scalar(
                            out=hn1[:], in0=hn1[:], scalar1=float(1.0 / DV), scalar2=EPS2,
                            op0=ALU.mult, op1=ALU.add))
                        b.wait("scalar", "vector", v_hn2)
                        sA = b.do("scalar", lambda e: e.activation(out=hn1[:], in_=hn1[:], func=AF.Ln))
                        b.wait("scalar", "scalar", sA)
                        b.do_("scalar", lambda e: e.activation(
                            out=hn1[:], in_=hn1[:], func=AF.Exp, scale=-0.5))
                        sB = b.s_settle()
                        b.wait("tensor", "scalar", sB)
                        if v_rrevac:
                            b.wait("tensor", "vector", v_rrevac)
                        t_rr = b.do("tensor", lambda e: e.matmul(
                            ps_rr[:], ones_row[:], hn1[:], start=True, stop=True))
                        b.wait("vector", "tensor", t_rr)
                        if hh >= 2 and v_hout_last.get((st, hh - 2)):
                            b.wait("vector", "vector", v_hout_last[(st, hh - 2)])
                        elif st > 0 and hh < 2:
                            b.wait("vector", "vector", v_hout_last[(st - 1, H - 2 + hh)])
                        v_rrevac = b.do("vector", lambda e, hh=hh: e.tensor_copy(
                            rstd_rep[hh % 2][:], ps_rr[:]))
                        b.wait("vector", "vector", v_rrevac)
                        if st == 0 and hh == 0 and t_x1_last:
                            b.wait("vector", "tensor", t_x1_last)
                        for k in range(2):
                            r = 2 * hh + k
                            if st > 0 or hh > 0 or k > 0:
                                pass
                            vv1 = b.do("vector", lambda e, r=r, st=st, k=k, hh=hh: e.scalar_tensor_tensor(
                                out=tmp1[k][:],
                                in0=hT[:, r * S + st * 512: r * S + (st + 1) * 512],
                                scalar=mhw_pp[:, r:r + 1],
                                in1=rstd_rep[hh % 2][:],
                                op0=ALU.mult, op1=ALU.mult))
                            b.wait("vector", "vector", vv1)
                            b.wait("vector", "scalar", s_sig[st * NCH + r])
                            v_hout[(st, r)] = b.do("vector", lambda e, r=r, k=k: e.tensor_mul(
                                houtT[:, r * 512:(r + 1) * 512], tmp1[k][:],
                                sig_blk[:, r * 512:(r + 1) * 512]))
                        v_hout_last[(st, hh)] = v_hout[(st, 2 * hh + 1)]
                    if st > 0:
                        b.wait("gpsimd", "vector", v_xa_last[st - 1])
                    lxc = {}
                    for mi in range(4):
                        lxc[mi] = b.dma("gpsimd", tok_xc, lambda e, st=st, mi=mi: e.dma_start(
                            out=xc_all[:, mi * D:(mi + 1) * D],
                            in_=I["x"][(st * 4 + mi) * 128:(st * 4 + mi + 1) * 128, :]))
                    if st > 0:
                        b.wait_tok("vector", tok_x1w, x1w[st - 1])
                    for nt in range(4):
                        gnt = st * 4 + nt
                        wsl = gnt % 2
                        if gnt >= 2:
                            b.wait("gpsimd", "tensor", t_x1w_use[gnt - 2])
                        lwo = b.dma("gpsimd", tok_wot[wsl], lambda e, nt=nt, wsl=wsl: e.dma_start(
                            out=wout_nt[wsl][:].rearrange("p (c o) -> p c o", o=512),
                            in_=I["wout"][:, nt * 512:(nt + 1) * 512].rearrange(
                                "(c p) o -> p c o", p=128)))
                        b.wait_tok("tensor", tok_wot[wsl], lwo)
                        for mi in range(4):
                            idx = st * 16 + nt * 4 + mi
                            sx = idx % 2
                            if idx >= 2:
                                b.wait("tensor", "vector", v_x1evac[idx - 2])
                            tm = 0
                            for c in range(NCH):
                                b.wait("tensor", "vector", v_hout[(st, c)])
                                fn = lambda e, c=c, mi=mi, sx=sx, wsl=wsl: e.matmul(
                                    ps_x1[sx][:],
                                    houtT[:, c * 512 + mi * 128: c * 512 + mi * 128 + 128],
                                    wout_nt[wsl][:, c * 512:(c + 1) * 512],
                                    start=(c == 0), stop=(c == NCH - 1))
                                if c == NCH - 1:
                                    tm = b.do("tensor", fn)
                                else:
                                    b.do_("tensor", fn)
                            t_x1_last = tm
                            t_x1w_use[gnt] = tm
                            b.wait("vector", "tensor", tm)
                            b.wait_tok("vector", tok_xc)
                            rb = st * 4 + mi
                            vv1 = b.do("vector", lambda e, mi=mi, nt=nt, rb=rb, sx=sx: e.tensor_scalar(
                                out=xa_t[sx][:],
                                in0=xc_all[:, mi * D + nt * 512: mi * D + (nt + 1) * 512],
                                scalar1=st_m[:, rb:rb + 1], scalar2=st_rstd[:, rb:rb + 1],
                                op0=ALU.subtract, op1=ALU.mult))
                            b.wait("vector", "vector", vv1)
                            v_x1evac[idx] = b.do("vector", lambda e, mi=mi, nt=nt, sx=sx: e.tensor_add(
                                x1st[:, mi * D + nt * 512: mi * D + (nt + 1) * 512],
                                xa_t[sx][:], ps_x1[sx][:]))
                    v_xa_last[st] = v_x1evac[st * 16 + 15]
                    b.wait("sync", "vector", v_xa_last[st])
                    for mi in range(4):
                        x1w[st] = b.dma("sync", tok_x1w, lambda e, st=st, mi=mi: e.dma_start(
                            out=scr_x1[(st * 4 + mi) * 128:(st * 4 + mi + 1) * 128, :],
                            in_=x1st[:, mi * D:(mi + 1) * D]))
            b.barrier()
        scope_ac.close()

        with contextlib.ExitStack() as pq:
            sup_sb = sb("sup_sb", [128, 64 * 32], F32, pq)
            sdn_sb = sb("sdn_sb", [128, 16 * 128], F32, pq)
            qtile = [sb("qt0", [128, 4096], I32, pq), sb("qt1", [128, 4096], I32, pq)]
            dq_t = [sb("dqt0", [128, 4096], BF16, pq), sb("dqt1", [128, 4096], BF16, pq)]
            stageU = sb("stageU", [128, NCH * 512], BF16, pq)
            stageD = sb("stageD", [128, 32 * 512], BF16, pq)
            tok_q = [b.dsem(es), b.dsem(es)]
            tok_sc = b.dsem(es)
            tok_stg = b.dsem(es)
            b.dma("sync", tok_sc, lambda e: e.dma_start(
                out=sup_sb[:].rearrange("p (fb c) -> p fb c", c=32),
                in_=I["s_up"].rearrange("(fb p) c -> p fb c", p=128)))
            b.dma("sync", tok_sc, lambda e: e.dma_start(
                out=sdn_sb[:].rearrange("p (db c) -> p db c", c=128),
                in_=I["s_dn"].rearrange("(db p) c -> p db c", p=128)))
            b.wait_tok("vector", tok_sc)
            with contextlib.ExitStack() as pqp:
                ps_t = [ps("ps_tq0", [128, 512], BF16, pqp), ps("ps_tq1", [128, 512], BF16, pqp)]
                state = {"seq": 0, "pkey": {0: None, 1: None}, "tpk": {0: None, 1: None},
                         "v_dq": {}, "t_tp": {}, "s_ev": {}}

                def dq_tile(qsrc, rows, cols, srep, stage, woff, nch_t):
                    st8 = state
                    slot = st8["seq"] % 2
                    width = cols[1] - cols[0]
                    if st8["pkey"][slot] is not None:
                        b.wait("sync", "vector", st8["v_dq"][st8["pkey"][slot]])
                    lq = b.dma("sync", tok_q[slot], lambda e, qsrc=qsrc, rows=rows, cols=cols, slot=slot, width=width: e.dma_start(
                        out=qtile[slot][:, 0:width],
                        in_=qsrc[rows[0]:rows[1], cols[0]:cols[1]]))
                    b.wait_tok("vector", tok_q[slot], lq)
                    if st8["pkey"][slot] is not None:
                        b.wait("vector", "tensor", st8["t_tp"][st8["pkey"][slot]])
                    key = st8["seq"]
                    st8["v_dq"][key] = b.do("vector", lambda e, slot=slot, srep=srep, width=width: e.scalar_tensor_tensor(
                        out=dq_t[slot][:, 0:width].rearrange("p (c r) -> p c r", r=64),
                        in0=qtile[slot][:, 0:width].rearrange("p (c r) -> p c r", r=64),
                        scalar=7.5, in1=srep,
                        op0=ALU.subtract, op1=ALU.mult))
                    st8["pkey"][slot] = key
                    b.wait("tensor", "vector", st8["v_dq"][key])
                    for g4 in range(nch_t // 4):
                        pbk = g4 % 2
                        if st8["tpk"][pbk] is not None:
                            b.wait("tensor", "scalar", st8["s_ev"][st8["tpk"][pbk]])
                        tm = 0
                        for jj in range(4):
                            c = g4 * 4 + jj
                            fn = lambda e, slot=slot, c=c, pbk=pbk, jj=jj: e.transpose(
                                ps_t[pbk][:, jj * 128:(jj + 1) * 128],
                                dq_t[slot][:, c * 128:(c + 1) * 128], identb[:])
                            if jj == 3:
                                tm = b.do("tensor", fn)
                            else:
                                b.do_("tensor", fn)
                        st8["t_tp"][key] = tm
                        b.wait("scalar", "tensor", tm)
                        sev = b.do("scalar", lambda e, g4=g4, woff=woff, stage=stage, pbk=pbk: e.activation(
                            out=stage[:].rearrange("p (c f) -> p c f", f=512)[
                                :, g4 * 4:(g4 + 1) * 4, woff:woff + 128],
                            in_=ps_t[pbk][:].rearrange("p (c f) -> p c f", f=128),
                            func=AF.Identity))
                        st8["s_ev"][(key, g4)] = sev
                        st8["tpk"][pbk] = (key, g4)
                    st8["seq"] += 1

                for fbg in range(16):
                    for fbi in range(4):
                        fb = fbg * 4 + fbi
                        srep = sup_sb[:, fb * 32:(fb + 1) * 32].unsqueeze(2).broadcast_to([128, 32, 64])
                        dq_tile(I["q_up"], (fb * 128, (fb + 1) * 128), (0, 2048), srep,
                                stageU, fbi * 128, 16)
                    b.wait("sync", "scalar", b.cnt["scalar"])
                    b.dma("sync", tok_stg, lambda e, fbg=fbg: e.dma_start(
                        out=scr_wup[fbg].rearrange("c p f -> p c f"),
                        in_=stageU[:].rearrange("p (c f) -> p c f", f=512)))
                    b.wait_tok("scalar", tok_stg)
                for dbg in range(4):
                    for fh in range(2):
                        for dbi in range(4):
                            db = dbg * 4 + dbi
                            srep = sdn_sb[:, db * 128 + fh * 64: db * 128 + fh * 64 + 64
                                          ].unsqueeze(2).broadcast_to([128, 64, 64])
                            dq_tile(I["q_dn"], (db * 128, (db + 1) * 128),
                                    (fh * 4096, (fh + 1) * 4096), srep,
                                    stageD, dbi * 128, 32)
                        b.wait("sync", "scalar", b.cnt["scalar"])
                        b.dma("sync", tok_stg, lambda e, dbg=dbg, fh=fh: e.dma_start(
                            out=scr_wdn[dbg, fh * 32:(fh + 1) * 32].rearrange("j p d -> p j d"),
                            in_=stageD[:].rearrange("p (j d) -> p j d", d=512)))
                        b.wait_tok("scalar", tok_stg)
            b.barrier()

        with contextlib.ExitStack() as pd:
            uT = sb("uT", [128, 64 * 1024], BF16, pd)
            st3 = sb("st3", [128, NRB], F32, pd)
            tok_x1r = [b.dsem(es), b.dsem(es)]
            tok_wg = [b.dsem(es), b.dsem(es)]
            tok_wu = [b.dsem(es), b.dsem(es)]
            tok_wd = [b.dsem(es), b.dsem(es)]
            tok_xg = [b.dsem(es), b.dsem(es)]
            tok_x1c = [b.dsem(es), b.dsem(es)]
            tok_yw = b.dsem(es)
            tok_n2 = b.dsem(es)
            def d_group(g2):
                if g2 > 0:
                    for tk in (tok_yw, tok_xg[0], tok_xg[1], tok_x1c[0], tok_x1c[1]):
                        b.wait_tok("sync", tk)
                        b.wait_tok("gpsimd", tk)
                        b.wait_tok("vector", tk)
                        b.wait_tok("scalar", tk)
                with contextlib.ExitStack() as pdu:
                    h2T = sb(f"h2T_{g2}", [128, NCH * 1024], BF16, pdu)
                    x1b = [sb(f"x1b0_{g2}", [128, D], BF16, pdu), sb(f"x1b1_{g2}", [128, D], BF16, pdu)]
                    h2tok = [sb(f"h2t0_{g2}", [128, D], BF16, pdu), sb(f"h2t1_{g2}", [128, D], BF16, pdu)]
                    pd_b = sb(f"pd_b_{g2}", [128, 512], F32, pdu)
                    pd_c = sb(f"pd_c_{g2}", [128, 512], F32, pdu)
                    wg_cb = [sb(f"wg_cba_{g2}", [128, NCH * 128], BF16, pdu),
                             sb(f"wg_cbb_{g2}", [128, NCH * 128], BF16, pdu)]
                    wu_cb = [sb(f"wu_cba_{g2}", [128, NCH * 128], BF16, pdu),
                             sb(f"wu_cbb_{g2}", [128, NCH * 128], BF16, pdu)]
                    silu_sb = [sb(f"sil0_{g2}", [128, 512], BF16, pdu), sb(f"sil1_{g2}", [128, 512], BF16, pdu)]
                    with contextlib.ExitStack() as pdp:
                        ps_tp2 = [ps(f"ps_tpa_{g2}", [128, 512], BF16, pdp), ps(f"ps_tpb_{g2}", [128, 512], BF16, pdp)]
                        ps_g = [ps(f"ps_ga_{g2}", [128, 512], F32, pdp), ps(f"ps_gb_{g2}", [128, 512], F32, pdp)]
                        ps_u = [ps(f"ps_ua_{g2}", [128, 512], F32, pdp), ps(f"ps_ub_{g2}", [128, 512], F32, pdp)]
                        vpd = b.do("vector", lambda e: e.memset(pd_b[:], 1.0))
                        b.wait("scalar", "vector", vpd)
                        s_sq = {}
                        l_x1 = {}
                        for mi in range(8):
                            rbg = g2 * 8 + mi
                            slot = mi % 2
                            if mi >= 2:
                                b.wait("sync", "scalar", s_sq[mi - 2])
                            l_x1[mi] = b.dma("sync", tok_x1r[slot], lambda e, rbg=rbg, slot=slot: e.dma_start(
                                out=x1b[slot][:], in_=scr_x1[rbg * 128:(rbg + 1) * 128, :]))
                            b.wait_tok("scalar", tok_x1r[slot], l_x1[mi])
                            s_sq[mi] = b.do("scalar", lambda e, slot=slot, mi=mi: e.activation(
                                out=h2tok[slot][:], in_=x1b[slot][:], func=AF.Square,
                                accum_out=pd_b[:, mi:mi + 1]))
                        smq = b.s_settle()
                        b.wait("vector", "scalar", smq)
                        v1 = b.do("vector", lambda e: e.tensor_scalar(
                            out=pd_c[:], in0=pd_b[:], scalar1=float(1.0 / D), scalar2=EPS2,
                            op0=ALU.mult, op1=ALU.add))
                        b.wait("scalar", "vector", v1)
                        sA = b.do("scalar", lambda e: e.activation(out=pd_c[:], in_=pd_c[:], func=AF.Ln))
                        b.wait("scalar", "scalar", sA)
                        b.do_("scalar", lambda e: e.activation(
                            out=pd_c[:], in_=pd_c[:], func=AF.Exp, scale=-0.5))
                        smq = b.s_settle()
                        b.wait("vector", "scalar", smq)
                        vst3 = b.do("vector", lambda e, g2=g2: e.tensor_copy(
                            st3[:, g2 * 8:(g2 + 1) * 8], pd_c[:, 0:8]))
                        v_h2evac = {}
                        t_tp3 = {}
                        v_h2 = {}
                        for mi in range(8):
                            rbg = g2 * 8 + mi
                            slot = mi % 2
                            if mi >= 2:
                                b.wait("sync", "vector", v_h2[mi - 2])
                            else:
                                b.wait("sync", "scalar", smq)
                            lx = b.dma("sync", tok_x1r[slot], lambda e, rbg=rbg, slot=slot: e.dma_start(
                                out=x1b[slot][:], in_=scr_x1[rbg * 128:(rbg + 1) * 128, :]))
                            b.wait_tok("vector", tok_x1r[slot], lx)
                            b.wait("vector", "vector", vst3)
                            if mi >= 2:
                                b.wait("vector", "tensor", t_tp3[(mi - 2, 3)])
                            v_h2[mi] = b.do("vector", lambda e, slot=slot, rbg=rbg: e.scalar_tensor_tensor(
                                out=h2tok[slot][:], in0=x1b[slot][:],
                                scalar=st3[:, rbg % NRB: rbg % NRB + 1] if False else st3[:, rbg:rbg + 1],
                                in1=n2w_rep[:], op0=ALU.mult, op1=ALU.mult))
                            b.wait("tensor", "vector", v_h2[mi])
                            for g4 in range(4):
                                pbk = g4 % 2
                                bidx = mi * 4 + g4
                                if bidx >= 2:
                                    b.wait("tensor", "vector", v_h2evac[bidx - 2])
                                tm = 0
                                for jj in range(4):
                                    c = g4 * 4 + jj
                                    fn = lambda e, slot=slot, c=c, pbk=pbk, jj=jj: e.transpose(
                                        ps_tp2[pbk][:, jj * 128:(jj + 1) * 128],
                                        h2tok[slot][:, c * 128:(c + 1) * 128], identb[:])
                                    if jj == 3:
                                        tm = b.do("tensor", fn)
                                    else:
                                        b.do_("tensor", fn)
                                t_tp3[(mi, g4)] = tm
                                b.wait("vector", "tensor", tm)
                                vv = b.do("vector", lambda e, g4=g4, mi=mi, pbk=pbk: e.tensor_copy(
                                    h2T[:].rearrange("p (c f) -> p c f", f=1024)[
                                        :, g4 * 4:(g4 + 1) * 4, mi * 128:(mi + 1) * 128],
                                    ps_tp2[pbk][:].rearrange("p (c f) -> p c f", f=128)))
                                v_h2evac[bidx] = vv
                        h2done = b.cnt["vector"]
                        t_g = {}
                        s_sil = {}
                        v_u = {}
                        for oc in range(64):
                            ws = oc % 2
                            if oc > 1:
                                b.wait("gpsimd", "tensor", t_g[oc - 2])
                            lg = b.dma("gpsimd", tok_wg[ws], lambda e, oc=oc, ws=ws: e.dma_start(
                                out=wg_cb[ws][:].rearrange("p (c o) -> p c o", o=128),
                                in_=I["wgate"][:, oc * 128:(oc + 1) * 128].rearrange(
                                    "(c p) o -> p c o", p=128)))
                            lu = b.dma("gpsimd", tok_wu[ws], lambda e, oc=oc, ws=ws: e.dma_start(
                                out=wu_cb[ws][:].rearrange("p (c o) -> p c o", o=128),
                                in_=I["wup"][:, oc * 128:(oc + 1) * 128].rearrange(
                                    "(c p) o -> p c o", p=128)))
                            b.wait_tok("tensor", tok_wg[ws], lg)
                            b.wait_tok("tensor", tok_wu[ws], lu)
                            b.wait("tensor", "vector", h2done)
                            for s2 in range(2):
                                gx = s2
                                gi = oc * 2 + s2
                                if gi >= 2:
                                    b.wait("tensor", "scalar", s_sil[gi - 2])
                                    b.wait("tensor", "vector", v_u[gi - 2])
                                for c in range(NCH):
                                    b.do_("tensor", lambda e, c=c, s2=s2, gx=gx, ws=ws: e.matmul(
                                        ps_g[gx][:], wg_cb[ws][:, c * 128:(c + 1) * 128],
                                        h2T[:, c * 1024 + s2 * 512: c * 1024 + (s2 + 1) * 512],
                                        start=(c == 0), stop=(c == NCH - 1)))
                                tm = 0
                                for c in range(NCH):
                                    fn = lambda e, c=c, s2=s2, gx=gx, ws=ws: e.matmul(
                                        ps_u[gx][:], wu_cb[ws][:, c * 128:(c + 1) * 128],
                                        h2T[:, c * 1024 + s2 * 512: c * 1024 + (s2 + 1) * 512],
                                        start=(c == 0), stop=(c == NCH - 1))
                                    if c == NCH - 1:
                                        tm = b.do("tensor", fn)
                                    else:
                                        b.do_("tensor", fn)
                                t_g[oc] = tm
                                b.wait("scalar", "tensor", tm)
                                s_sil[gi] = b.do("scalar", lambda e, gx=gx, gi=gi: e.activation(
                                    out=silu_sb[gi % 2][:], in_=ps_g[gx][:], func=AF.Silu))
                                b.wait("vector", "scalar", s_sil[gi])
                                v_u[gi] = b.do("vector", lambda e, oc=oc, s2=s2, gx=gx, gi=gi: e.tensor_mul(
                                    uT[:, oc * 1024 + s2 * 512: oc * 1024 + (s2 + 1) * 512],
                                    silu_sb[gi % 2][:], ps_u[gx][:]))
                        v_uT_last = v_u[127]
                with contextlib.ExitStack() as pdd:
                    wd_t = [sb(f"wd0_{g2}", [128, 4 * 512], BF16, pdd), sb(f"wd1_{g2}", [128, 4 * 512], BF16, pdd)]
                    xcol = [sb(f"xcol0_{g2}", [128, 8 * 512], F32, pdd)]
                    x1col = [sb(f"x1col0_{g2}", [128, 8 * 512], BF16, pdd)]
                    ystage = sb(f"ystage_{g2}", [128, 8 * 512], F32, pdd)
                    ytmp = [sb(f"ytmp0_{g2}", [128, 512], F32, pdd), sb(f"ytmp1_{g2}", [128, 512], F32, pdd)]
                    with contextlib.ExitStack() as pdq:
                        ps_d = [ps(f"ps_d{i}_{g2}", [128, 512], F32, pdq) for i in range(8)]
                        v_yevac = {}
                        t_wd_use = {}
                        yw = 0
                        for nt in range(4):
                            slot = 0
                            if nt >= 1:
                                b.wait("sync", "vector", v_yevac[(nt - 1, 7)])
                            else:
                                b.wait("sync", "vector", v_uT_last)
                            lxc = b.dma("sync", tok_xg[slot], lambda e, g2=g2, nt=nt, slot=slot: e.dma_start(
                                out=xcol[slot][:].rearrange("p (mi d) -> p mi d", d=512),
                                in_=I["x"][g2 * 1024:(g2 + 1) * 1024, nt * 512:(nt + 1) * 512
                                           ].rearrange("(mi p) d -> p mi d", p=128)))
                            l1c = b.dma("sync", tok_x1c[slot], lambda e, g2=g2, nt=nt, slot=slot: e.dma_start(
                                out=x1col[slot][:].rearrange("p (mi d) -> p mi d", d=512),
                                in_=scr_x1[g2 * 1024:(g2 + 1) * 1024, nt * 512:(nt + 1) * 512
                                           ].rearrange("(mi p) d -> p mi d", p=128)))
                            for jb in range(16):
                                js = jb % 2
                                prevkey = (nt, jb - 2) if jb >= 2 else (nt - 1, 14 + jb)
                                if prevkey in t_wd_use:
                                    b.wait("gpsimd", "tensor", t_wd_use[prevkey])
                                else:
                                    b.wait("gpsimd", "tensor", t_g[63])
                                lwd = b.dma("gpsimd", tok_wd[js], lambda e, jb=jb, nt=nt, js=js: e.dma_start(
                                    out=wd_t[js][:].rearrange("p (j d) -> p j d", d=512),
                                    in_=I["wdown"][jb * 512:(jb + 1) * 512, nt * 512:(nt + 1) * 512
                                                   ].rearrange("(j p) d -> p j d", p=128)))
                                b.wait_tok("tensor", tok_wd[js], lwd)
                                if jb == 0:
                                    b.wait("tensor", "vector", v_uT_last)
                                    if nt >= 1:
                                        b.wait("tensor", "vector", v_yevac[(nt - 1, 7)])
                                tm = 0
                                for jj in range(4):
                                    j = jb * 4 + jj
                                    for mi in range(8):
                                        fn = lambda e, j=j, jj=jj, mi=mi, js=js: e.matmul(
                                            ps_d[mi][:],
                                            uT[:, j * 1024 + mi * 128: j * 1024 + (mi + 1) * 128],
                                            wd_t[js][:, jj * 512:(jj + 1) * 512],
                                            start=(j == 0), stop=(j == 63), skip_group_check=True)
                                        if mi == 7 and jj == 3:
                                            tm = b.do("tensor", fn)
                                        else:
                                            b.do_("tensor", fn)
                                t_wd_use[(nt, jb)] = tm
                            b.wait("vector", "tensor", t_wd_use[(nt, 15)])
                            b.wait_tok("vector", tok_xg[slot], lxc)
                            b.wait_tok("vector", tok_x1c[slot], l1c)
                            if nt > 0:
                                b.wait_tok("vector", tok_yw, yw)
                            for mi in range(8):
                                vv1 = b.do("vector", lambda e, mi=mi, slot=slot: e.tensor_add(
                                    ytmp[mi % 2][:], ps_d[mi][:], xcol[slot][:, mi * 512:(mi + 1) * 512]))
                                b.wait("vector", "vector", vv1)
                                v_yevac[(nt, mi)] = b.do("vector", lambda e, mi=mi, slot=slot: e.tensor_add(
                                    ystage[:, mi * 512:(mi + 1) * 512], ytmp[mi % 2][:],
                                    x1col[slot][:, mi * 512:(mi + 1) * 512]))
                            b.wait("sync", "vector", v_yevac[(nt, 7)])
                            yw = b.dma("sync", tok_yw, lambda e, g2=g2, nt=nt: e.dma_start(
                                out=scr_y[g2 * 1024:(g2 + 1) * 1024, nt * 512:(nt + 1) * 512
                                          ].rearrange("(mi p) d -> p mi d", p=128),
                                in_=ystage[:].rearrange("p (mi d) -> p mi d", d=512)))
                return v_uT_last

            v_uT_last = 0
            for g2 in range(2):
                v_uT_last = d_group(g2)
            b.barrier()

        with contextlib.ExitStack() as pe:
            u2T = sb("u2T", [128, 64 * 512], BF16, pe)
            lua_sb = sb("lua_sb", [128, NCH * 16], BF16, pe)
            lda_sb = sb("lda_sb", [128, 64 * 16], BF16, pe)
            ldb_sb = sb("ldb_sb", [16, D], BF16, pe)
            bdn_rep = sb("bdn_rep", [128, D], F32, pe)
            pe_a = sb("pe_a", [128, 512], F32, pe)
            pe_b = sb("pe_b", [128, 512], F32, pe)
            pe_m = sb("pe_m", [128, 512], F32, pe)
            pe_c = sb("pe_c", [128, 512], F32, pe)
            t2T = sb("t2T", [16, 512], BF16, pe)
            tok_el = b.dsem(es)
            tok_y2 = [b.dsem(es), b.dsem(es)]
            tok_wupt = [b.dsem(es), b.dsem(es)]
            tok_lub = [b.dsem(es), b.dsem(es)]
            tok_wdt = [b.dsem(es), b.dsem(es), b.dsem(es)]
            tok_yc = [b.dsem(es), b.dsem(es)]
            tok_out = b.dsem(es)
            for c in range(NCH):
                b.dma("gpsimd", tok_el, lambda e, c=c: e.dma_start(
                    out=lua_sb[:, c * 16:(c + 1) * 16], in_=I["lua"][c * 128:(c + 1) * 128, :]))
            for c in range(64):
                b.dma("gpsimd", tok_el, lambda e, c=c: e.dma_start(
                    out=lda_sb[:, c * 16:(c + 1) * 16], in_=I["lda"][c * 128:(c + 1) * 128, :]))
            b.dma("gpsimd", tok_el, lambda e: e.dma_start(out=ldb_sb[:], in_=I["ldb"]))
            with contextlib.ExitStack() as peb:
                bdn_row = sb("bdn_row", [1, D], F32, peb)
                ps_br = ps("ps_br", [128, 512], F32, peb)
                tok_br = b.dsem(es)
                lbr = b.dma("sync", tok_br, lambda e: e.dma_start(
                    out=bdn_row[:], in_=I["b_dn"].unsqueeze(0)))
                b.wait_tok("tensor", tok_br, lbr)
                vbr = 0
                for q4 in range(4):
                    if q4 > 0:
                        b.wait("tensor", "vector", vbr)
                    tbr = b.do("tensor", lambda e, q4=q4: e.matmul(
                        ps_br[:], ones_row[:], bdn_row[0:1, q4 * 512:(q4 + 1) * 512],
                        start=True, stop=True))
                    b.wait("vector", "tensor", tbr)
                    vbr = b.do("vector", lambda e, q4=q4: e.tensor_copy(
                        bdn_rep[:, q4 * 512:(q4 + 1) * 512], ps_br[:]))
                b.wait("tensor", "vector", vbr)

            t_dn_last_box = [0]

            def e_group(ge):
                for tk in (tok_out, tok_yc[0], tok_yc[1], tok_wdt[0], tok_wdt[1]):
                    b.wait_tok("sync", tk)
                    b.wait_tok("gpsimd", tk)
                    b.wait_tok("vector", tk)
                    b.wait_tok("scalar", tk)
                with contextlib.ExitStack() as peu:
                    x1pT = sb(f"x1pT_{ge}", [128, NCH * 512], BF16, peu)
                    yblk = [sb(f"ybk0_{ge}", [128, D], F32, peu), sb(f"ybk1_{ge}", [128, D], F32, peu)]
                    x1ptok = [sb(f"x1p0_{ge}", [128, D], BF16, peu), sb(f"x1p1_{ge}", [128, D], BF16, peu)]
                    wupt = [sb(f"wupta_{ge}", [128, NCH * 512], BF16, peu),
                            sb(f"wuptb_{ge}", [128, NCH * 512], BF16, peu)]
                    lub_oc = [sb(f"lub0_{ge}", [16, 128], BF16, peu), sb(f"lub1_{ge}", [16, 128], BF16, peu)]
                    tupT = sb(f"tupT_{ge}", [16, 512], BF16, peu)
                    with contextlib.ExitStack() as pep:
                        ps_tp4 = [ps(f"ps_tc0_{ge}", [128, 512], BF16, pep),
                                  ps(f"ps_tc1_{ge}", [128, 512], BF16, pep)]
                        ps_l = ps(f"ps_l_{ge}", [16, 512], F32, pep)
                        ps_y1 = [ps(f"ps_y1a_{ge}", [128, 512], F32, pep),
                                 ps(f"ps_y1b_{ge}", [128, 512], F32, pep)]
                        b.do_("vector", lambda e: e.memset(pe_a[:], 1.0))
                        vpd = b.do("vector", lambda e: e.memset(pe_b[:], 1.0))
                        b.wait("scalar", "vector", vpd)
                        b.flush("vector")
                        s_sq = {}
                        for mi in range(4):
                            rge = ge * 4 + mi
                            slot = mi % 2
                            if mi >= 2:
                                b.wait("sync", "scalar", s_sq[mi - 2])
                                b.wait("sync", "vector", b.cnt["vector"])
                            ly = b.dma("sync", tok_y2[slot], lambda e, rge=rge, slot=slot: e.dma_start(
                                out=yblk[slot][:], in_=scr_y[rge * 128:(rge + 1) * 128, :]))
                            b.wait_tok("vector", tok_y2[slot], ly)
                            b.do("vector", lambda e, mi=mi, slot=slot: e.reduce_sum(
                                pe_a[:, mi:mi + 1], yblk[slot][:], axis=mybir.AxisListType.X))
                            b.wait_tok("scalar", tok_y2[slot], ly)
                            s_sq[mi] = b.do("scalar", lambda e, mi=mi, slot=slot: e.activation(
                                out=x1ptok[slot][:], in_=yblk[slot][:], func=AF.Square,
                                accum_out=pe_b[:, mi:mi + 1]))
                        smq = b.s_settle()
                        b.wait("vector", "scalar", smq)
                        b.flush("vector")
                        v1 = b.do("vector", lambda e: e.tensor_scalar(
                            out=pe_m[:], in0=pe_a[:], scalar1=float(1.0 / D), scalar2=None, op0=ALU.mult))
                        b.do_("vector", lambda e: e.tensor_scalar(
                            out=pe_c[:], in0=pe_b[:], scalar1=float(1.0 / D), scalar2=None, op0=ALU.mult))
                        b.wait("vector", "vector", v1)
                        v2 = b.do("vector", lambda e: e.tensor_mul(pe_a[:], pe_m[:], pe_m[:]))
                        b.wait("vector", "vector", v2)
                        v3 = b.do("vector", lambda e: e.tensor_sub(pe_c[:], pe_c[:], pe_a[:]))
                        b.wait("vector", "vector", v3)
                        vm = b.do("vector", lambda e: e.tensor_scalar(
                            out=pe_c[:], in0=pe_c[:], scalar1=EPS1, scalar2=None, op0=ALU.add))
                        b.wait("scalar", "vector", vm)
                        sA = b.do("scalar", lambda e: e.activation(out=pe_c[:], in_=pe_c[:], func=AF.Ln))
                        b.wait("scalar", "scalar", sA)
                        b.do_("scalar", lambda e: e.activation(
                            out=pe_c[:], in_=pe_c[:], func=AF.Exp, scale=-0.5))
                        smq = b.s_settle()
                        b.wait("vector", "scalar", smq)
                        vsm = b.do("vector", lambda e, ge=ge: e.tensor_copy(
                            sy_m[:, ge * 4:(ge + 1) * 4], pe_m[:, 0:4]))
                        b.wait("vector", "vector", vsm)
                        vsr = b.do("vector", lambda e, ge=ge: e.tensor_copy(
                            sy_rstd[:, ge * 4:(ge + 1) * 4], pe_c[:, 0:4]))
                        b.wait("vector", "vector", vsr)
                        v_xp = {}
                        t_tp5 = {}
                        v_xpev = {}
                        for mi in range(4):
                            rge = ge * 4 + mi
                            slot = mi % 2
                            if mi >= 2:
                                b.wait("sync", "vector", v_xp[mi - 2])
                            else:
                                b.wait("sync", "scalar", smq)
                                b.wait("sync", "vector", vsr)
                            ly = b.dma("sync", tok_y2[slot], lambda e, rge=rge, slot=slot: e.dma_start(
                                out=yblk[slot][:], in_=scr_y[rge * 128:(rge + 1) * 128, :]))
                            b.wait_tok("vector", tok_y2[slot], ly)
                            if mi >= 2:
                                b.wait("vector", "tensor", t_tp5[(mi - 2, 3)])
                            v_xp[mi] = b.do("vector", lambda e, slot=slot, rge=rge: e.tensor_scalar(
                                out=x1ptok[slot][:], in0=yblk[slot][:],
                                scalar1=sy_m[:, rge:rge + 1], scalar2=sy_rstd[:, rge:rge + 1],
                                op0=ALU.subtract, op1=ALU.mult))
                            b.wait("tensor", "vector", v_xp[mi])
                            for g4 in range(4):
                                pbk = g4 % 2
                                bidx = mi * 4 + g4
                                if bidx >= 2:
                                    b.wait("tensor", "vector", v_xpev[bidx - 2])
                                tm = 0
                                for jj in range(4):
                                    c = g4 * 4 + jj
                                    fn = lambda e, slot=slot, c=c, pbk=pbk, jj=jj: e.transpose(
                                        ps_tp4[pbk][:, jj * 128:(jj + 1) * 128],
                                        x1ptok[slot][:, c * 128:(c + 1) * 128], identb[:])
                                    if jj == 3:
                                        tm = b.do("tensor", fn)
                                    else:
                                        b.do_("tensor", fn)
                                t_tp5[(mi, g4)] = tm
                                b.wait("vector", "tensor", tm)
                                vv = b.do("vector", lambda e, g4=g4, mi=mi, pbk=pbk: e.tensor_copy(
                                    x1pT[:].rearrange("p (c f) -> p c f", f=512)[
                                        :, g4 * 4:(g4 + 1) * 4, mi * 128:(mi + 1) * 128],
                                    ps_tp4[pbk][:].rearrange("p (c f) -> p c f", f=128)))
                                v_xpev[bidx] = vv
                        xp_done = b.cnt["vector"]
                        b.wait("tensor", "vector", xp_done)
                        b.wait_tok("tensor", tok_el)
                        tm = 0
                        for c in range(NCH):
                            fn = lambda e, c=c: e.matmul(
                                ps_l[:], lua_sb[:, c * 16:(c + 1) * 16],
                                x1pT[:, c * 512:(c + 1) * 512],
                                start=(c == 0), stop=(c == NCH - 1))
                            if c == NCH - 1:
                                tm = b.do("tensor", fn)
                            else:
                                b.do_("tensor", fn)
                        b.wait("vector", "tensor", tm)
                        v_tup = b.do("vector", lambda e: e.tensor_copy(tupT[:], ps_l[:]))
                        t_fb_use = {}
                        t_oc = {}
                        s_relu = {}
                        lwu = 0
                        for oc in range(64):
                            fb = oc // 4
                            wslot = fb % 2
                            if oc % 4 == 0:
                                if fb > 1:
                                    b.wait("sync", "tensor", t_fb_use[fb - 2])
                                lwu = b.dma("sync", tok_wupt[wslot], lambda e, fb=fb, wslot=wslot: e.dma_start(
                                    out=wupt[wslot][:].rearrange("p (c f) -> p c f", f=512),
                                    in_=scr_wup[fb].rearrange("c p f -> p c f")))
                            slot = oc % 2
                            if oc >= 2:
                                b.wait("gpsimd", "tensor", t_oc[oc - 2])

                            llu = b.dma("gpsimd", tok_lub[slot], lambda e, oc=oc, slot=slot: e.dma_start(
                                out=lub_oc[slot][:], in_=I["lub"][:, oc * 128:(oc + 1) * 128]))
                            b.wait_tok("tensor", tok_wupt[wslot], lwu)
                            b.wait_tok("tensor", tok_lub[slot], llu)
                            if oc == 0:
                                b.wait("tensor", "vector", v_tup)
                                if t_dn_last_box[0]:
                                    b.wait("tensor", "tensor", t_dn_last_box[0]) if False else None
                            if oc >= 2:
                                b.wait("tensor", "scalar", s_relu[oc - 2])
                            for c in range(NCH):
                                b.do_("tensor", lambda e, c=c, oc=oc, slot=slot, wslot=wslot: e.matmul(
                                    ps_y1[slot][:],
                                    wupt[wslot][:, c * 512 + (oc % 4) * 128: c * 512 + (oc % 4) * 128 + 128],
                                    x1pT[:, c * 512:(c + 1) * 512],
                                    start=(c == 0), stop=False))
                            tm = b.do("tensor", lambda e, oc=oc, slot=slot: e.matmul(
                                ps_y1[slot][:], lub_oc[slot][:], tupT[:],
                                start=False, stop=True))
                            if oc % 4 == 3:
                                t_fb_use[fb] = tm
                            t_oc[oc] = tm
                            b.wait("scalar", "tensor", tm)
                            s_relu[oc] = b.do("scalar", lambda e, oc=oc, slot=slot: e.activation(
                                out=u2T[:, oc * 512:(oc + 1) * 512], in_=ps_y1[slot][:],
                                func=AF.Relu, bias=bup_pp[:, oc:oc + 1]))
                        sm_relu = b.s_settle()
                        b.wait("tensor", "scalar", sm_relu)
                        b.wait("tensor", "vector", v_tup)
                        tm = 0
                        for j in range(64):
                            fn = lambda e, j=j: e.matmul(
                                ps_l[:], lda_sb[:, j * 16:(j + 1) * 16],
                                u2T[:, j * 512:(j + 1) * 512],
                                start=(j == 0), stop=(j == 63))
                            if j == 63:
                                tm = b.do("tensor", fn)
                            else:
                                b.do_("tensor", fn)
                        b.wait("vector", "tensor", tm)
                        v_t2 = b.do("vector", lambda e: e.tensor_copy(t2T[:], ps_l[:]))
                        b.wait("tensor", "vector", v_t2)
                b.wait("sync", "vector", v_t2)
                b.wait("gpsimd", "vector", v_t2)
                with contextlib.ExitStack() as ped:
                    wdn_t = [sb(f"wdt{i}_{ge}", [128, 4 * 512], BF16, ped) for i in range(3)]
                    ycol = [sb(f"ycl0_{ge}", [128, 4 * 512], F32, ped),
                            sb(f"ycl1_{ge}", [128, 4 * 512], F32, ped)]
                    etmp = [sb(f"etp0_{ge}", [128, 512], F32, ped),
                            sb(f"etp1_{ge}", [128, 512], F32, ped)]
                    outst = sb(f"outst_{ge}", [128, 4 * D], F32, ped)
                    with contextlib.ExitStack() as peq:
                        ps_d = [ps(f"ps_e{i}_{ge}", [128, 512], F32, peq) for i in range(4)]
                        v_oev = {}
                        t_wd_use = {}
                        t_dn = 0
                        for nt in range(4):
                            slot = nt % 2
                            if nt >= 2:
                                b.wait("sync", "vector", v_oev[(nt - 2, 3)])
                            lyc = b.dma("sync", tok_yc[slot], lambda e, ge=ge, nt=nt, slot=slot: e.dma_start(
                                out=ycol[slot][:].rearrange("p (mi d) -> p mi d", d=512),
                                in_=scr_y[ge * 512:(ge + 1) * 512, nt * 512:(nt + 1) * 512
                                          ].rearrange("(mi p) d -> p mi d", p=128)))
                            for jb in range(16):
                                js = jb % 3
                                prevkey = (nt, jb - 3) if jb >= 3 else (nt - 1, {0: 15, 1: 13, 2: 14}[jb])
                                if prevkey in t_wd_use:
                                    b.wait("sync", "tensor", t_wd_use[prevkey])
                                lwd = b.dma("sync", tok_wdt[js], lambda e, jb=jb, nt=nt, js=js: e.dma_start(
                                    out=wdn_t[js][:].rearrange("p (j d) -> p j d", d=512),
                                    in_=scr_wdn[nt, jb * 4:(jb + 1) * 4].rearrange("j p d -> p j d")))
                                b.wait_tok("tensor", tok_wdt[js], lwd)
                                if jb == 0 and nt >= 1:
                                    b.wait("tensor", "vector", v_oev[(nt - 1, 3)])
                                tm = 0
                                for jj in range(4):
                                    j = jb * 4 + jj
                                    for mi in range(4):
                                        fn = lambda e, j=j, jj=jj, mi=mi, js=js: e.matmul(
                                            ps_d[mi][:],
                                            u2T[:, j * 512 + mi * 128: j * 512 + (mi + 1) * 128],
                                            wdn_t[js][:, jj * 512:(jj + 1) * 512],
                                            start=(j == 0), stop=False, skip_group_check=True)
                                        if mi == 3 and jj == 3:
                                            tm = b.do("tensor", fn)
                                        else:
                                            b.do_("tensor", fn)
                                t_wd_use[(nt, jb)] = tm
                            tm = 0
                            for mi in range(4):
                                fn = lambda e, mi=mi, nt=nt: e.matmul(
                                    ps_d[mi][:], t2T[:, mi * 128:(mi + 1) * 128],
                                    ldb_sb[:, nt * 512:(nt + 1) * 512],
                                    start=False, stop=True, skip_group_check=True)
                                if mi == 3:
                                    tm = b.do("tensor", fn)
                                else:
                                    b.do_("tensor", fn)
                            t_dn = tm
                            b.wait("vector", "tensor", tm)
                            b.wait_tok("vector", tok_yc[slot], lyc)
                            for mi in range(4):
                                vv1 = b.do("vector", lambda e, mi=mi, nt=nt: e.tensor_add(
                                    etmp[mi % 2][:], ps_d[mi][:],
                                    bdn_rep[:, nt * 512:(nt + 1) * 512]))
                                b.wait("vector", "vector", vv1)
                                v_oev[(nt, mi)] = b.do("vector", lambda e, mi=mi, nt=nt, slot=slot: e.tensor_add(
                                    outst[:, mi * D + nt * 512: mi * D + (nt + 1) * 512],
                                    etmp[mi % 2][:], ycol[slot][:, mi * 512:(mi + 1) * 512]))
                        b.wait("sync", "vector", v_oev[(3, 3)])
                        for mi in range(4):
                            b.dma("sync", tok_out, lambda e, ge=ge, mi=mi: e.dma_start(
                                out=out[(ge * 4 + mi) * 128:(ge * 4 + mi + 1) * 128, :],
                                in_=outst[:, mi * D:(mi + 1) * D]))
                        t_dn_last_box[0] = t_dn

            for ge in range(4):
                e_group(ge)
            b.barrier()

        if "kq" in taps:
            t = add_tap("kq", [128, 2 * NCH * S // 2 * 2], BF16) if False else add_tap("kq", [128, NCH * S], BF16)
            b.dma("sync", tok_tap, lambda e, t=t: e.dma_start(out=t[:], in_=big_b[:]))
        if "hT" in taps:
            t = add_tap("hT", [128, NCH * S], BF16)
            b.dma("sync", tok_tap, lambda e, t=t: e.dma_start(out=t[:], in_=big_a[:]))
        if "x1" in taps:
            t = add_tap("x1", [S, D], BF16)
            b.dma("sync", tok_tap, lambda e, t=t: e.dma_start(out=t[:], in_=scr_x1[:]))
        if "scrv" in taps:
            t = add_tap("scrv", [S, D], BF16)
            b.dma("sync", tok_tap, lambda e, t=t: e.dma_start(out=t[:], in_=scr_v[:]))

        if "hinT" in taps:
            t = add_tap("hinT", [128, NCH * S])
            with contextlib.ExitStack() as tp:
                tmp = sb("tap_tmp", [128, 2048], F32, tp)
                for c in range(NCH):
                    vmt = b.do("vector", lambda e, c=c: e.tensor_copy(
                        tmp[:], hinT[:, c * S:(c + 1) * S]))
                    b.wait("sync", "vector", vmt)
                    smt = b.dma("sync", tok_tap, lambda e, c=c, t=t: e.dma_start(
                        out=t[:, c * S:(c + 1) * S], in_=tmp[:]))
                    b.wait_tok("vector", tok_tap, smt)
            b.barrier()
        if "stats" in taps:
            t = add_tap("stats", [128, 3 * NRB])
            b.dma("sync", tok_tap, lambda e, t=t: e.dma_start(out=t[:, 0:16], in_=st_m[:]))
            b.dma("sync", tok_tap, lambda e, t=t: e.dma_start(out=t[:, 16:32], in_=st_rstd[:]))
            b.dma("sync", tok_tap, lambda e, t=t: e.dma_start(out=t[:, 32:48], in_=st_c1[:]))
        if "gcol" in taps:
            t = add_tap("gcol", [128, 128])
            b.dma("sync", tok_tap, lambda e, t=t: e.dma_start(out=t[:], in_=Gcol[:]))

        b.barrier()
        for tok in b.all_toks:
            b.wait_tok("vector", tok)
        b.mark("vector")

        with nc.Block() as block:
            b.emit(block)
    return nc, tap_outs


def make_consts():
    ident = np.eye(128, dtype=np.float32)
    dm = np.zeros((128, 4 * 512), np.float32)
    tl = np.arange(128)[:, None]
    sl = np.arange(512)[None, :]
    for k in range(4):
        dm[:, k * 512:(k + 1) * 512] = (128 * k + tl <= sl).astype(np.float32)
    return {
        "c_ident": ident, "c_dmask": dm,
        "c_ones_row": np.ones((1, 128), np.float32),
        "c_ones_col": np.ones((128, 1), np.float32),
    }


def make_in_maps(inputs):
    x = np.ascontiguousarray(np.asarray(inputs["L0_attn_x"], dtype=np.float32))
    w = {
        "n1w": inputs["L0_attn_norm1_w"], "wq": inputs["L0_attn_wq"],
        "wk": inputs["L0_attn_wk"], "wv": inputs["L0_attn_wv"],
        "wig": inputs["L0_attn_w_ig"], "big": inputs["L0_attn_b_ig"],
        "wfg": inputs["L0_attn_w_fg"], "bfg": inputs["L0_attn_b_fg"],
        "wog": inputs["L0_attn_w_og"], "mhw": inputs["L0_attn_mh_w"],
        "wout": inputs["L0_attn_w_out"], "n2w": inputs["L0_attn_norm2_w"],
        "wgate": inputs["L0_attn_w_gate"], "wup": inputs["L0_attn_w_up"],
        "wdown": inputs["L0_attn_w_down"],
        "q_up": inputs["L0_ffn_w_up_q"], "s_up": inputs["L0_ffn_w_up_scale"],
        "b_up": inputs["L0_ffn_b_up"],
        "lua": inputs["L0_ffn_w_up_lora_a"], "lub": inputs["L0_ffn_w_up_lora_b"],
        "q_dn": inputs["L0_ffn_w_down_q"], "s_dn": inputs["L0_ffn_w_down_scale"],
        "b_dn": inputs["L0_ffn_b_down"],
        "lda": inputs["L0_ffn_w_down_lora_a"], "ldb": inputs["L0_ffn_w_down_lora_b"],
    }
    w = {k: np.ascontiguousarray(np.asarray(v)) for k, v in w.items()}
    w.update(make_consts())
    maps = []
    for core in range(8):
        m = dict(w)
        m["x"] = np.ascontiguousarray(x[core])
        maps.append(m)
    return maps


def kernel(**inputs):
    maps = make_in_maps(inputs)
    nc, _ = build_core()
    res = run_bass_kernel_spmd(nc, maps, core_ids=list(range(8)))
    return np.stack([r["out"] for r in res.results], axis=0)


# version 07
# speedup vs baseline: 1.0411x; 1.0411x over previous
import contextlib
import numpy as np
import ml_dtypes
import concourse.bass as bass
import concourse.mybir as mybir
from concourse.bass_utils import run_bass_kernel_spmd

F32 = mybir.dt.float32
BF16 = mybir.dt.bfloat16
I32 = mybir.dt.int32
AF = mybir.ActivationFunctionType
ALU = mybir.AluOpType
BFNP = ml_dtypes.bfloat16

S = 2048
D = 2048
H = 8
DQK = 128
DV = 256
FF = 8192
NCH = 16
NRB = 16
EPS1 = 1e-5
EPS2 = 1e-6
CAPV = 15.0
LOG_ISQ = float(np.log(1.0 / np.sqrt(float(DQK))))

ENGS = ("sync", "gpsimd", "scalar", "vector", "tensor")


class Bld:
    def __init__(self, nc):
        self.nc = nc
        self.ops = {e: [] for e in ENGS}
        self.cnt = {e: 0 for e in ENGS}
        self.tail_inc = {e: False for e in ENGS}
        self.sems = {}

    def alloc_sems(self, es):
        for e in ENGS:
            self.sems[e] = es.enter_context(self.nc.semaphore("sem_" + e))
        for e in ("sync", "gpsimd"):
            self.sems[e + "_m"] = es.enter_context(self.nc.semaphore("sem_" + e + "_m"))
            self.cnt[e + "_m"] = 0
        self._ns = 0

    def dsem(self, stack):
        self._ns += 1
        tok = [stack.enter_context(self.nc.semaphore(f"ds{self._ns}")), 0]
        if not hasattr(self, "all_toks"):
            self.all_toks = []
        self.all_toks.append(tok)
        return tok

    def dma(self, q, tok, fn):
        tok[1] += 16

        def thunk(eng, fn=fn, sem=tok[0], tgt=tok[1]):
            fn(eng).then_inc(sem, 16)

        self.ops[q].append(thunk)
        self.tail_inc[q] = False
        return tok[1]

    def wait_tok(self, e, tok, val=None):
        if val is None:
            val = tok[1]
        if val <= 0:
            return
        self.ops[e].append(
            lambda eng, sem=tok[0], val=val: eng.wait_ge(sem, val))

    def do(self, e, fn):
        step = 16 if e in ("sync", "gpsimd") else 1
        self.cnt[e] += step

        def thunk(eng, fn=fn, e=e, step=step):
            fn(eng).then_inc(self.sems[e], step)

        self.ops[e].append(thunk)
        self.tail_inc[e] = True
        return self.cnt[e]

    def do_(self, e, fn):
        self.ops[e].append(fn)
        self.tail_inc[e] = False

    def wait(self, e, other, val):
        if val <= 0:
            return
        self.ops[e].append(
            lambda eng, other=other, val=val: eng.wait_ge(self.sems[other], val)
        )
        self.tail_inc[e] = False

    def mark(self, e):
        if e not in ("sync", "gpsimd") and self.tail_inc[e]:
            return self.cnt[e]
        if e in ("sync", "gpsimd"):
            self.cnt[e + "_m"] += 1

            def thunk(eng, e=e):
                eng.nop().then_inc(self.sems[e + "_m"], 1)

            self.ops[e].append(thunk)
            return self.cnt[e + "_m"]
        return self.do(e, lambda eng: eng.nop())

    def flush(self, e):
        self.wait(e, e, self.cnt[e])

    def s_settle(self):
        if getattr(self, "last_settle", 0):
            self.wait("scalar", "scalar", self.last_settle)
        self.last_settle = self.do("scalar", lambda eng: eng.activation(
            out=self.settle_dst[:], in_=self.settle_dst[:], func=AF.Abs))
        return self.last_settle

    def barrier(self):
        self.s_settle()
        for tok in getattr(self, "all_toks", []):
            if tok[1] > 0:
                self.wait_tok("sync", tok)
                self.wait_tok("gpsimd", tok)
        marks = {e: self.mark(e) for e in ENGS}
        for e in ENGS:
            for o in ENGS:
                if o != e:
                    sem = o + "_m" if o in ("sync", "gpsimd") else o
                    self.ops[e].append(
                        lambda eng, sem=sem, val=marks[o]: eng.wait_ge(self.sems[sem], val))

    def emit(self, block):
        def run(eng, e):
            for fn in self.ops[e]:
                fn(eng)

        block.sync(lambda eng: run(eng, "sync"))
        block.gpsimd(lambda eng: run(eng, "gpsimd"))
        block.scalar(lambda eng: run(eng, "scalar"))
        block.vector(lambda eng: run(eng, "vector"))
        block.tensor(lambda eng: run(eng, "tensor"))


def build_core(taps=()):
    nc = bass.Bass("TRN2", target_bir_lowering=False, debug=False)
    taps = set(taps)

    I = {}
    I["x"] = nc.dram_tensor("x", [S, D], F32, kind="ExternalInput").ap()
    for nm, shp in [
        ("n1w", [D]), ("wq", [D, 1024]), ("wk", [D, 1024]), ("wv", [D, D]),
        ("wig", [D, H]), ("big", [H]), ("wfg", [D, H]), ("bfg", [H]),
        ("wog", [D, D]), ("mhw", [D]), ("wout", [D, D]), ("n2w", [D]),
        ("wgate", [D, FF]), ("wup", [D, FF]), ("wdown", [FF, D]),
        ("s_up", [FF, 32]), ("b_up", [FF]),
        ("lua", [D, 16]), ("lub", [16, FF]),
        ("s_dn", [D, 128]), ("b_dn", [D]),
        ("lda", [FF, 16]), ("ldb", [16, D]),
    ]:
        I[nm] = nc.dram_tensor(nm, shp, F32, kind="ExternalInput").ap()
    I["q_up"] = nc.dram_tensor("q_up", [FF, D], I32, kind="ExternalInput").ap()
    I["q_dn"] = nc.dram_tensor("q_dn", [D, FF], I32, kind="ExternalInput").ap()
    I["c_ident"] = nc.dram_tensor("c_ident", [128, 128], F32, kind="ExternalInput").ap()
    I["c_dmask"] = nc.dram_tensor("c_dmask", [128, 4 * 512], F32, kind="ExternalInput").ap()
    I["c_ones_row"] = nc.dram_tensor("c_ones_row", [1, 128], F32, kind="ExternalInput").ap()
    I["c_ones_col"] = nc.dram_tensor("c_ones_col", [128, 1], F32, kind="ExternalInput").ap()

    out = nc.dram_tensor("out", [S, D], F32, kind="ExternalOutput").ap()

    scr_hin = nc.dram_tensor("scr_hin", [D, S], BF16, kind="Internal").ap()
    scr_v = nc.dram_tensor("scr_v", [S, D], BF16, kind="Internal").ap()
    scr_x1 = nc.dram_tensor("scr_x1", [S, D], BF16, kind="Internal").ap()
    scr_y = nc.dram_tensor("scr_y", [S, D], F32, kind="Internal").ap()
    scr_gme = nc.dram_tensor("scr_gme", [16, S], F32, kind="Internal").ap()
    scr_wup = nc.dram_tensor("scr_wup", [16, 16, 128, 512], BF16, kind="Internal").ap()
    scr_wdn = nc.dram_tensor("scr_wdn", [4, 64, 128, 512], BF16, kind="Internal").ap()

    tap_outs = {}

    def add_tap(name, shape, dtype=F32):
        tap_outs[name] = nc.dram_tensor("tap_" + name, shape, dtype, kind="ExternalOutput").ap()
        return tap_outs[name]

    b = Bld(nc)
    es = contextlib.ExitStack()
    with es:
        b.alloc_sems(es)

        def sb(name, shape, dtype, stack=es):
            return stack.enter_context(nc.sbuf_tensor(name, shape, dtype))

        def ps(name, shape, dtype, stack):
            return stack.enter_context(nc.psum_tensor(name, shape, dtype))

        identb = sb("identb", [128, 128], BF16)
        ident32 = sb("ident32", [128, 128], F32)
        ones_row = sb("ones_row", [1, 128], F32)
        settle_dst = sb("settle_dst", [128, 512], F32)
        b.settle_dst = settle_dst
        n2w_rep = sb("n2w_repg", [128, D], BF16)
        ones_colb = sb("ones_colb", [128, 1], BF16)
        n1w_pp = sb("n1w_pp", [128, NCH], F32)
        mhw_pp = sb("mhw_pp", [128, NCH], F32)
        bup_pp = sb("bup_pp", [128, 64], F32)
        big_sb = sb("big_sb", [8, 1], F32)
        bfg_sb = sb("bfg_sb", [8, 1], F32)
        b15i = sb("b15i", [8, 1], F32)
        b15f = sb("b15f", [8, 1], F32)
        one8 = sb("one8", [8, 1], F32)
        st_m = sb("st_m", [128, NRB], F32)
        st_rstd = sb("st_rstd", [128, NRB], F32)
        st_c1 = sb("st_c1", [128, NRB], F32)
        sy_m = sb("sy_m", [128, NRB], F32)
        sy_rstd = sb("sy_rstd", [128, NRB], F32)
        KOFF = 0
        QOFF = NCH * S // 2

        scope_ac = contextlib.ExitStack()
        es.enter_context(scope_ac)
        big_a = sb("big_a", [128, NCH * S], BF16, scope_ac)
        Gcol = sb("Gcol", [128, 128], F32, scope_ac)
        hinT = big_a

        vset = b.do("vector", lambda e: e.memset(settle_dst[:], 1.0))
        b.wait("scalar", "vector", vset)
        b.last_settle = 0
        tok_c = b.dsem(es)
        tok_g = b.dsem(es)
        tok_x = [b.dsem(es) for _ in range(4)]
        tok_scrh = b.dsem(es)
        tok_tap = b.dsem(es)
        with contextlib.ExitStack() as p0:
            wigt = sb("wig_sb", [128, NCH * H], BF16, p0)
            wfgt = sb("wfg_sb", [128, NCH * H], BF16, p0)
            gM = sb("gM", [8, S], F32, p0)
            gEn = sb("gEn", [8, S], F32, p0)
            b.dma("sync", tok_c, lambda e: e.dma_start(out=ident32[:], in_=I["c_ident"]))
            b.dma("gpsimd", tok_g, lambda e: e.dma_start(out=identb[:], in_=I["c_ident"]))
            b.dma("sync", tok_c, lambda e: e.dma_start(out=ones_row[:], in_=I["c_ones_row"]))
            b.dma("gpsimd", tok_g, lambda e: e.dma_start(out=ones_colb[:], in_=I["c_ones_col"]))
            for c in range(NCH):
                b.dma("sync", tok_c, lambda e, c=c: e.dma_start(
                    out=n1w_pp[:, c:c + 1], in_=I["n1w"][c * 128:(c + 1) * 128].unsqueeze(1)))
                b.dma("sync", tok_c, lambda e, c=c: e.dma_start(
                    out=mhw_pp[:, c:c + 1], in_=I["mhw"][c * 128:(c + 1) * 128].unsqueeze(1)))
                b.dma("gpsimd", tok_g, lambda e, c=c: e.dma_start(
                    out=wigt[:, c * H:(c + 1) * H], in_=I["wig"][c * 128:(c + 1) * 128, :]))
                b.dma("gpsimd", tok_g, lambda e, c=c: e.dma_start(
                    out=wfgt[:, c * H:(c + 1) * H], in_=I["wfg"][c * 128:(c + 1) * 128, :]))
            for c in range(64):
                b.dma("sync", tok_c, lambda e, c=c: e.dma_start(
                    out=bup_pp[:, c:c + 1], in_=I["b_up"][c * 128:(c + 1) * 128].unsqueeze(1)))
            b.dma("sync", tok_c, lambda e: e.dma_start(out=big_sb[:], in_=I["big"].unsqueeze(1)))
            b.dma("sync", tok_c, lambda e: e.dma_start(out=bfg_sb[:], in_=I["bfg"].unsqueeze(1)))
            b.wait_tok("vector", tok_c)
            b.wait_tok("vector", tok_g)
            b.do_("vector", lambda e: e.tensor_scalar(
                out=b15i[:], in0=big_sb[:], scalar1=float(1 / CAPV), scalar2=None, op0=ALU.mult))
            b.do_("vector", lambda e: e.tensor_scalar(
                out=b15f[:], in0=bfg_sb[:], scalar1=float(1 / CAPV), scalar2=None, op0=ALU.mult))
            b.do("vector", lambda e: e.memset(one8[:], 1.0))
            with contextlib.ExitStack() as p0r:
                n2w_row = sb("n2w_rowg", [1, D], F32, p0r)
                ps_p0 = ps("ps_p0", [128, 512], F32, p0r)
                tok_n2g = b.dsem(es)
                ln2 = b.dma("sync", tok_n2g, lambda e: e.dma_start(
                    out=n2w_row[:], in_=I["n2w"].unsqueeze(0)))
                b.wait_tok("tensor", tok_n2g, ln2)
                b.wait_tok("tensor", tok_c)
                b.wait_tok("tensor", tok_g)
                vrepm = 0
                for q4 in range(4):
                    if q4 > 0:
                        b.wait("tensor", "vector", vrepm)
                    trm = b.do("tensor", lambda e, q4=q4: e.matmul(
                        ps_p0[:], ones_row[:], n2w_row[0:1, q4 * 512:(q4 + 1) * 512],
                        start=True, stop=True))
                    b.wait("vector", "tensor", trm)
                    vrepm = b.do("vector", lambda e, q4=q4: e.tensor_copy(
                        n2w_rep[:, q4 * 512:(q4 + 1) * 512], ps_p0[:]))
            for e in ENGS:
                b.wait_tok(e, tok_c)
                b.wait_tok(e, tok_g)
            b.barrier()

            with contextlib.ExitStack() as pa:
                xbuf = [sb(f"xbuf{i}", [128, D], F32, pa) for i in range(4)]
                xsq = [sb("xsq0", [128, D], F32, pa), sb("xsq1", [128, D], F32, pa)]
                hb = [sb("hb0", [128, D], BF16, pa), sb("hb1", [128, D], BF16, pa)]
                pa_a = sb("pa_a", [128, 512], F32, pa)
                pa_b = sb("pa_b", [128, 512], F32, pa)
                pa_m = sb("pa_m", [128, 512], F32, pa)
                pa_t = sb("pa_t", [128, 512], F32, pa)
                pa_t2 = sb("pa_t2", [128, 512], F32, pa)
                pa_c = sb("pa_c", [128, 512], F32, pa)
                b.do_("vector", lambda e: e.memset(pa_a[:], 1.0))
                vmem = b.do("vector", lambda e: e.memset(pa_b[:], 1.0))
                b.wait("scalar", "vector", vmem)
                b.flush("vector")
                vdone = {}
                sdone = {}
                lxs = {}
                for rb in range(NRB):
                    bi = rb % 2
                    if rb >= 2:
                        b.wait("sync", "vector", vdone[rb - 2])
                        b.wait("sync", "scalar", sdone[rb - 2])
                    lxs[rb] = b.dma("sync", tok_x[bi], lambda e, rb=rb, bi=bi: e.dma_start(
                        out=xbuf[bi][:], in_=I["x"][rb * 128:(rb + 1) * 128, :]))
                    b.wait_tok("vector", tok_x[bi], lxs[rb])
                    vdone[rb] = b.do("vector", lambda e, rb=rb, bi=bi: e.reduce_sum(
                        pa_a[:, rb:rb + 1], xbuf[bi][:], axis=mybir.AxisListType.X))
                    b.wait_tok("scalar", tok_x[bi], lxs[rb])
                    sdone[rb] = b.do("scalar", lambda e, rb=rb, bi=bi: e.activation(
                        out=xsq[bi][:], in_=xbuf[bi][:], func=AF.Square,
                        accum_out=pa_b[:, rb:rb + 1]))
                sm = b.s_settle()
                b.wait("vector", "scalar", sm)
                b.flush("vector")
                v1 = b.do("vector", lambda e: e.tensor_scalar(
                    out=pa_m[:], in0=pa_a[:], scalar1=float(1.0 / D), scalar2=None, op0=ALU.mult))
                b.do_("vector", lambda e: e.tensor_scalar(
                    out=pa_t[:], in0=pa_b[:], scalar1=float(1.0 / D), scalar2=None, op0=ALU.mult))
                b.wait("vector", "vector", v1)
                v2 = b.do("vector", lambda e: e.tensor_mul(pa_t2[:], pa_m[:], pa_m[:]))
                b.wait("vector", "vector", v2)
                v3 = b.do("vector", lambda e: e.tensor_sub(pa_t[:], pa_t[:], pa_t2[:]))
                b.wait("vector", "vector", v3)
                vm = b.do("vector", lambda e: e.tensor_scalar(
                    out=pa_t2[:], in0=pa_t[:], scalar1=EPS1, scalar2=None, op0=ALU.add))
                b.wait("scalar", "vector", vm)
                s1 = b.do("scalar", lambda e: e.activation(out=pa_b[:], in_=pa_t2[:], func=AF.Ln))
                b.wait("scalar", "scalar", s1)
                s2 = b.do("scalar", lambda e: e.activation(
                    out=pa_c[:], in_=pa_b[:], func=AF.Exp, scale=-1.0))
                b.wait("scalar", "scalar", s2)
                b.do_("scalar", lambda e: e.activation(
                    out=pa_b[:], in_=pa_b[:], func=AF.Exp, scale=-0.5))
                sm = b.s_settle()
                b.wait("vector", "scalar", sm)
                v4 = b.do("vector", lambda e: e.tensor_mul(pa_a[:], pa_t[:], pa_c[:]))
                b.wait("vector", "vector", v4)
                vm = b.do("vector", lambda e: e.tensor_scalar(
                    out=pa_a[:], in0=pa_a[:], scalar1=EPS2, scalar2=None, op0=ALU.add))
                b.wait("scalar", "vector", vm)
                s4 = b.do("scalar", lambda e: e.activation(out=pa_c[:], in_=pa_a[:], func=AF.Ln))
                b.wait("scalar", "scalar", s4)
                b.do_("scalar", lambda e: e.activation(
                    out=pa_t2[:], in_=pa_c[:], func=AF.Exp, scale=-0.5))
                sm = b.s_settle()
                b.wait("vector", "scalar", sm)
                v5 = b.do("vector", lambda e: e.tensor_mul(pa_a[:], pa_b[:], pa_t2[:]))
                b.do_("vector", lambda e: e.tensor_copy(st_m[:], pa_m[:, 0:NRB]))
                b.do_("vector", lambda e: e.tensor_copy(st_rstd[:], pa_b[:, 0:NRB]))
                b.wait("vector", "vector", v5)
                vm = b.do("vector", lambda e: e.tensor_copy(st_c1[:], pa_a[:, 0:NRB]))

                with contextlib.ExitStack() as pt:
                    ps_tp = [ps(f"ps_tp{i}", [128, 512], BF16, pt) for i in range(4)]
                    b.wait("vector", "vector", vm)
                    vdone2 = {}
                    tdone = {}
                    vevac = {}
                    for rb in range(NRB):
                        bi = rb % 4
                        if rb < 4:
                            b.wait("sync", "vector", vm)
                            b.wait("sync", "scalar", b.cnt["scalar"])
                        else:
                            b.wait("sync", "vector", vdone2[rb - 4])
                        lx = b.dma("sync", tok_x[bi], lambda e, rb=rb, bi=bi: e.dma_start(
                            out=xbuf[bi][:], in_=I["x"][rb * 128:(rb + 1) * 128, :]))
                        b.wait_tok("vector", tok_x[bi], lx)
                        if rb >= 2:
                            b.wait("vector", "tensor", tdone[(rb - 2) * 4 + 3])
                        hm = b.do("vector", lambda e, rb=rb, bi=bi: e.tensor_scalar(
                            out=hb[rb % 2][:], in0=xbuf[bi][:],
                            scalar1=st_m[:, rb:rb + 1], scalar2=st_c1[:, rb:rb + 1],
                            op0=ALU.subtract, op1=ALU.mult))
                        vdone2[rb] = hm
                        b.wait("tensor", "vector", hm)
                        for g4 in range(4):
                            pb = g4 % 4
                            bidx = rb * 4 + g4
                            if bidx >= 4:
                                b.wait("tensor", "vector", vevac[bidx - 4])
                            for j in range(4):
                                c = g4 * 4 + j
                                fn = lambda e, rb=rb, c=c, pb=pb, j=j: e.transpose(
                                    ps_tp[pb][:, j * 128:(j + 1) * 128],
                                    hb[rb % 2][:, c * 128:(c + 1) * 128], identb[:])
                                if j == 3:
                                    tdone[bidx] = b.do("tensor", fn)
                                else:
                                    b.do_("tensor", fn)
                            b.wait("vector", "tensor", tdone[bidx])
                            for j in range(4):
                                c = g4 * 4 + j
                                fn = lambda e, c=c, pb=pb, j=j, rb=rb: e.tensor_scalar(
                                    out=hinT[:, c * S + rb * 128: c * S + (rb + 1) * 128],
                                    in0=ps_tp[pb][:, j * 128:(j + 1) * 128],
                                    scalar1=n1w_pp[:, c:c + 1], scalar2=None, op0=ALU.mult)
                                if j == 3:
                                    vevac[bidx] = b.do("vector", fn)
                                else:
                                    b.do_("vector", fn)
                        if rb % 4 == 3:
                            g = rb // 4
                            b.wait("sync", "vector", b.cnt["vector"])
                            b.dma("sync", tok_scrh, lambda e, g=g: e.dma_start(
                                out=scr_hin[:, g * 512:(g + 1) * 512].rearrange(
                                    "(c p) s -> p c s", p=128),
                                in_=hinT[:].rearrange("p (c s) -> p c s", s=S)[
                                    :, :, g * 512:(g + 1) * 512]))
                b.barrier()

            with contextlib.ExitStack() as pg:
                gA = sb("gA", [8, S], F32, pg)
                gB = sb("gB", [8, S], F32, pg)
                gC = sb("gC", [8, S], F32, pg)
                gZ = sb("gZ", [8, S], F32, pg)
                with contextlib.ExitStack() as pgp:
                    ps_ig = ps("ps_ig", [8, S], F32, pgp)
                    ps_fg = ps("ps_fg", [8, S], F32, pgp)
                    tg = 0
                    for st in range(4):
                        for c in range(NCH):
                            b.do_("tensor", lambda e, st=st, c=c: e.matmul(
                                ps_ig[:, st * 512:(st + 1) * 512],
                                wigt[:, c * H:(c + 1) * H],
                                hinT[:, c * S + st * 512: c * S + (st + 1) * 512],
                                start=(c == 0), stop=(c == NCH - 1)))
                        for c in range(NCH):
                            fn = lambda e, st=st, c=c: e.matmul(
                                ps_fg[:, st * 512:(st + 1) * 512],
                                wfgt[:, c * H:(c + 1) * H],
                                hinT[:, c * S + st * 512: c * S + (st + 1) * 512],
                                start=(c == 0), stop=(c == NCH - 1))
                            if c == NCH - 1:
                                tg = b.do("tensor", fn)
                            else:
                                b.do_("tensor", fn)
                    b.wait("scalar", "tensor", tg)
                    s1 = b.do("scalar", lambda e: e.activation(
                        out=gA[:], in_=ps_ig[:], func=AF.Tanh, scale=float(1 / CAPV),
                        bias=b15i[:]))
                    sm = b.do("scalar", lambda e: e.activation(
                        out=gB[:], in_=ps_fg[:], func=AF.Tanh, scale=float(1 / CAPV),
                        bias=b15f[:]))
                    b.wait("vector", "scalar", sm)
                    b.do_("vector", lambda e: e.memset(gZ[:], 0.0))
                    b.do_("vector", lambda e: e.tensor_scalar(
                        out=gA[:], in0=gA[:], scalar1=CAPV, scalar2=None, op0=ALU.mult))
                    vm = b.do("vector", lambda e: e.tensor_scalar(
                        out=gB[:], in0=gB[:], scalar1=CAPV, scalar2=None, op0=ALU.mult))
                if "gates" in taps:
                    t = add_tap("gates_if", [8, 2 * S])
                    b.wait("sync", "vector", vm)
                    b.dma("sync", tok_tap, lambda e, t=t: e.dma_start(out=t[:, 0:S], in_=gA[:]))
                    b.dma("sync", tok_tap, lambda e, t=t: e.dma_start(out=t[:, S:2 * S], in_=gB[:]))
                b.wait("scalar", "vector", vm)
                if "gates" in taps:
                    b.wait_tok("scalar", tok_tap)
                s2 = b.do("scalar", lambda e: e.activation(out=gC[:], in_=gB[:], func=AF.Exp, scale=-1.0))
                b.wait("scalar", "scalar", s2)
                sm = b.do("scalar", lambda e: e.activation(out=gB[:], in_=gC[:], func=AF.Ln, bias=one8[:]))
                b.wait("vector", "scalar", sm)
                b.flush("vector")
                v1 = b.do("vector", lambda e: e.tensor_tensor_scan(
                    out=gC[:], data0=gB[:], data1=gZ[:], initial=0.0,
                    op0=ALU.add, op1=ALU.add))
                b.wait("vector", "vector", v1)
                v2 = b.do("vector", lambda e: e.tensor_add(gA[:], gA[:], gC[:]))
                b.wait("vector", "vector", v2)
                v3 = b.do("vector", lambda e: e.tensor_tensor_scan(
                    out=gM[:], data0=gA[:], data1=gZ[:], initial=-1e30,
                    op0=ALU.max, op1=ALU.add))
                b.wait("vector", "vector", v3)
                vm = b.do("vector", lambda e: e.tensor_sub(gB[:], gC[:], gM[:]))
                b.wait("scalar", "vector", vm)
                sm = b.do("scalar", lambda e: e.activation(out=gEn[:], in_=gB[:], func=AF.Exp))
                if "gates" in taps:
                    t = add_tap("gates2", [8, 4 * S])
                    b.wait("sync", "vector", vm)
                    b.wait("sync", "scalar", sm)
                    b.dma("sync", tok_tap, lambda e, t=t: e.dma_start(out=t[:, 0:S], in_=gC[:]))
                    b.dma("sync", tok_tap, lambda e, t=t: e.dma_start(out=t[:, S:2 * S], in_=gA[:]))
                    b.dma("sync", tok_tap, lambda e, t=t: e.dma_start(out=t[:, 2 * S:3 * S], in_=gM[:]))
                    b.dma("sync", tok_tap, lambda e, t=t: e.dma_start(out=t[:, 3 * S:4 * S], in_=gEn[:]))
                b.wait("sync", "vector", vm)
                b.wait("sync", "scalar", sm)
                b.dma("sync", tok_tap, lambda e: e.dma_start(out=scr_gme[0:8, :], in_=gM[:]))
                b.dma("sync", tok_tap, lambda e: e.dma_start(out=scr_gme[8:16, :], in_=gEn[:]))
                with contextlib.ExitStack() as pgc:
                    ps_gc = ps("ps_gc", [128, 32], F32, pgc)
                    b.wait("tensor", "vector", vm)
                    for g4 in range(4):
                        for j in range(4):
                            tc = g4 * 4 + j
                            fn = lambda e, tc=tc, j=j: e.transpose(
                                ps_gc[:, j * 8:(j + 1) * 8], gA[:, tc * 128:(tc + 1) * 128],
                                ident32[0:8, 0:8])
                            if j == 3:
                                tm = b.do("tensor", fn)
                            else:
                                b.do_("tensor", fn)
                        b.wait("vector", "tensor", tm)
                        vmg = b.do("vector", lambda e, g4=g4: e.tensor_scalar(
                            out=Gcol[:, g4 * 32:(g4 + 1) * 32], in0=ps_gc[:],
                            scalar1=LOG_ISQ, scalar2=None, op0=ALU.add))
                        b.wait("tensor", "vector", vmg)
                a2_sig = b.cnt["scalar"]
                a2_vm = b.cnt["vector"]
                for e_ in ("tensor", "sync", "gpsimd"):
                    b.wait(e_, "scalar", a2_sig)
                    b.wait(e_, "vector", a2_vm)
                for tok in b.all_toks:
                    if tok[1] > 0:
                        b.wait_tok("sync", tok)
                        b.wait_tok("gpsimd", tok)

        scope_bm = contextlib.ExitStack()
        scope_ac.enter_context(scope_bm)
        big_b = sb("big_b", [128, NCH * S], BF16, scope_bm)
        with contextlib.ExitStack() as pb:
            wkq_cb = [sb("wkq0", [128, NCH * 128], BF16, pb),
                      sb("wkq1", [128, NCH * 128], BF16, pb)]
            wv_ct = sb("wv_ct", [128, NCH * 512], BF16, pb)
            vstage = sb("vstage", [128, NCH * 512], BF16, pb)
            tok_wb = [b.dsem(es), b.dsem(es)]
            tok_wv = b.dsem(es)
            tok_scrv = b.dsem(es)
            with contextlib.ExitStack() as pbp:
                ps_b = [ps(f"ps_b{i}", [128, 512], F32, pbp) for i in range(4)]
                t_used = {}
                v_evac = {}
                lwc = {}
                for idx, (wname, off) in enumerate([("wk", KOFF), ("wq", QOFF)]):
                    for oc in range(8):
                        gi = idx * 8 + oc
                        slot = gi % 2
                        if gi >= 2:
                            b.wait("gpsimd", "tensor", t_used[gi - 2])
                        lwc[gi] = b.dma("gpsimd", tok_wb[slot], lambda e, wname=wname, oc=oc, slot=slot: e.dma_start(
                            out=wkq_cb[slot][:].rearrange("p (c o) -> p c o", o=128),
                            in_=I[wname][:, oc * 128:(oc + 1) * 128].rearrange(
                                "(c p) o -> p c o", p=128)))
                        b.wait_tok("tensor", tok_wb[slot], lwc[gi])
                        for nt in range(4):
                            ti = gi * 4 + nt
                            pbk = nt % 4
                            if ti >= 4:
                                b.wait("tensor", "vector", v_evac[ti - 4])
                            tm = 0
                            for c in range(NCH):
                                fn = lambda e, slot=slot, c=c, pbk=pbk, nt=nt: e.matmul(
                                    ps_b[pbk][:],
                                    wkq_cb[slot][:, c * 128:(c + 1) * 128],
                                    hinT[:, c * S + nt * 512: c * S + (nt + 1) * 512],
                                    start=(c == 0), stop=(c == NCH - 1))
                                if c == NCH - 1:
                                    tm = b.do("tensor", fn)
                                else:
                                    b.do_("tensor", fn)
                            if nt == 3:
                                t_used[gi] = tm
                            b.wait("vector", "tensor", tm)
                            v_evac[ti] = b.do("vector", lambda e, off=off, oc=oc, nt=nt, pbk=pbk: e.tensor_copy(
                                big_b[:, off + oc * S + nt * 512: off + oc * S + (nt + 1) * 512],
                                ps_b[pbk][:]))
                vm_dma = 0
                for dvt in range(4):
                    if dvt > 0:
                        b.wait("gpsimd", "tensor", t_used[("v", dvt - 1)])
                    lv = b.dma("gpsimd", tok_wv, lambda e, dvt=dvt: e.dma_start(
                        out=wv_ct[:].rearrange("p (c o) -> p c o", o=512),
                        in_=I["wv"][:, dvt * 512:(dvt + 1) * 512].rearrange(
                            "(c p) o -> p c o", p=128)))
                    b.wait_tok("tensor", tok_wv, lv)
                    if dvt > 0:
                        b.wait_tok("tensor", tok_scrv, vm_dma)
                    for m in range(NRB):
                        pbk = m % 4
                        if m >= 4:
                            b.wait("tensor", "vector", v_evac[("v", dvt, m - 4)])
                        elif dvt > 0:
                            b.wait("tensor", "vector", v_evac[("v", dvt - 1, NRB - 1)])
                        else:
                            b.wait("tensor", "vector", v_evac[63])
                        tm = 0
                        for c in range(NCH):
                            fn = lambda e, c=c, m=m, pbk=pbk: e.matmul(
                                ps_b[pbk][:],
                                hinT[:, c * S + m * 128: c * S + (m + 1) * 128],
                                wv_ct[:, c * 512:(c + 1) * 512],
                                start=(c == 0), stop=(c == NCH - 1))
                            if c == NCH - 1:
                                tm = b.do("tensor", fn)
                            else:
                                b.do_("tensor", fn)
                        if m == NRB - 1:
                            t_used[("v", dvt)] = tm
                        b.wait("vector", "tensor", tm)
                        v_evac[("v", dvt, m)] = b.do("vector", lambda e, m=m, pbk=pbk: e.tensor_copy(
                            vstage[:, m * 512:(m + 1) * 512], ps_b[pbk][:]))
                    b.wait("sync", "vector", v_evac[("v", dvt, NRB - 1)])
                    vm_dma = b.dma("sync", tok_scrv, lambda e, dvt=dvt: e.dma_start(
                        out=scr_v[:, dvt * 512:(dvt + 1) * 512].rearrange(
                            "(m p) d -> p m d", p=128),
                        in_=vstage[:].rearrange("p (m d) -> p m d", d=512)))
            b.barrier()

        hT = big_a
        with contextlib.ExitStack() as pm:
            V_h = [sb("V_h0", [128, NCH * 256], BF16, pm), sb("V_h1", [128, NCH * 256], BF16, pm)]
            M_rep = sb("M_rep", [128, S], F32, pm)
            Mrow1 = sb("Mrow1", [1, S], F32, pm)
            EnRow1 = sb("EnRow1", [1, 512], F32, pm)
            E_sb = [sb(f"E_sb{i}", [128, 512], F32, pm) for i in range(3)]
            C_sb = [sb(f"C_sb{i}", [128, 512], BF16, pm) for i in range(3)]
            nrep = sb("nrep", [128, 512], F32, pm)
            nabs = sb("nabs", [1, 512], F32, pm)
            nmax = sb("nmax", [1, 512], F32, pm)
            ninv = sb("ninv", [1, 512], F32, pm)
            dmask_sb = sb("dmask_sb", [128, 4 * 512], F32, pm)
            tok_vh = [b.dsem(es), b.dsem(es)]
            tok_row = b.dsem(es)
            tok_er = b.dsem(es)
            tok_dm = b.dsem(es)
            b.dma("sync", tok_dm, lambda e: e.dma_start(out=dmask_sb[:], in_=I["c_dmask"]))
            for e in ("vector",):
                b.wait_tok(e, tok_dm)
            with contextlib.ExitStack() as pmp:
                ps_qk = [ps(f"ps_qk{i}", [128, 512], F32, pmp) for i in range(2)]
                ps_h = [ps(f"ps_h{i}", [128, 512], F32, pmp) for i in range(4)]
                ps_row = ps("ps_row", [1, 512], F32, pmp)
                ps_rep2 = ps("ps_rep2", [128, 512], F32, pmp)
                sE_prev = {}
                vC = {}
                last_vh_use = {}
                s_absmark = 0
                v_hevac = 0
                v_hevac_pair = {}
                trep = 0
                last_rep_read = 0
                ti = 0
                for h in range(H):
                    slot = h % 2
                    if h >= 2:
                        b.wait("sync", "tensor", last_vh_use[h - 2])
                    lv = b.dma("sync", tok_vh[slot], lambda e, h=h, slot=slot: e.dma_start(
                        out=V_h[slot][:].rearrange("p (tc d) -> p tc d", d=256),
                        in_=scr_v[:, h * 256:(h + 1) * 256].rearrange(
                            "(tc p) d -> p tc d", p=128)))
                    if h > 0:
                        b.wait("sync", "tensor", mrep_t)
                    lr = b.dma("sync", tok_row, lambda e, h=h: e.dma_start(
                        out=Mrow1[:], in_=scr_gme[h:h + 1, :]))
                    b.wait_tok("tensor", tok_row, lr)
                    if h > 0:
                        b.wait("tensor", "scalar", sE_prev[h - 1])
                        b.wait("tensor", "vector", last_rep_read)
                    for q4 in range(4):
                        if q4 > 0:
                            b.wait("tensor", "vector", last_rep_read)
                        mrep_t = b.do("tensor", lambda e, q4=q4: e.matmul(
                            ps_rep2[:], ones_row[:], Mrow1[0:1, q4 * 512:(q4 + 1) * 512],
                            start=True, stop=True))
                        b.wait("vector", "tensor", mrep_t)
                        v_mrep_evac = b.do("vector", lambda e, q4=q4: e.tensor_copy(
                            M_rep[:, q4 * 512:(q4 + 1) * 512], ps_rep2[:]))
                        last_rep_read = v_mrep_evac
                    for st in range(4):
                        ntc = 4 * (st + 1)
                        if not (h == 0 and st == 0):
                            b.wait("sync", "vector", v_nmax)
                        ler = b.dma("sync", tok_er, lambda e, h=h, st=st: e.dma_start(
                            out=EnRow1[:], in_=scr_gme[8 + h:9 + h, st * 512:(st + 1) * 512]))
                        hx = st % 2
                        def emit_qk(tc, qx):
                            b.wait_tok("tensor", tok_vh[slot], lv)
                            return b.do("tensor", lambda e, h=h, st=st, tc=tc, qx=qx: e.matmul(
                                ps_qk[qx][:],
                                big_b[:, KOFF + h * S + tc * 128: KOFF + h * S + (tc + 1) * 128],
                                big_b[:, QOFF + h * S + st * 512: QOFF + h * S + (st + 1) * 512],
                                start=True, stop=True))
                        if ti >= 2:
                            b.wait("tensor", "vector", vC[ti - 2])
                        tqk_pend = emit_qk(0, ti % 2)
                        for tc in range(ntc):
                            qx = ti % 2
                            tqk = tqk_pend
                            if ti >= 2:
                                b.wait("scalar", "vector", vC[ti - 2])
                            b.wait("scalar", "vector", v_mrep_evac)
                            sE = b.do("scalar", lambda e, h=h, st=st, tc=tc, qx=qx: e.activation(
                                out=E_sb[qx][:], in_=M_rep[:, st * 512:(st + 1) * 512],
                                func=AF.Exp, scale=-1.0,
                                bias=Gcol[:, tc * 8 + h: tc * 8 + h + 1]))
                            sE_prev[h] = sE
                            b.wait("vector", "scalar", sE)
                            if tc >= 4 * st:
                                k = tc - 4 * st
                                vmm = b.do("vector", lambda e, qx=qx, k=k: e.tensor_mul(
                                    E_sb[qx][:], E_sb[qx][:],
                                    dmask_sb[:, k * 512:(k + 1) * 512]))
                                b.wait("vector", "vector", vmm)
                            b.wait("vector", "tensor", tqk)
                            vC[ti] = b.do("vector", lambda e, qx=qx: e.tensor_mul(
                                C_sb[qx][:], ps_qk[qx][:], E_sb[qx][:]))
                            if tc + 1 < ntc:
                                if ti + 1 >= 2:
                                    b.wait("tensor", "vector", vC[ti - 1])
                                tqk_pend = emit_qk(tc + 1, (ti + 1) % 2)
                            b.wait("tensor", "vector", vC[ti])
                            if tc == 0 and s_absmark:
                                b.wait("tensor", "scalar", s_absmark)
                            b.do_("tensor", lambda e, tc=tc, qx=qx, ntc=ntc: e.matmul(
                                ps_row[:], ones_colb[:], C_sb[qx][:],
                                start=(tc == 0), stop=(tc == ntc - 1),
                                skip_group_check=True))
                            if tc == 0 and v_hevac_pair.get(hx):
                                b.wait("tensor", "vector", v_hevac_pair[hx])
                            for dvc in range(2):
                                fn = lambda e, slot=slot, tc=tc, dvc=dvc, qx=qx, ntc=ntc, hx=hx: e.matmul(
                                    ps_h[2 * hx + dvc][:],
                                    V_h[slot][:, tc * 256 + dvc * 128: tc * 256 + dvc * 128 + 128],
                                    C_sb[qx][:],
                                    start=(tc == 0), stop=(tc == ntc - 1),
                                    skip_group_check=True)
                                if dvc == 1 and tc == ntc - 1:
                                    th_done = b.do("tensor", fn)
                                else:
                                    b.do_("tensor", fn)
                            ti += 1
                        last_vh_use[h] = th_done
                        b.wait("scalar", "tensor", th_done)
                        s_absmark = b.do("scalar", lambda e: e.activation(
                            out=nabs[:], in_=ps_row[:], func=AF.Abs))
                        b.wait("vector", "scalar", s_absmark)
                        b.wait_tok("vector", tok_er, ler)
                        v_nmax = b.do("vector", lambda e: e.tensor_max(
                            nmax[:], nabs[:], EnRow1[0:1, :]))
                        b.wait("scalar", "vector", v_nmax)
                        if trep:
                            b.wait("scalar", "tensor", trep)
                        sl = b.do("scalar", lambda e: e.activation(out=ninv[:], in_=nmax[:], func=AF.Ln))
                        b.wait("scalar", "scalar", sl)
                        sl2 = b.do("scalar", lambda e: e.activation(
                            out=ninv[:], in_=ninv[:], func=AF.Exp, scale=-1.0))
                        sl2 = b.s_settle()
                        b.wait("tensor", "scalar", sl2)
                        b.wait("tensor", "vector", last_rep_read)
                        trep = b.do("tensor", lambda e: e.matmul(
                            ps_rep2[:], ones_row[:], ninv[:], start=True, stop=True))
                        b.wait("vector", "tensor", trep)
                        vrep = b.do("vector", lambda e: e.tensor_copy(nrep[:], ps_rep2[:]))
                        last_rep_read = vrep
                        b.wait("vector", "vector", vrep)
                        for dvc in range(2):
                            r = h * 2 + dvc
                            fn = lambda e, r=r, st=st, dvc=dvc, hx=hx: e.tensor_mul(
                                hT[:, r * S + st * 512: r * S + (st + 1) * 512],
                                ps_h[2 * hx + dvc][:], nrep[:])
                            v_hevac = b.do("vector", fn)
                        v_hevac_pair[hx] = v_hevac
            b.barrier()
        scope_bm.close()

        with contextlib.ExitStack() as pc:
            hintile = sb("hintile", [128, NCH * 512], BF16, pc)
            sig_blk = sb("sig_blk", [128, NCH * 512], BF16, pc)
            houtT = sb("houtT", [128, NCH * 512], BF16, pc)
            wog_cb = [sb("wog0", [128, NCH * 128], BF16, pc),
                      sb("wog1", [128, NCH * 128], BF16, pc)]
            wout_nt = [sb("wout_nt0", [128, NCH * 512], BF16, pc),
                       sb("wout_nt1", [128, NCH * 512], BF16, pc)]
            xc_all = sb("xc_all", [128, 4 * D], BF16, pc)
            x1st = sb("x1st", [128, 4 * D], BF16, pc)
            hsq = sb("hsq", [128, 512], BF16, pc)
            hn1 = sb("hn1", [1, 512], F32, pc)
            rstd_rep = [sb("rsr0", [128, 512], F32, pc), sb("rsr1", [128, 512], F32, pc)]
            tmp1 = [sb("tmp1a", [128, 512], BF16, pc), sb("tmp1b", [128, 512], BF16, pc)]
            xa_t = [sb("xa0", [128, 512], F32, pc), sb("xa1", [128, 512], F32, pc)]
            tok_hin = b.dsem(es)
            tok_wog = [b.dsem(es), b.dsem(es)]
            tok_wot = [b.dsem(es), b.dsem(es)]
            tok_xc = b.dsem(es)
            tok_x1w = b.dsem(es)
            with contextlib.ExitStack() as pcp:
                ps_o = [ps(f"ps_o{i}", [128, 512], F32, pcp) for i in range(4)]
                ps_hn = ps("ps_hn", [1, 512], F32, pcp)
                ps_rr = ps("ps_rr", [128, 512], F32, pcp)
                ps_x1 = [ps("ps_x10", [128, 512], F32, pcp), ps("ps_x11", [128, 512], F32, pcp)]
                t_wog_used = {}
                s_sig = {}
                v_hout = {}
                v_hn1 = 0
                v_rrevac = 0
                t_x1_last = 0
                t_x1w_use = {}
                t_hin_last = {}
                v_x1evac = {}
                v_hout_last = {}
                v_xa_last = {}
                x1w = {}
                t_rr = 0
                lh_pend = b.dma("sync", tok_hin, lambda e: e.dma_start(
                    out=hintile[:].rearrange("p (c s) -> p c s", s=512),
                    in_=scr_hin[:, 0:512].rearrange("(c p) s -> p c s", p=128)))
                for st in range(4):
                    lh = lh_pend
                    if st > 0:
                        b.wait("scalar", "vector", v_hout_last[(st - 1, H - 1)])
                    for oc in range(NCH):
                        gi = st * NCH + oc
                        slot = gi % 2
                        if gi >= 2:
                            b.wait("gpsimd", "tensor", t_wog_used[gi - 2])
                        lw = b.dma("gpsimd", tok_wog[slot], lambda e, oc=oc, slot=slot: e.dma_start(
                            out=wog_cb[slot][:].rearrange("p (c o) -> p c o", o=128),
                            in_=I["wog"][:, oc * 128:(oc + 1) * 128].rearrange(
                                "(c p) o -> p c o", p=128)))
                        b.wait_tok("tensor", tok_wog[slot], lw)
                        b.wait_tok("tensor", tok_hin, lh)
                        if gi >= 4:
                            b.wait("tensor", "scalar", s_sig[gi - 4])
                        tm = 0
                        for c in range(NCH):
                            fn = lambda e, slot=slot, gi=gi, c=c: e.matmul(
                                ps_o[gi % 4][:],
                                wog_cb[slot][:, c * 128:(c + 1) * 128],
                                hintile[:, c * 512:(c + 1) * 512],
                                start=(c == 0), stop=(c == NCH - 1))
                            if c == NCH - 1:
                                tm = b.do("tensor", fn)
                            else:
                                b.do_("tensor", fn)
                        t_wog_used[gi] = tm
                        if oc == NCH - 1:
                            t_hin_last[st] = tm
                            if st < 3:
                                b.wait("sync", "tensor", t_hin_last[st])
                                lh_pend = b.dma("sync", tok_hin, lambda e, st=st: e.dma_start(
                                    out=hintile[:].rearrange("p (c s) -> p c s", s=512),
                                    in_=scr_hin[:, (st + 1) * 512:(st + 2) * 512].rearrange(
                                        "(c p) s -> p c s", p=128)))
                        b.wait("scalar", "tensor", tm)
                        s_sig[gi] = b.do("scalar", lambda e, oc=oc, gi=gi: e.activation(
                            out=sig_blk[:, oc * 512:(oc + 1) * 512], in_=ps_o[gi % 4][:],
                            func=AF.Sigmoid))
                    for hh in range(H):
                        t_hn = 0
                        for k in range(2):
                            r = 2 * hh + k
                            if t_hn:
                                b.wait("vector", "tensor", t_hn)
                            vq = b.do("vector", lambda e, r=r, st=st: e.tensor_mul(
                                hsq[:], hT[:, r * S + st * 512: r * S + (st + 1) * 512],
                                hT[:, r * S + st * 512: r * S + (st + 1) * 512]))
                            b.wait("tensor", "vector", vq)
                            if k == 0 and v_hn1:
                                b.wait("tensor", "vector", v_hn1)
                            fn = lambda e, k=k: e.matmul(
                                ps_hn[:], ones_colb[:], hsq[:],
                                start=(k == 0), stop=(k == 1), skip_group_check=True)
                            t_hn = b.do("tensor", fn)
                        b.wait("vector", "tensor", t_hn)
                        v_hn1 = b.do("vector", lambda e: e.tensor_copy(hn1[:], ps_hn[:]))
                        b.wait("vector", "vector", v_hn1)
                        v_hn2 = b.do("vector", lambda e: e.tensor_scalar(
                            out=hn1[:], in0=hn1[:], scalar1=float(1.0 / DV), scalar2=EPS2,
                            op0=ALU.mult, op1=ALU.add))
                        b.wait("scalar", "vector", v_hn2)
                        sA = b.do("scalar", lambda e: e.activation(out=hn1[:], in_=hn1[:], func=AF.Ln))
                        b.wait("scalar", "scalar", sA)
                        b.do_("scalar", lambda e: e.activation(
                            out=hn1[:], in_=hn1[:], func=AF.Exp, scale=-0.5))
                        sB = b.s_settle()
                        b.wait("tensor", "scalar", sB)
                        if v_rrevac:
                            b.wait("tensor", "vector", v_rrevac)
                        t_rr = b.do("tensor", lambda e: e.matmul(
                            ps_rr[:], ones_row[:], hn1[:], start=True, stop=True))
                        b.wait("vector", "tensor", t_rr)
                        if hh >= 2 and v_hout_last.get((st, hh - 2)):
                            b.wait("vector", "vector", v_hout_last[(st, hh - 2)])
                        elif st > 0 and hh < 2:
                            b.wait("vector", "vector", v_hout_last[(st - 1, H - 2 + hh)])
                        v_rrevac = b.do("vector", lambda e, hh=hh: e.tensor_copy(
                            rstd_rep[hh % 2][:], ps_rr[:]))
                        b.wait("vector", "vector", v_rrevac)
                        if st == 0 and hh == 0 and t_x1_last:
                            b.wait("vector", "tensor", t_x1_last)
                        for k in range(2):
                            r = 2 * hh + k
                            if st > 0 or hh > 0 or k > 0:
                                pass
                            vv1 = b.do("vector", lambda e, r=r, st=st, k=k, hh=hh: e.scalar_tensor_tensor(
                                out=tmp1[k][:],
                                in0=hT[:, r * S + st * 512: r * S + (st + 1) * 512],
                                scalar=mhw_pp[:, r:r + 1],
                                in1=rstd_rep[hh % 2][:],
                                op0=ALU.mult, op1=ALU.mult))
                            b.wait("vector", "vector", vv1)
                            b.wait("vector", "scalar", s_sig[st * NCH + r])
                            v_hout[(st, r)] = b.do("vector", lambda e, r=r, k=k: e.tensor_mul(
                                houtT[:, r * 512:(r + 1) * 512], tmp1[k][:],
                                sig_blk[:, r * 512:(r + 1) * 512]))
                        v_hout_last[(st, hh)] = v_hout[(st, 2 * hh + 1)]
                    if st > 0:
                        b.wait("gpsimd", "vector", v_xa_last[st - 1])
                    lxc = {}
                    for mi in range(4):
                        lxc[mi] = b.dma("gpsimd", tok_xc, lambda e, st=st, mi=mi: e.dma_start(
                            out=xc_all[:, mi * D:(mi + 1) * D],
                            in_=I["x"][(st * 4 + mi) * 128:(st * 4 + mi + 1) * 128, :]))
                    if st > 0:
                        b.wait_tok("vector", tok_x1w, x1w[st - 1])
                    for nt in range(4):
                        gnt = st * 4 + nt
                        wsl = gnt % 2
                        if gnt >= 2:
                            b.wait("gpsimd", "tensor", t_x1w_use[gnt - 2])
                        lwo = b.dma("gpsimd", tok_wot[wsl], lambda e, nt=nt, wsl=wsl: e.dma_start(
                            out=wout_nt[wsl][:].rearrange("p (c o) -> p c o", o=512),
                            in_=I["wout"][:, nt * 512:(nt + 1) * 512].rearrange(
                                "(c p) o -> p c o", p=128)))
                        b.wait_tok("tensor", tok_wot[wsl], lwo)
                        for mi in range(4):
                            idx = st * 16 + nt * 4 + mi
                            sx = idx % 2
                            if idx >= 2:
                                b.wait("tensor", "vector", v_x1evac[idx - 2])
                            tm = 0
                            for c in range(NCH):
                                b.wait("tensor", "vector", v_hout[(st, c)])
                                fn = lambda e, c=c, mi=mi, sx=sx, wsl=wsl: e.matmul(
                                    ps_x1[sx][:],
                                    houtT[:, c * 512 + mi * 128: c * 512 + mi * 128 + 128],
                                    wout_nt[wsl][:, c * 512:(c + 1) * 512],
                                    start=(c == 0), stop=(c == NCH - 1))
                                if c == NCH - 1:
                                    tm = b.do("tensor", fn)
                                else:
                                    b.do_("tensor", fn)
                            t_x1_last = tm
                            t_x1w_use[gnt] = tm
                            b.wait("vector", "tensor", tm)
                            b.wait_tok("vector", tok_xc)
                            rb = st * 4 + mi
                            vv1 = b.do("vector", lambda e, mi=mi, nt=nt, rb=rb, sx=sx: e.tensor_scalar(
                                out=xa_t[sx][:],
                                in0=xc_all[:, mi * D + nt * 512: mi * D + (nt + 1) * 512],
                                scalar1=st_m[:, rb:rb + 1], scalar2=st_rstd[:, rb:rb + 1],
                                op0=ALU.subtract, op1=ALU.mult))
                            b.wait("vector", "vector", vv1)
                            v_x1evac[idx] = b.do("vector", lambda e, mi=mi, nt=nt, sx=sx: e.tensor_add(
                                x1st[:, mi * D + nt * 512: mi * D + (nt + 1) * 512],
                                xa_t[sx][:], ps_x1[sx][:]))
                    v_xa_last[st] = v_x1evac[st * 16 + 15]
                    b.wait("sync", "vector", v_xa_last[st])
                    for mi in range(4):
                        x1w[st] = b.dma("sync", tok_x1w, lambda e, st=st, mi=mi: e.dma_start(
                            out=scr_x1[(st * 4 + mi) * 128:(st * 4 + mi + 1) * 128, :],
                            in_=x1st[:, mi * D:(mi + 1) * D]))
            b.barrier()
        scope_ac.close()

        with contextlib.ExitStack() as pq:
            sup_sb = sb("sup_sb", [128, 64 * 32], F32, pq)
            sdn_sb = sb("sdn_sb", [128, 16 * 128], F32, pq)
            qtile = [sb("qt0", [128, 4096], I32, pq), sb("qt1", [128, 4096], I32, pq)]
            dq_t = [sb("dqt0", [128, 4096], BF16, pq), sb("dqt1", [128, 4096], BF16, pq)]
            stageU = sb("stageU", [128, NCH * 512], BF16, pq)
            stageD = sb("stageD", [128, 32 * 512], BF16, pq)
            tok_q = [b.dsem(es), b.dsem(es)]
            tok_sc = b.dsem(es)
            tok_stg = b.dsem(es)
            b.dma("sync", tok_sc, lambda e: e.dma_start(
                out=sup_sb[:].rearrange("p (fb c) -> p fb c", c=32),
                in_=I["s_up"].rearrange("(fb p) c -> p fb c", p=128)))
            b.dma("sync", tok_sc, lambda e: e.dma_start(
                out=sdn_sb[:].rearrange("p (db c) -> p db c", c=128),
                in_=I["s_dn"].rearrange("(db p) c -> p db c", p=128)))
            b.wait_tok("vector", tok_sc)
            with contextlib.ExitStack() as pqp:
                ps_t = [ps("ps_tq0", [128, 512], BF16, pqp), ps("ps_tq1", [128, 512], BF16, pqp)]
                state = {"seq": 0, "pkey": {0: None, 1: None}, "tpk": {0: None, 1: None},
                         "v_dq": {}, "t_tp": {}, "s_ev": {}}

                def dq_tile(qsrc, rows, cols, srep, stage, woff, nch_t):
                    st8 = state
                    slot = st8["seq"] % 2
                    width = cols[1] - cols[0]
                    if st8["pkey"][slot] is not None:
                        b.wait("sync", "vector", st8["v_dq"][st8["pkey"][slot]])
                    lq = b.dma("sync", tok_q[slot], lambda e, qsrc=qsrc, rows=rows, cols=cols, slot=slot, width=width: e.dma_start(
                        out=qtile[slot][:, 0:width],
                        in_=qsrc[rows[0]:rows[1], cols[0]:cols[1]]))
                    b.wait_tok("vector", tok_q[slot], lq)
                    if st8["pkey"][slot] is not None:
                        b.wait("vector", "tensor", st8["t_tp"][st8["pkey"][slot]])
                    key = st8["seq"]
                    st8["v_dq"][key] = b.do("vector", lambda e, slot=slot, srep=srep, width=width: e.scalar_tensor_tensor(
                        out=dq_t[slot][:, 0:width].rearrange("p (c r) -> p c r", r=64),
                        in0=qtile[slot][:, 0:width].rearrange("p (c r) -> p c r", r=64),
                        scalar=7.5, in1=srep,
                        op0=ALU.subtract, op1=ALU.mult))
                    st8["pkey"][slot] = key
                    b.wait("tensor", "vector", st8["v_dq"][key])
                    for g4 in range(nch_t // 4):
                        pbk = g4 % 2
                        if st8["tpk"][pbk] is not None:
                            b.wait("tensor", "scalar", st8["s_ev"][st8["tpk"][pbk]])
                        tm = 0
                        for jj in range(4):
                            c = g4 * 4 + jj
                            fn = lambda e, slot=slot, c=c, pbk=pbk, jj=jj: e.transpose(
                                ps_t[pbk][:, jj * 128:(jj + 1) * 128],
                                dq_t[slot][:, c * 128:(c + 1) * 128], identb[:])
                            if jj == 3:
                                tm = b.do("tensor", fn)
                            else:
                                b.do_("tensor", fn)
                        st8["t_tp"][key] = tm
                        b.wait("scalar", "tensor", tm)
                        sev = b.do("scalar", lambda e, g4=g4, woff=woff, stage=stage, pbk=pbk: e.activation(
                            out=stage[:].rearrange("p (c f) -> p c f", f=512)[
                                :, g4 * 4:(g4 + 1) * 4, woff:woff + 128],
                            in_=ps_t[pbk][:].rearrange("p (c f) -> p c f", f=128),
                            func=AF.Identity))
                        st8["s_ev"][(key, g4)] = sev
                        st8["tpk"][pbk] = (key, g4)
                    st8["seq"] += 1

                for fbg in range(16):
                    for fbi in range(4):
                        fb = fbg * 4 + fbi
                        srep = sup_sb[:, fb * 32:(fb + 1) * 32].unsqueeze(2).broadcast_to([128, 32, 64])
                        dq_tile(I["q_up"], (fb * 128, (fb + 1) * 128), (0, 2048), srep,
                                stageU, fbi * 128, 16)
                    b.wait("sync", "scalar", b.cnt["scalar"])
                    b.dma("sync", tok_stg, lambda e, fbg=fbg: e.dma_start(
                        out=scr_wup[fbg].rearrange("c p f -> p c f"),
                        in_=stageU[:].rearrange("p (c f) -> p c f", f=512)))
                    b.wait_tok("scalar", tok_stg)
                for dbg in range(4):
                    for fh in range(2):
                        for dbi in range(4):
                            db = dbg * 4 + dbi
                            srep = sdn_sb[:, db * 128 + fh * 64: db * 128 + fh * 64 + 64
                                          ].unsqueeze(2).broadcast_to([128, 64, 64])
                            dq_tile(I["q_dn"], (db * 128, (db + 1) * 128),
                                    (fh * 4096, (fh + 1) * 4096), srep,
                                    stageD, dbi * 128, 32)
                        b.wait("sync", "scalar", b.cnt["scalar"])
                        b.dma("sync", tok_stg, lambda e, dbg=dbg, fh=fh: e.dma_start(
                            out=scr_wdn[dbg, fh * 32:(fh + 1) * 32].rearrange("j p d -> p j d"),
                            in_=stageD[:].rearrange("p (j d) -> p j d", d=512)))
                        b.wait_tok("scalar", tok_stg)
            b.barrier()

        with contextlib.ExitStack() as pd:
            uT = sb("uT", [128, 64 * 1024], BF16, pd)
            st3 = sb("st3", [128, NRB], F32, pd)
            tok_x1r = [b.dsem(es), b.dsem(es)]
            tok_wg = [b.dsem(es), b.dsem(es)]
            tok_wu = [b.dsem(es), b.dsem(es)]
            tok_wd = [b.dsem(es), b.dsem(es)]
            tok_xg = [b.dsem(es), b.dsem(es)]
            tok_x1c = [b.dsem(es), b.dsem(es)]
            tok_yw = b.dsem(es)
            tok_n2 = b.dsem(es)
            def d_group(g2):
                if g2 > 0:
                    for tk in (tok_yw, tok_xg[0], tok_xg[1], tok_x1c[0], tok_x1c[1]):
                        b.wait_tok("sync", tk)
                        b.wait_tok("gpsimd", tk)
                        b.wait_tok("vector", tk)
                        b.wait_tok("scalar", tk)
                with contextlib.ExitStack() as pdu:
                    h2T = sb(f"h2T_{g2}", [128, NCH * 1024], BF16, pdu)
                    x1b = [sb(f"x1b0_{g2}", [128, D], BF16, pdu), sb(f"x1b1_{g2}", [128, D], BF16, pdu)]
                    h2tok = [sb(f"h2t0_{g2}", [128, D], BF16, pdu), sb(f"h2t1_{g2}", [128, D], BF16, pdu)]
                    pd_b = sb(f"pd_b_{g2}", [128, 512], F32, pdu)
                    pd_c = sb(f"pd_c_{g2}", [128, 512], F32, pdu)
                    wg_cb = [sb(f"wg_cba_{g2}", [128, NCH * 128], BF16, pdu),
                             sb(f"wg_cbb_{g2}", [128, NCH * 128], BF16, pdu)]
                    wu_cb = [sb(f"wu_cba_{g2}", [128, NCH * 128], BF16, pdu),
                             sb(f"wu_cbb_{g2}", [128, NCH * 128], BF16, pdu)]
                    silu_sb = [sb(f"sil0_{g2}", [128, 512], BF16, pdu), sb(f"sil1_{g2}", [128, 512], BF16, pdu)]
                    with contextlib.ExitStack() as pdp:
                        ps_tp2 = [ps(f"ps_tpa_{g2}", [128, 512], BF16, pdp), ps(f"ps_tpb_{g2}", [128, 512], BF16, pdp)]
                        ps_g = [ps(f"ps_ga_{g2}", [128, 512], F32, pdp), ps(f"ps_gb_{g2}", [128, 512], F32, pdp)]
                        ps_u = [ps(f"ps_ua_{g2}", [128, 512], F32, pdp), ps(f"ps_ub_{g2}", [128, 512], F32, pdp)]
                        vpd = b.do("vector", lambda e: e.memset(pd_b[:], 1.0))
                        b.wait("scalar", "vector", vpd)
                        s_sq = {}
                        l_x1 = {}
                        for mi in range(8):
                            rbg = g2 * 8 + mi
                            slot = mi % 2
                            if mi >= 2:
                                b.wait("sync", "scalar", s_sq[mi - 2])
                            l_x1[mi] = b.dma("sync", tok_x1r[slot], lambda e, rbg=rbg, slot=slot: e.dma_start(
                                out=x1b[slot][:], in_=scr_x1[rbg * 128:(rbg + 1) * 128, :]))
                            b.wait_tok("scalar", tok_x1r[slot], l_x1[mi])
                            s_sq[mi] = b.do("scalar", lambda e, slot=slot, mi=mi: e.activation(
                                out=h2tok[slot][:], in_=x1b[slot][:], func=AF.Square,
                                accum_out=pd_b[:, mi:mi + 1]))
                        smq = b.s_settle()
                        b.wait("vector", "scalar", smq)
                        v1 = b.do("vector", lambda e: e.tensor_scalar(
                            out=pd_c[:], in0=pd_b[:], scalar1=float(1.0 / D), scalar2=EPS2,
                            op0=ALU.mult, op1=ALU.add))
                        b.wait("scalar", "vector", v1)
                        sA = b.do("scalar", lambda e: e.activation(out=pd_c[:], in_=pd_c[:], func=AF.Ln))
                        b.wait("scalar", "scalar", sA)
                        b.do_("scalar", lambda e: e.activation(
                            out=pd_c[:], in_=pd_c[:], func=AF.Exp, scale=-0.5))
                        smq = b.s_settle()
                        b.wait("vector", "scalar", smq)
                        vst3 = b.do("vector", lambda e, g2=g2: e.tensor_copy(
                            st3[:, g2 * 8:(g2 + 1) * 8], pd_c[:, 0:8]))
                        v_h2evac = {}
                        t_tp3 = {}
                        v_h2 = {}
                        for mi in range(8):
                            rbg = g2 * 8 + mi
                            slot = mi % 2
                            if mi >= 2:
                                b.wait("sync", "vector", v_h2[mi - 2])
                            else:
                                b.wait("sync", "scalar", smq)
                            lx = b.dma("sync", tok_x1r[slot], lambda e, rbg=rbg, slot=slot: e.dma_start(
                                out=x1b[slot][:], in_=scr_x1[rbg * 128:(rbg + 1) * 128, :]))
                            b.wait_tok("vector", tok_x1r[slot], lx)
                            b.wait("vector", "vector", vst3)
                            if mi >= 2:
                                b.wait("vector", "tensor", t_tp3[(mi - 2, 3)])
                            v_h2[mi] = b.do("vector", lambda e, slot=slot, rbg=rbg: e.scalar_tensor_tensor(
                                out=h2tok[slot][:], in0=x1b[slot][:],
                                scalar=st3[:, rbg % NRB: rbg % NRB + 1] if False else st3[:, rbg:rbg + 1],
                                in1=n2w_rep[:], op0=ALU.mult, op1=ALU.mult))
                            b.wait("tensor", "vector", v_h2[mi])
                            for g4 in range(4):
                                pbk = g4 % 2
                                bidx = mi * 4 + g4
                                if bidx >= 2:
                                    b.wait("tensor", "vector", v_h2evac[bidx - 2])
                                tm = 0
                                for jj in range(4):
                                    c = g4 * 4 + jj
                                    fn = lambda e, slot=slot, c=c, pbk=pbk, jj=jj: e.transpose(
                                        ps_tp2[pbk][:, jj * 128:(jj + 1) * 128],
                                        h2tok[slot][:, c * 128:(c + 1) * 128], identb[:])
                                    if jj == 3:
                                        tm = b.do("tensor", fn)
                                    else:
                                        b.do_("tensor", fn)
                                t_tp3[(mi, g4)] = tm
                                b.wait("vector", "tensor", tm)
                                vv = b.do("vector", lambda e, g4=g4, mi=mi, pbk=pbk: e.tensor_copy(
                                    h2T[:].rearrange("p (c f) -> p c f", f=1024)[
                                        :, g4 * 4:(g4 + 1) * 4, mi * 128:(mi + 1) * 128],
                                    ps_tp2[pbk][:].rearrange("p (c f) -> p c f", f=128)))
                                v_h2evac[bidx] = vv
                        h2done = b.cnt["vector"]
                        t_g = {}
                        s_sil = {}
                        v_u = {}
                        for oc in range(64):
                            ws = oc % 2
                            if oc > 1:
                                b.wait("gpsimd", "tensor", t_g[oc - 2])
                            lg = b.dma("gpsimd", tok_wg[ws], lambda e, oc=oc, ws=ws: e.dma_start(
                                out=wg_cb[ws][:].rearrange("p (c o) -> p c o", o=128),
                                in_=I["wgate"][:, oc * 128:(oc + 1) * 128].rearrange(
                                    "(c p) o -> p c o", p=128)))
                            lu = b.dma("gpsimd", tok_wu[ws], lambda e, oc=oc, ws=ws: e.dma_start(
                                out=wu_cb[ws][:].rearrange("p (c o) -> p c o", o=128),
                                in_=I["wup"][:, oc * 128:(oc + 1) * 128].rearrange(
                                    "(c p) o -> p c o", p=128)))
                            b.wait_tok("tensor", tok_wg[ws], lg)
                            b.wait_tok("tensor", tok_wu[ws], lu)
                            b.wait("tensor", "vector", h2done)
                            for s2 in range(2):
                                gx = s2
                                gi = oc * 2 + s2
                                if gi >= 2:
                                    b.wait("tensor", "scalar", s_sil[gi - 2])
                                    b.wait("tensor", "vector", v_u[gi - 2])
                                for c in range(NCH):
                                    b.do_("tensor", lambda e, c=c, s2=s2, gx=gx, ws=ws: e.matmul(
                                        ps_g[gx][:], wg_cb[ws][:, c * 128:(c + 1) * 128],
                                        h2T[:, c * 1024 + s2 * 512: c * 1024 + (s2 + 1) * 512],
                                        start=(c == 0), stop=(c == NCH - 1)))
                                tm = 0
                                for c in range(NCH):
                                    fn = lambda e, c=c, s2=s2, gx=gx, ws=ws: e.matmul(
                                        ps_u[gx][:], wu_cb[ws][:, c * 128:(c + 1) * 128],
                                        h2T[:, c * 1024 + s2 * 512: c * 1024 + (s2 + 1) * 512],
                                        start=(c == 0), stop=(c == NCH - 1))
                                    if c == NCH - 1:
                                        tm = b.do("tensor", fn)
                                    else:
                                        b.do_("tensor", fn)
                                t_g[oc] = tm
                                b.wait("scalar", "tensor", tm)
                                s_sil[gi] = b.do("scalar", lambda e, gx=gx, gi=gi: e.activation(
                                    out=silu_sb[gi % 2][:], in_=ps_g[gx][:], func=AF.Silu))
                                b.wait("vector", "scalar", s_sil[gi])
                                v_u[gi] = b.do("vector", lambda e, oc=oc, s2=s2, gx=gx, gi=gi: e.tensor_mul(
                                    uT[:, oc * 1024 + s2 * 512: oc * 1024 + (s2 + 1) * 512],
                                    silu_sb[gi % 2][:], ps_u[gx][:]))
                        v_uT_last = v_u[127]
                with contextlib.ExitStack() as pdd:
                    wd_t = [sb(f"wd0_{g2}", [128, 4 * 512], BF16, pdd), sb(f"wd1_{g2}", [128, 4 * 512], BF16, pdd)]
                    xcol = [sb(f"xcol0_{g2}", [128, 8 * 512], F32, pdd)]
                    x1col = [sb(f"x1col0_{g2}", [128, 8 * 512], BF16, pdd)]
                    ystage = sb(f"ystage_{g2}", [128, 8 * 512], F32, pdd)
                    ytmp = [sb(f"ytmp0_{g2}", [128, 512], F32, pdd), sb(f"ytmp1_{g2}", [128, 512], F32, pdd)]
                    with contextlib.ExitStack() as pdq:
                        ps_d = [ps(f"ps_d{i}_{g2}", [128, 512], F32, pdq) for i in range(8)]
                        v_yevac = {}
                        t_wd_use = {}
                        yw = 0
                        for nt in range(4):
                            slot = 0
                            if nt >= 1:
                                b.wait("sync", "vector", v_yevac[(nt - 1, 7)])
                            else:
                                b.wait("sync", "vector", v_uT_last)
                            lxc = b.dma("sync", tok_xg[slot], lambda e, g2=g2, nt=nt, slot=slot: e.dma_start(
                                out=xcol[slot][:].rearrange("p (mi d) -> p mi d", d=512),
                                in_=I["x"][g2 * 1024:(g2 + 1) * 1024, nt * 512:(nt + 1) * 512
                                           ].rearrange("(mi p) d -> p mi d", p=128)))
                            l1c = b.dma("sync", tok_x1c[slot], lambda e, g2=g2, nt=nt, slot=slot: e.dma_start(
                                out=x1col[slot][:].rearrange("p (mi d) -> p mi d", d=512),
                                in_=scr_x1[g2 * 1024:(g2 + 1) * 1024, nt * 512:(nt + 1) * 512
                                           ].rearrange("(mi p) d -> p mi d", p=128)))
                            for jb in range(16):
                                js = jb % 2
                                prevkey = (nt, jb - 2) if jb >= 2 else (nt - 1, 14 + jb)
                                if prevkey in t_wd_use:
                                    b.wait("gpsimd", "tensor", t_wd_use[prevkey])
                                else:
                                    b.wait("gpsimd", "tensor", t_g[63])
                                lwd = b.dma("gpsimd", tok_wd[js], lambda e, jb=jb, nt=nt, js=js: e.dma_start(
                                    out=wd_t[js][:].rearrange("p (j d) -> p j d", d=512),
                                    in_=I["wdown"][jb * 512:(jb + 1) * 512, nt * 512:(nt + 1) * 512
                                                   ].rearrange("(j p) d -> p j d", p=128)))
                                b.wait_tok("tensor", tok_wd[js], lwd)
                                if jb == 0:
                                    b.wait("tensor", "vector", v_uT_last)
                                    if nt >= 1:
                                        b.wait("tensor", "vector", v_yevac[(nt - 1, 7)])
                                tm = 0
                                for jj in range(4):
                                    j = jb * 4 + jj
                                    for mi in range(8):
                                        fn = lambda e, j=j, jj=jj, mi=mi, js=js: e.matmul(
                                            ps_d[mi][:],
                                            uT[:, j * 1024 + mi * 128: j * 1024 + (mi + 1) * 128],
                                            wd_t[js][:, jj * 512:(jj + 1) * 512],
                                            start=(j == 0), stop=(j == 63), skip_group_check=True)
                                        if mi == 7 and jj == 3:
                                            tm = b.do("tensor", fn)
                                        else:
                                            b.do_("tensor", fn)
                                t_wd_use[(nt, jb)] = tm
                            b.wait("vector", "tensor", t_wd_use[(nt, 15)])
                            b.wait_tok("vector", tok_xg[slot], lxc)
                            b.wait_tok("vector", tok_x1c[slot], l1c)
                            if nt > 0:
                                b.wait_tok("vector", tok_yw, yw)
                            for mi in range(8):
                                vv1 = b.do("vector", lambda e, mi=mi, slot=slot: e.tensor_add(
                                    ytmp[mi % 2][:], ps_d[mi][:], xcol[slot][:, mi * 512:(mi + 1) * 512]))
                                b.wait("vector", "vector", vv1)
                                v_yevac[(nt, mi)] = b.do("vector", lambda e, mi=mi, slot=slot: e.tensor_add(
                                    ystage[:, mi * 512:(mi + 1) * 512], ytmp[mi % 2][:],
                                    x1col[slot][:, mi * 512:(mi + 1) * 512]))
                            b.wait("sync", "vector", v_yevac[(nt, 7)])
                            yw = b.dma("sync", tok_yw, lambda e, g2=g2, nt=nt: e.dma_start(
                                out=scr_y[g2 * 1024:(g2 + 1) * 1024, nt * 512:(nt + 1) * 512
                                          ].rearrange("(mi p) d -> p mi d", p=128),
                                in_=ystage[:].rearrange("p (mi d) -> p mi d", d=512)))
                return v_uT_last

            v_uT_last = 0
            for g2 in range(2):
                v_uT_last = d_group(g2)
            b.barrier()

        with contextlib.ExitStack() as pe:
            u2T = sb("u2T", [128, 64 * 512], BF16, pe)
            lua_sb = sb("lua_sb", [128, NCH * 16], BF16, pe)
            lda_sb = sb("lda_sb", [128, 64 * 16], BF16, pe)
            ldb_sb = sb("ldb_sb", [16, D], BF16, pe)
            bdn_rep = sb("bdn_rep", [128, D], F32, pe)
            pe_a = sb("pe_a", [128, 512], F32, pe)
            pe_b = sb("pe_b", [128, 512], F32, pe)
            pe_m = sb("pe_m", [128, 512], F32, pe)
            pe_c = sb("pe_c", [128, 512], F32, pe)
            t2T = sb("t2T", [16, 512], BF16, pe)
            tok_el = b.dsem(es)
            tok_y2 = [b.dsem(es), b.dsem(es)]
            tok_wupt = [b.dsem(es), b.dsem(es)]
            tok_lub = [b.dsem(es), b.dsem(es)]
            tok_wdt = [b.dsem(es), b.dsem(es), b.dsem(es)]
            tok_yc = [b.dsem(es), b.dsem(es)]
            tok_out = b.dsem(es)
            for c in range(NCH):
                b.dma("gpsimd", tok_el, lambda e, c=c: e.dma_start(
                    out=lua_sb[:, c * 16:(c + 1) * 16], in_=I["lua"][c * 128:(c + 1) * 128, :]))
            for c in range(64):
                b.dma("gpsimd", tok_el, lambda e, c=c: e.dma_start(
                    out=lda_sb[:, c * 16:(c + 1) * 16], in_=I["lda"][c * 128:(c + 1) * 128, :]))
            b.dma("gpsimd", tok_el, lambda e: e.dma_start(out=ldb_sb[:], in_=I["ldb"]))
            with contextlib.ExitStack() as peb:
                bdn_row = sb("bdn_row", [1, D], F32, peb)
                ps_br = ps("ps_br", [128, 512], F32, peb)
                tok_br = b.dsem(es)
                lbr = b.dma("sync", tok_br, lambda e: e.dma_start(
                    out=bdn_row[:], in_=I["b_dn"].unsqueeze(0)))
                b.wait_tok("tensor", tok_br, lbr)
                vbr = 0
                for q4 in range(4):
                    if q4 > 0:
                        b.wait("tensor", "vector", vbr)
                    tbr = b.do("tensor", lambda e, q4=q4: e.matmul(
                        ps_br[:], ones_row[:], bdn_row[0:1, q4 * 512:(q4 + 1) * 512],
                        start=True, stop=True))
                    b.wait("vector", "tensor", tbr)
                    vbr = b.do("vector", lambda e, q4=q4: e.tensor_copy(
                        bdn_rep[:, q4 * 512:(q4 + 1) * 512], ps_br[:]))
                b.wait("tensor", "vector", vbr)

            t_dn_last_box = [0]
            yblk = [sb("ybkg0", [128, D], F32, pe), sb("ybkg1", [128, D], F32, pe)]
            x1ptok = [sb("x1pg0", [128, D], BF16, pe), sb("x1pg1", [128, D], BF16, pe)]
            est_done = {}

            def e_stats(ge):
                b.do_("vector", lambda e: e.memset(pe_a[:], 1.0))
                vpd = b.do("vector", lambda e: e.memset(pe_b[:], 1.0))
                b.wait("scalar", "vector", vpd)
                b.flush("vector")
                s_sq = {}
                for mi in range(4):
                    rge = ge * 4 + mi
                    slot = mi % 2
                    if mi >= 2:
                        b.wait("gpsimd", "scalar", s_sq[mi - 2])
                        b.wait("gpsimd", "vector", b.cnt["vector"])
                    b.wait_tok("gpsimd", tok_y2[slot])
                    ly = b.dma("gpsimd", tok_y2[slot], lambda e, rge=rge, slot=slot: e.dma_start(
                        out=yblk[slot][:], in_=scr_y[rge * 128:(rge + 1) * 128, :]))
                    b.wait_tok("vector", tok_y2[slot], ly)
                    b.do("vector", lambda e, mi=mi, slot=slot: e.reduce_sum(
                        pe_a[:, mi:mi + 1], yblk[slot][:], axis=mybir.AxisListType.X))
                    b.wait_tok("scalar", tok_y2[slot], ly)
                    s_sq[mi] = b.do("scalar", lambda e, mi=mi, slot=slot: e.activation(
                        out=x1ptok[slot][:], in_=yblk[slot][:], func=AF.Square,
                        accum_out=pe_b[:, mi:mi + 1]))
                smq = b.s_settle()
                b.wait("vector", "scalar", smq)
                b.flush("vector")
                v1 = b.do("vector", lambda e: e.tensor_scalar(
                    out=pe_m[:], in0=pe_a[:], scalar1=float(1.0 / D), scalar2=None, op0=ALU.mult))
                b.do_("vector", lambda e: e.tensor_scalar(
                    out=pe_c[:], in0=pe_b[:], scalar1=float(1.0 / D), scalar2=None, op0=ALU.mult))
                b.wait("vector", "vector", v1)
                v2 = b.do("vector", lambda e: e.tensor_mul(pe_a[:], pe_m[:], pe_m[:]))
                b.wait("vector", "vector", v2)
                v3 = b.do("vector", lambda e: e.tensor_sub(pe_c[:], pe_c[:], pe_a[:]))
                b.wait("vector", "vector", v3)
                vm = b.do("vector", lambda e: e.tensor_scalar(
                    out=pe_c[:], in0=pe_c[:], scalar1=EPS1, scalar2=None, op0=ALU.add))
                b.wait("scalar", "vector", vm)
                sA = b.do("scalar", lambda e: e.activation(out=pe_c[:], in_=pe_c[:], func=AF.Ln))
                b.wait("scalar", "scalar", sA)
                b.do_("scalar", lambda e: e.activation(
                    out=pe_c[:], in_=pe_c[:], func=AF.Exp, scale=-0.5))
                smq = b.s_settle()
                b.wait("vector", "scalar", smq)
                vsm = b.do("vector", lambda e, ge=ge: e.tensor_copy(
                    sy_m[:, ge * 4:(ge + 1) * 4], pe_m[:, 0:4]))
                b.wait("vector", "vector", vsm)
                vsr = b.do("vector", lambda e, ge=ge: e.tensor_copy(
                    sy_rstd[:, ge * 4:(ge + 1) * 4], pe_c[:, 0:4]))
                b.wait("vector", "vector", vsr)
                est_done[ge] = (smq, vsr)

            def e_group(ge):
                for tk in (tok_out, tok_yc[0], tok_yc[1], tok_wdt[0], tok_wdt[1]):
                    b.wait_tok("sync", tk)
                    b.wait_tok("gpsimd", tk)
                    b.wait_tok("vector", tk)
                    b.wait_tok("scalar", tk)
                with contextlib.ExitStack() as peu:
                    x1pT = sb(f"x1pT_{ge}", [128, NCH * 512], BF16, peu)
                    wupt = [sb(f"wupta_{ge}", [128, NCH * 512], BF16, peu),
                            sb(f"wuptb_{ge}", [128, NCH * 512], BF16, peu)]
                    lub_oc = [sb(f"lub0_{ge}", [16, 128], BF16, peu), sb(f"lub1_{ge}", [16, 128], BF16, peu)]
                    tupT = sb(f"tupT_{ge}", [16, 512], BF16, peu)
                    with contextlib.ExitStack() as pep:
                        ps_tp4 = [ps(f"ps_tc0_{ge}", [128, 512], BF16, pep),
                                  ps(f"ps_tc1_{ge}", [128, 512], BF16, pep)]
                        ps_l = ps(f"ps_l_{ge}", [16, 512], F32, pep)
                        ps_y1 = [ps(f"ps_y1a_{ge}", [128, 512], F32, pep),
                                 ps(f"ps_y1b_{ge}", [128, 512], F32, pep)]
                        smq, vsr = est_done[ge]
                        v_xp = {}
                        t_tp5 = {}
                        v_xpev = {}
                        b.wait("vector", "scalar", smq)
                        for mi in range(4):
                            rge = ge * 4 + mi
                            slot = mi % 2
                            if mi >= 2:
                                b.wait("gpsimd", "vector", v_xp[mi - 2])
                            else:
                                b.wait("gpsimd", "scalar", smq)
                                b.wait("gpsimd", "vector", vsr)
                                b.wait_tok("gpsimd", tok_y2[slot])
                            ly = b.dma("gpsimd", tok_y2[slot], lambda e, rge=rge, slot=slot: e.dma_start(
                                out=yblk[slot][:], in_=scr_y[rge * 128:(rge + 1) * 128, :]))
                            b.wait_tok("vector", tok_y2[slot], ly)
                            if mi >= 2:
                                b.wait("vector", "tensor", t_tp5[(mi - 2, 3)])
                            v_xp[mi] = b.do("vector", lambda e, slot=slot, rge=rge: e.tensor_scalar(
                                out=x1ptok[slot][:], in0=yblk[slot][:],
                                scalar1=sy_m[:, rge:rge + 1], scalar2=sy_rstd[:, rge:rge + 1],
                                op0=ALU.subtract, op1=ALU.mult))
                            b.wait("tensor", "vector", v_xp[mi])
                            for g4 in range(4):
                                pbk = g4 % 2
                                bidx = mi * 4 + g4
                                if bidx >= 2:
                                    b.wait("tensor", "vector", v_xpev[bidx - 2])
                                tm = 0
                                for jj in range(4):
                                    c = g4 * 4 + jj
                                    fn = lambda e, slot=slot, c=c, pbk=pbk, jj=jj: e.transpose(
                                        ps_tp4[pbk][:, jj * 128:(jj + 1) * 128],
                                        x1ptok[slot][:, c * 128:(c + 1) * 128], identb[:])
                                    if jj == 3:
                                        tm = b.do("tensor", fn)
                                    else:
                                        b.do_("tensor", fn)
                                t_tp5[(mi, g4)] = tm
                                b.wait("vector", "tensor", tm)
                                vv = b.do("vector", lambda e, g4=g4, mi=mi, pbk=pbk: e.tensor_copy(
                                    x1pT[:].rearrange("p (c f) -> p c f", f=512)[
                                        :, g4 * 4:(g4 + 1) * 4, mi * 128:(mi + 1) * 128],
                                    ps_tp4[pbk][:].rearrange("p (c f) -> p c f", f=128)))
                                v_xpev[bidx] = vv
                        xp_done = b.cnt["vector"]
                        b.wait("tensor", "vector", xp_done)
                        b.wait_tok("tensor", tok_el)
                        tm = 0
                        for c in range(NCH):
                            fn = lambda e, c=c: e.matmul(
                                ps_l[:], lua_sb[:, c * 16:(c + 1) * 16],
                                x1pT[:, c * 512:(c + 1) * 512],
                                start=(c == 0), stop=(c == NCH - 1))
                            if c == NCH - 1:
                                tm = b.do("tensor", fn)
                            else:
                                b.do_("tensor", fn)
                        b.wait("vector", "tensor", tm)
                        v_tup = b.do("vector", lambda e: e.tensor_copy(tupT[:], ps_l[:]))
                        t_fb_use = {}
                        t_oc = {}
                        s_relu = {}
                        lwu = 0
                        for oc in range(64):
                            fb = oc // 4
                            wslot = fb % 2
                            if oc % 4 == 0:
                                if fb > 1:
                                    b.wait("sync", "tensor", t_fb_use[fb - 2])
                                lwu = b.dma("sync", tok_wupt[wslot], lambda e, fb=fb, wslot=wslot: e.dma_start(
                                    out=wupt[wslot][:].rearrange("p (c f) -> p c f", f=512),
                                    in_=scr_wup[fb].rearrange("c p f -> p c f")))
                            slot = oc % 2
                            if oc >= 2:
                                b.wait("gpsimd", "tensor", t_oc[oc - 2])

                            llu = b.dma("gpsimd", tok_lub[slot], lambda e, oc=oc, slot=slot: e.dma_start(
                                out=lub_oc[slot][:], in_=I["lub"][:, oc * 128:(oc + 1) * 128]))
                            b.wait_tok("tensor", tok_wupt[wslot], lwu)
                            b.wait_tok("tensor", tok_lub[slot], llu)
                            if oc == 0:
                                b.wait("tensor", "vector", v_tup)
                                if t_dn_last_box[0]:
                                    b.wait("tensor", "tensor", t_dn_last_box[0]) if False else None
                            if oc >= 2:
                                b.wait("tensor", "scalar", s_relu[oc - 2])
                            for c in range(NCH):
                                b.do_("tensor", lambda e, c=c, oc=oc, slot=slot, wslot=wslot: e.matmul(
                                    ps_y1[slot][:],
                                    wupt[wslot][:, c * 512 + (oc % 4) * 128: c * 512 + (oc % 4) * 128 + 128],
                                    x1pT[:, c * 512:(c + 1) * 512],
                                    start=(c == 0), stop=False))
                            tm = b.do("tensor", lambda e, oc=oc, slot=slot: e.matmul(
                                ps_y1[slot][:], lub_oc[slot][:], tupT[:],
                                start=False, stop=True))
                            if oc % 4 == 3:
                                t_fb_use[fb] = tm
                            t_oc[oc] = tm
                            b.wait("scalar", "tensor", tm)
                            s_relu[oc] = b.do("scalar", lambda e, oc=oc, slot=slot: e.activation(
                                out=u2T[:, oc * 512:(oc + 1) * 512], in_=ps_y1[slot][:],
                                func=AF.Relu, bias=bup_pp[:, oc:oc + 1]))
                        sm_relu = b.s_settle()
                        b.wait("tensor", "scalar", sm_relu)
                        b.wait("tensor", "vector", v_tup)
                        tm = 0
                        for j in range(64):
                            fn = lambda e, j=j: e.matmul(
                                ps_l[:], lda_sb[:, j * 16:(j + 1) * 16],
                                u2T[:, j * 512:(j + 1) * 512],
                                start=(j == 0), stop=(j == 63))
                            if j == 63:
                                tm = b.do("tensor", fn)
                            else:
                                b.do_("tensor", fn)
                        b.wait("vector", "tensor", tm)
                        v_t2 = b.do("vector", lambda e: e.tensor_copy(t2T[:], ps_l[:]))
                        b.wait("tensor", "vector", v_t2)
                        if ge < 3:
                            b.wait("gpsimd", "vector", xp_done)
                            e_stats(ge + 1)
                b.wait("sync", "vector", v_t2)
                b.wait("gpsimd", "vector", v_t2)
                with contextlib.ExitStack() as ped:
                    wdn_t = [sb(f"wdt{i}_{ge}", [128, 4 * 512], BF16, ped) for i in range(3)]
                    ycol = [sb(f"ycl0_{ge}", [128, 4 * 512], F32, ped),
                            sb(f"ycl1_{ge}", [128, 4 * 512], F32, ped)]
                    etmp = [sb(f"etp0_{ge}", [128, 512], F32, ped),
                            sb(f"etp1_{ge}", [128, 512], F32, ped)]
                    outst = sb(f"outst_{ge}", [128, 4 * D], F32, ped)
                    with contextlib.ExitStack() as peq:
                        ps_d = [ps(f"ps_e{i}_{ge}", [128, 512], F32, peq) for i in range(4)]
                        v_oev = {}
                        t_wd_use = {}
                        t_dn = 0
                        for nt in range(4):
                            slot = nt % 2
                            if nt >= 2:
                                b.wait("sync", "vector", v_oev[(nt - 2, 3)])
                            lyc = b.dma("sync", tok_yc[slot], lambda e, ge=ge, nt=nt, slot=slot: e.dma_start(
                                out=ycol[slot][:].rearrange("p (mi d) -> p mi d", d=512),
                                in_=scr_y[ge * 512:(ge + 1) * 512, nt * 512:(nt + 1) * 512
                                          ].rearrange("(mi p) d -> p mi d", p=128)))
                            for jb in range(16):
                                js = jb % 3
                                prevkey = (nt, jb - 3) if jb >= 3 else (nt - 1, {0: 15, 1: 13, 2: 14}[jb])
                                if prevkey in t_wd_use:
                                    b.wait("sync", "tensor", t_wd_use[prevkey])
                                lwd = b.dma("sync", tok_wdt[js], lambda e, jb=jb, nt=nt, js=js: e.dma_start(
                                    out=wdn_t[js][:].rearrange("p (j d) -> p j d", d=512),
                                    in_=scr_wdn[nt, jb * 4:(jb + 1) * 4].rearrange("j p d -> p j d")))
                                b.wait_tok("tensor", tok_wdt[js], lwd)
                                if jb == 0 and nt >= 1:
                                    b.wait("tensor", "vector", v_oev[(nt - 1, 3)])
                                tm = 0
                                for jj in range(4):
                                    j = jb * 4 + jj
                                    for mi in range(4):
                                        fn = lambda e, j=j, jj=jj, mi=mi, js=js: e.matmul(
                                            ps_d[mi][:],
                                            u2T[:, j * 512 + mi * 128: j * 512 + (mi + 1) * 128],
                                            wdn_t[js][:, jj * 512:(jj + 1) * 512],
                                            start=(j == 0), stop=False, skip_group_check=True)
                                        if mi == 3 and jj == 3:
                                            tm = b.do("tensor", fn)
                                        else:
                                            b.do_("tensor", fn)
                                t_wd_use[(nt, jb)] = tm
                            tm = 0
                            for mi in range(4):
                                fn = lambda e, mi=mi, nt=nt: e.matmul(
                                    ps_d[mi][:], t2T[:, mi * 128:(mi + 1) * 128],
                                    ldb_sb[:, nt * 512:(nt + 1) * 512],
                                    start=False, stop=True, skip_group_check=True)
                                if mi == 3:
                                    tm = b.do("tensor", fn)
                                else:
                                    b.do_("tensor", fn)
                            t_dn = tm
                            b.wait("vector", "tensor", tm)
                            b.wait_tok("vector", tok_yc[slot], lyc)
                            for mi in range(4):
                                vv1 = b.do("vector", lambda e, mi=mi, nt=nt: e.tensor_add(
                                    etmp[mi % 2][:], ps_d[mi][:],
                                    bdn_rep[:, nt * 512:(nt + 1) * 512]))
                                b.wait("vector", "vector", vv1)
                                v_oev[(nt, mi)] = b.do("vector", lambda e, mi=mi, nt=nt, slot=slot: e.tensor_add(
                                    outst[:, mi * D + nt * 512: mi * D + (nt + 1) * 512],
                                    etmp[mi % 2][:], ycol[slot][:, mi * 512:(mi + 1) * 512]))
                        b.wait("sync", "vector", v_oev[(3, 3)])
                        for mi in range(4):
                            b.dma("sync", tok_out, lambda e, ge=ge, mi=mi: e.dma_start(
                                out=out[(ge * 4 + mi) * 128:(ge * 4 + mi + 1) * 128, :],
                                in_=outst[:, mi * D:(mi + 1) * D]))
                        t_dn_last_box[0] = t_dn

            b.wait("gpsimd", "tensor", b.cnt["tensor"])
            e_stats(0)
            for ge in range(4):
                e_group(ge)
            b.barrier()

        if "kq" in taps:
            t = add_tap("kq", [128, 2 * NCH * S // 2 * 2], BF16) if False else add_tap("kq", [128, NCH * S], BF16)
            b.dma("sync", tok_tap, lambda e, t=t: e.dma_start(out=t[:], in_=big_b[:]))
        if "hT" in taps:
            t = add_tap("hT", [128, NCH * S], BF16)
            b.dma("sync", tok_tap, lambda e, t=t: e.dma_start(out=t[:], in_=big_a[:]))
        if "x1" in taps:
            t = add_tap("x1", [S, D], BF16)
            b.dma("sync", tok_tap, lambda e, t=t: e.dma_start(out=t[:], in_=scr_x1[:]))
        if "scrv" in taps:
            t = add_tap("scrv", [S, D], BF16)
            b.dma("sync", tok_tap, lambda e, t=t: e.dma_start(out=t[:], in_=scr_v[:]))

        if "hinT" in taps:
            t = add_tap("hinT", [128, NCH * S])
            with contextlib.ExitStack() as tp:
                tmp = sb("tap_tmp", [128, 2048], F32, tp)
                for c in range(NCH):
                    vmt = b.do("vector", lambda e, c=c: e.tensor_copy(
                        tmp[:], hinT[:, c * S:(c + 1) * S]))
                    b.wait("sync", "vector", vmt)
                    smt = b.dma("sync", tok_tap, lambda e, c=c, t=t: e.dma_start(
                        out=t[:, c * S:(c + 1) * S], in_=tmp[:]))
                    b.wait_tok("vector", tok_tap, smt)
            b.barrier()
        if "stats" in taps:
            t = add_tap("stats", [128, 3 * NRB])
            b.dma("sync", tok_tap, lambda e, t=t: e.dma_start(out=t[:, 0:16], in_=st_m[:]))
            b.dma("sync", tok_tap, lambda e, t=t: e.dma_start(out=t[:, 16:32], in_=st_rstd[:]))
            b.dma("sync", tok_tap, lambda e, t=t: e.dma_start(out=t[:, 32:48], in_=st_c1[:]))
        if "gcol" in taps:
            t = add_tap("gcol", [128, 128])
            b.dma("sync", tok_tap, lambda e, t=t: e.dma_start(out=t[:], in_=Gcol[:]))

        b.barrier()
        for tok in b.all_toks:
            b.wait_tok("vector", tok)
        b.mark("vector")

        with nc.Block() as block:
            b.emit(block)
    return nc, tap_outs


def make_consts():
    ident = np.eye(128, dtype=np.float32)
    dm = np.zeros((128, 4 * 512), np.float32)
    tl = np.arange(128)[:, None]
    sl = np.arange(512)[None, :]
    for k in range(4):
        dm[:, k * 512:(k + 1) * 512] = (128 * k + tl <= sl).astype(np.float32)
    return {
        "c_ident": ident, "c_dmask": dm,
        "c_ones_row": np.ones((1, 128), np.float32),
        "c_ones_col": np.ones((128, 1), np.float32),
    }


def make_in_maps(inputs):
    x = np.ascontiguousarray(np.asarray(inputs["L0_attn_x"], dtype=np.float32))
    w = {
        "n1w": inputs["L0_attn_norm1_w"], "wq": inputs["L0_attn_wq"],
        "wk": inputs["L0_attn_wk"], "wv": inputs["L0_attn_wv"],
        "wig": inputs["L0_attn_w_ig"], "big": inputs["L0_attn_b_ig"],
        "wfg": inputs["L0_attn_w_fg"], "bfg": inputs["L0_attn_b_fg"],
        "wog": inputs["L0_attn_w_og"], "mhw": inputs["L0_attn_mh_w"],
        "wout": inputs["L0_attn_w_out"], "n2w": inputs["L0_attn_norm2_w"],
        "wgate": inputs["L0_attn_w_gate"], "wup": inputs["L0_attn_w_up"],
        "wdown": inputs["L0_attn_w_down"],
        "q_up": inputs["L0_ffn_w_up_q"], "s_up": inputs["L0_ffn_w_up_scale"],
        "b_up": inputs["L0_ffn_b_up"],
        "lua": inputs["L0_ffn_w_up_lora_a"], "lub": inputs["L0_ffn_w_up_lora_b"],
        "q_dn": inputs["L0_ffn_w_down_q"], "s_dn": inputs["L0_ffn_w_down_scale"],
        "b_dn": inputs["L0_ffn_b_down"],
        "lda": inputs["L0_ffn_w_down_lora_a"], "ldb": inputs["L0_ffn_w_down_lora_b"],
    }
    w = {k: np.ascontiguousarray(np.asarray(v)) for k, v in w.items()}
    w.update(make_consts())
    maps = []
    for core in range(8):
        m = dict(w)
        m["x"] = np.ascontiguousarray(x[core])
        maps.append(m)
    return maps


def kernel(**inputs):
    maps = make_in_maps(inputs)
    nc, _ = build_core()
    res = run_bass_kernel_spmd(nc, maps, core_ids=list(range(8)))
    return np.stack([r["out"] for r in res.results], axis=0)


# version 08
# speedup vs baseline: 1.0795x; 1.0369x over previous
import contextlib
import numpy as np
import ml_dtypes
import concourse.bass as bass
import concourse.mybir as mybir
from concourse.bass_utils import run_bass_kernel_spmd

F32 = mybir.dt.float32
BF16 = mybir.dt.bfloat16
I32 = mybir.dt.int32
AF = mybir.ActivationFunctionType
ALU = mybir.AluOpType
BFNP = ml_dtypes.bfloat16

S = 2048
D = 2048
H = 8
DQK = 128
DV = 256
FF = 8192
NCH = 16
NRB = 16
EPS1 = 1e-5
EPS2 = 1e-6
CAPV = 15.0
LOG_ISQ = float(np.log(1.0 / np.sqrt(float(DQK))))

ENGS = ("sync", "gpsimd", "scalar", "vector", "tensor")


class Bld:
    def __init__(self, nc):
        self.nc = nc
        self.ops = {e: [] for e in ENGS}
        self.cnt = {e: 0 for e in ENGS}
        self.tail_inc = {e: False for e in ENGS}
        self.sems = {}

    def alloc_sems(self, es):
        for e in ENGS:
            self.sems[e] = es.enter_context(self.nc.semaphore("sem_" + e))
        for e in ("sync", "gpsimd"):
            self.sems[e + "_m"] = es.enter_context(self.nc.semaphore("sem_" + e + "_m"))
            self.cnt[e + "_m"] = 0
        self._ns = 0

    def dsem(self, stack):
        self._ns += 1
        tok = [stack.enter_context(self.nc.semaphore(f"ds{self._ns}")), 0]
        if not hasattr(self, "all_toks"):
            self.all_toks = []
        self.all_toks.append(tok)
        return tok

    def dma(self, q, tok, fn):
        tok[1] += 16

        def thunk(eng, fn=fn, sem=tok[0], tgt=tok[1]):
            fn(eng).then_inc(sem, 16)

        self.ops[q].append(thunk)
        self.tail_inc[q] = False
        return tok[1]

    def wait_tok(self, e, tok, val=None):
        if val is None:
            val = tok[1]
        if val <= 0:
            return
        self.ops[e].append(
            lambda eng, sem=tok[0], val=val: eng.wait_ge(sem, val))

    def do(self, e, fn):
        step = 16 if e in ("sync", "gpsimd") else 1
        self.cnt[e] += step

        def thunk(eng, fn=fn, e=e, step=step):
            fn(eng).then_inc(self.sems[e], step)

        self.ops[e].append(thunk)
        self.tail_inc[e] = True
        return self.cnt[e]

    def do_(self, e, fn):
        self.ops[e].append(fn)
        self.tail_inc[e] = False

    def wait(self, e, other, val):
        if val <= 0:
            return
        self.ops[e].append(
            lambda eng, other=other, val=val: eng.wait_ge(self.sems[other], val)
        )
        self.tail_inc[e] = False

    def mark(self, e):
        if e not in ("sync", "gpsimd") and self.tail_inc[e]:
            return self.cnt[e]
        if e in ("sync", "gpsimd"):
            self.cnt[e + "_m"] += 1

            def thunk(eng, e=e):
                eng.nop().then_inc(self.sems[e + "_m"], 1)

            self.ops[e].append(thunk)
            return self.cnt[e + "_m"]
        return self.do(e, lambda eng: eng.nop())

    def flush(self, e):
        self.wait(e, e, self.cnt[e])

    def s_settle(self):
        if getattr(self, "last_settle", 0):
            self.wait("scalar", "scalar", self.last_settle)
        self.last_settle = self.do("scalar", lambda eng: eng.activation(
            out=self.settle_dst[:], in_=self.settle_dst[:], func=AF.Abs))
        return self.last_settle

    def barrier(self):
        self.s_settle()
        for tok in getattr(self, "all_toks", []):
            if tok[1] > 0:
                self.wait_tok("sync", tok)
                self.wait_tok("gpsimd", tok)
        marks = {e: self.mark(e) for e in ENGS}
        for e in ENGS:
            for o in ENGS:
                if o != e:
                    sem = o + "_m" if o in ("sync", "gpsimd") else o
                    self.ops[e].append(
                        lambda eng, sem=sem, val=marks[o]: eng.wait_ge(self.sems[sem], val))

    def emit(self, block):
        def run(eng, e):
            for fn in self.ops[e]:
                fn(eng)

        block.sync(lambda eng: run(eng, "sync"))
        block.gpsimd(lambda eng: run(eng, "gpsimd"))
        block.scalar(lambda eng: run(eng, "scalar"))
        block.vector(lambda eng: run(eng, "vector"))
        block.tensor(lambda eng: run(eng, "tensor"))


def build_core(taps=()):
    nc = bass.Bass("TRN2", target_bir_lowering=False, debug=False)
    taps = set(taps)

    I = {}
    I["x"] = nc.dram_tensor("x", [S, D], F32, kind="ExternalInput").ap()
    for nm, shp in [
        ("n1w", [D]), ("wq", [D, 1024]), ("wk", [D, 1024]), ("wv", [D, D]),
        ("wig", [D, H]), ("big", [H]), ("wfg", [D, H]), ("bfg", [H]),
        ("wog", [D, D]), ("mhw", [D]), ("wout", [D, D]), ("n2w", [D]),
        ("wgate", [D, FF]), ("wup", [D, FF]), ("wdown", [FF, D]),
        ("s_up", [FF, 32]), ("b_up", [FF]),
        ("lua", [D, 16]), ("lub", [16, FF]),
        ("s_dn", [D, 128]), ("b_dn", [D]),
        ("lda", [FF, 16]), ("ldb", [16, D]),
    ]:
        I[nm] = nc.dram_tensor(nm, shp, F32, kind="ExternalInput").ap()
    I["q_up"] = nc.dram_tensor("q_up", [FF, D], I32, kind="ExternalInput").ap()
    I["q_dn"] = nc.dram_tensor("q_dn", [D, FF], I32, kind="ExternalInput").ap()
    I["c_ident"] = nc.dram_tensor("c_ident", [128, 128], F32, kind="ExternalInput").ap()
    I["c_dmask"] = nc.dram_tensor("c_dmask", [128, 4 * 512], F32, kind="ExternalInput").ap()
    I["c_ones_row"] = nc.dram_tensor("c_ones_row", [1, 128], F32, kind="ExternalInput").ap()
    I["c_ones_col"] = nc.dram_tensor("c_ones_col", [128, 1], F32, kind="ExternalInput").ap()

    out = nc.dram_tensor("out", [S, D], F32, kind="ExternalOutput").ap()

    scr_hin = nc.dram_tensor("scr_hin", [D, S], BF16, kind="Internal").ap()
    scr_v = nc.dram_tensor("scr_v", [S, D], BF16, kind="Internal").ap()
    scr_x1 = nc.dram_tensor("scr_x1", [S, D], BF16, kind="Internal").ap()
    scr_y = nc.dram_tensor("scr_y", [S, D], F32, kind="Internal").ap()
    scr_gme = nc.dram_tensor("scr_gme", [16, S], F32, kind="Internal").ap()
    scr_wup = nc.dram_tensor("scr_wup", [16, 16, 128, 512], BF16, kind="Internal").ap()
    scr_wdn = nc.dram_tensor("scr_wdn", [4, 64, 128, 512], BF16, kind="Internal").ap()

    tap_outs = {}

    def add_tap(name, shape, dtype=F32):
        tap_outs[name] = nc.dram_tensor("tap_" + name, shape, dtype, kind="ExternalOutput").ap()
        return tap_outs[name]

    b = Bld(nc)
    es = contextlib.ExitStack()
    with es:
        b.alloc_sems(es)

        def sb(name, shape, dtype, stack=es):
            return stack.enter_context(nc.sbuf_tensor(name, shape, dtype))

        def ps(name, shape, dtype, stack):
            return stack.enter_context(nc.psum_tensor(name, shape, dtype))

        identb = sb("identb", [128, 128], BF16)
        ident32 = sb("ident32", [128, 128], F32)
        ones_row = sb("ones_row", [1, 128], F32)
        settle_dst = sb("settle_dst", [128, 512], F32)
        b.settle_dst = settle_dst
        n2w_rep = sb("n2w_repg", [128, D], BF16)
        ones_colb = sb("ones_colb", [128, 1], BF16)
        n1w_pp = sb("n1w_pp", [128, NCH], F32)
        mhw_pp = sb("mhw_pp", [128, NCH], F32)
        bup_pp = sb("bup_pp", [128, 64], F32)
        big_sb = sb("big_sb", [8, 1], F32)
        bfg_sb = sb("bfg_sb", [8, 1], F32)
        b15i = sb("b15i", [8, 1], F32)
        b15f = sb("b15f", [8, 1], F32)
        one8 = sb("one8", [8, 1], F32)
        st_m = sb("st_m", [128, NRB], F32)
        st_rstd = sb("st_rstd", [128, NRB], F32)
        st_c1 = sb("st_c1", [128, NRB], F32)
        sy_m = sb("sy_m", [128, NRB], F32)
        sy_rstd = sb("sy_rstd", [128, NRB], F32)
        KOFF = 0
        QOFF = NCH * S // 2

        scope_ac = contextlib.ExitStack()
        es.enter_context(scope_ac)
        big_a = sb("big_a", [128, NCH * S], BF16, scope_ac)
        Gcol = sb("Gcol", [128, 128], F32, scope_ac)
        hinT = big_a

        vset = b.do("vector", lambda e: e.memset(settle_dst[:], 1.0))
        b.wait("scalar", "vector", vset)
        b.last_settle = 0
        tok_c = b.dsem(es)
        tok_g = b.dsem(es)
        tok_x = [b.dsem(es) for _ in range(4)]
        tok_scrh = b.dsem(es)
        tok_tap = b.dsem(es)
        with contextlib.ExitStack() as p0:
            wigt = sb("wig_sb", [128, NCH * H], BF16, p0)
            wfgt = sb("wfg_sb", [128, NCH * H], BF16, p0)
            gM = sb("gM", [8, S], F32, p0)
            gEn = sb("gEn", [8, S], F32, p0)
            b.dma("sync", tok_c, lambda e: e.dma_start(out=ident32[:], in_=I["c_ident"]))
            b.dma("gpsimd", tok_g, lambda e: e.dma_start(out=identb[:], in_=I["c_ident"]))
            b.dma("sync", tok_c, lambda e: e.dma_start(out=ones_row[:], in_=I["c_ones_row"]))
            b.dma("gpsimd", tok_g, lambda e: e.dma_start(out=ones_colb[:], in_=I["c_ones_col"]))
            for c in range(NCH):
                b.dma("sync", tok_c, lambda e, c=c: e.dma_start(
                    out=n1w_pp[:, c:c + 1], in_=I["n1w"][c * 128:(c + 1) * 128].unsqueeze(1)))
                b.dma("sync", tok_c, lambda e, c=c: e.dma_start(
                    out=mhw_pp[:, c:c + 1], in_=I["mhw"][c * 128:(c + 1) * 128].unsqueeze(1)))
                b.dma("gpsimd", tok_g, lambda e, c=c: e.dma_start(
                    out=wigt[:, c * H:(c + 1) * H], in_=I["wig"][c * 128:(c + 1) * 128, :]))
                b.dma("gpsimd", tok_g, lambda e, c=c: e.dma_start(
                    out=wfgt[:, c * H:(c + 1) * H], in_=I["wfg"][c * 128:(c + 1) * 128, :]))
            for c in range(64):
                b.dma("sync", tok_c, lambda e, c=c: e.dma_start(
                    out=bup_pp[:, c:c + 1], in_=I["b_up"][c * 128:(c + 1) * 128].unsqueeze(1)))
            b.dma("sync", tok_c, lambda e: e.dma_start(out=big_sb[:], in_=I["big"].unsqueeze(1)))
            b.dma("sync", tok_c, lambda e: e.dma_start(out=bfg_sb[:], in_=I["bfg"].unsqueeze(1)))
            b.wait_tok("vector", tok_c)
            b.wait_tok("vector", tok_g)
            b.do_("vector", lambda e: e.tensor_scalar(
                out=b15i[:], in0=big_sb[:], scalar1=float(1 / CAPV), scalar2=None, op0=ALU.mult))
            b.do_("vector", lambda e: e.tensor_scalar(
                out=b15f[:], in0=bfg_sb[:], scalar1=float(1 / CAPV), scalar2=None, op0=ALU.mult))
            b.do("vector", lambda e: e.memset(one8[:], 1.0))
            with contextlib.ExitStack() as p0r:
                n2w_row = sb("n2w_rowg", [1, D], F32, p0r)
                ps_p0 = ps("ps_p0", [128, 512], F32, p0r)
                tok_n2g = b.dsem(es)
                ln2 = b.dma("sync", tok_n2g, lambda e: e.dma_start(
                    out=n2w_row[:], in_=I["n2w"].unsqueeze(0)))
                b.wait_tok("tensor", tok_n2g, ln2)
                b.wait_tok("tensor", tok_c)
                b.wait_tok("tensor", tok_g)
                vrepm = 0
                for q4 in range(4):
                    if q4 > 0:
                        b.wait("tensor", "vector", vrepm)
                    trm = b.do("tensor", lambda e, q4=q4: e.matmul(
                        ps_p0[:], ones_row[:], n2w_row[0:1, q4 * 512:(q4 + 1) * 512],
                        start=True, stop=True))
                    b.wait("vector", "tensor", trm)
                    vrepm = b.do("vector", lambda e, q4=q4: e.tensor_copy(
                        n2w_rep[:, q4 * 512:(q4 + 1) * 512], ps_p0[:]))
            for e in ENGS:
                b.wait_tok(e, tok_c)
                b.wait_tok(e, tok_g)
            b.barrier()

            with contextlib.ExitStack() as pa:
                xbuf = [sb(f"xbuf{i}", [128, D], F32, pa) for i in range(4)]
                xsq = [sb("xsq0", [128, D], F32, pa), sb("xsq1", [128, D], F32, pa)]
                hb = [sb("hb0", [128, D], BF16, pa), sb("hb1", [128, D], BF16, pa)]
                pa_a = sb("pa_a", [128, 512], F32, pa)
                pa_b = sb("pa_b", [128, 512], F32, pa)
                pa_m = sb("pa_m", [128, 512], F32, pa)
                pa_t = sb("pa_t", [128, 512], F32, pa)
                pa_t2 = sb("pa_t2", [128, 512], F32, pa)
                pa_c = sb("pa_c", [128, 512], F32, pa)
                b.do_("vector", lambda e: e.memset(pa_a[:], 1.0))
                vmem = b.do("vector", lambda e: e.memset(pa_b[:], 1.0))
                b.wait("scalar", "vector", vmem)
                b.flush("vector")
                vdone = {}
                sdone = {}
                lxs = {}
                for rb in range(NRB):
                    bi = rb % 2
                    if rb >= 2:
                        b.wait("sync", "vector", vdone[rb - 2])
                        b.wait("sync", "scalar", sdone[rb - 2])
                    lxs[rb] = b.dma("sync", tok_x[bi], lambda e, rb=rb, bi=bi: e.dma_start(
                        out=xbuf[bi][:], in_=I["x"][rb * 128:(rb + 1) * 128, :]))
                    b.wait_tok("vector", tok_x[bi], lxs[rb])
                    vdone[rb] = b.do("vector", lambda e, rb=rb, bi=bi: e.reduce_sum(
                        pa_a[:, rb:rb + 1], xbuf[bi][:], axis=mybir.AxisListType.X))
                    b.wait_tok("scalar", tok_x[bi], lxs[rb])
                    sdone[rb] = b.do("scalar", lambda e, rb=rb, bi=bi: e.activation(
                        out=xsq[bi][:], in_=xbuf[bi][:], func=AF.Square,
                        accum_out=pa_b[:, rb:rb + 1]))
                sm = b.s_settle()
                b.wait("vector", "scalar", sm)
                b.flush("vector")
                v1 = b.do("vector", lambda e: e.tensor_scalar(
                    out=pa_m[:], in0=pa_a[:], scalar1=float(1.0 / D), scalar2=None, op0=ALU.mult))
                b.do_("vector", lambda e: e.tensor_scalar(
                    out=pa_t[:], in0=pa_b[:], scalar1=float(1.0 / D), scalar2=None, op0=ALU.mult))
                b.wait("vector", "vector", v1)
                v2 = b.do("vector", lambda e: e.tensor_mul(pa_t2[:], pa_m[:], pa_m[:]))
                b.wait("vector", "vector", v2)
                v3 = b.do("vector", lambda e: e.tensor_sub(pa_t[:], pa_t[:], pa_t2[:]))
                b.wait("vector", "vector", v3)
                vm = b.do("vector", lambda e: e.tensor_scalar(
                    out=pa_t2[:], in0=pa_t[:], scalar1=EPS1, scalar2=None, op0=ALU.add))
                b.wait("scalar", "vector", vm)
                s1 = b.do("scalar", lambda e: e.activation(out=pa_b[:], in_=pa_t2[:], func=AF.Ln))
                b.wait("scalar", "scalar", s1)
                s2 = b.do("scalar", lambda e: e.activation(
                    out=pa_c[:], in_=pa_b[:], func=AF.Exp, scale=-1.0))
                b.wait("scalar", "scalar", s2)
                b.do_("scalar", lambda e: e.activation(
                    out=pa_b[:], in_=pa_b[:], func=AF.Exp, scale=-0.5))
                sm = b.s_settle()
                b.wait("vector", "scalar", sm)
                v4 = b.do("vector", lambda e: e.tensor_mul(pa_a[:], pa_t[:], pa_c[:]))
                b.wait("vector", "vector", v4)
                vm = b.do("vector", lambda e: e.tensor_scalar(
                    out=pa_a[:], in0=pa_a[:], scalar1=EPS2, scalar2=None, op0=ALU.add))
                b.wait("scalar", "vector", vm)
                s4 = b.do("scalar", lambda e: e.activation(out=pa_c[:], in_=pa_a[:], func=AF.Ln))
                b.wait("scalar", "scalar", s4)
                b.do_("scalar", lambda e: e.activation(
                    out=pa_t2[:], in_=pa_c[:], func=AF.Exp, scale=-0.5))
                sm = b.s_settle()
                b.wait("vector", "scalar", sm)
                v5 = b.do("vector", lambda e: e.tensor_mul(pa_a[:], pa_b[:], pa_t2[:]))
                b.do_("vector", lambda e: e.tensor_copy(st_m[:], pa_m[:, 0:NRB]))
                b.do_("vector", lambda e: e.tensor_copy(st_rstd[:], pa_b[:, 0:NRB]))
                b.wait("vector", "vector", v5)
                vm = b.do("vector", lambda e: e.tensor_copy(st_c1[:], pa_a[:, 0:NRB]))

                with contextlib.ExitStack() as pt:
                    ps_tp = [ps(f"ps_tp{i}", [128, 512], BF16, pt) for i in range(4)]
                    b.wait("vector", "vector", vm)
                    vdone2 = {}
                    tdone = {}
                    vevac = {}
                    for rb in range(NRB):
                        bi = rb % 4
                        if rb < 4:
                            b.wait("sync", "vector", vm)
                            b.wait("sync", "scalar", b.cnt["scalar"])
                        else:
                            b.wait("sync", "vector", vdone2[rb - 4])
                        lx = b.dma("sync", tok_x[bi], lambda e, rb=rb, bi=bi: e.dma_start(
                            out=xbuf[bi][:], in_=I["x"][rb * 128:(rb + 1) * 128, :]))
                        b.wait_tok("vector", tok_x[bi], lx)
                        if rb >= 2:
                            b.wait("vector", "tensor", tdone[(rb - 2) * 4 + 3])
                        hm = b.do("vector", lambda e, rb=rb, bi=bi: e.tensor_scalar(
                            out=hb[rb % 2][:], in0=xbuf[bi][:],
                            scalar1=st_m[:, rb:rb + 1], scalar2=st_c1[:, rb:rb + 1],
                            op0=ALU.subtract, op1=ALU.mult))
                        vdone2[rb] = hm
                        b.wait("tensor", "vector", hm)
                        for g4 in range(4):
                            pb = g4 % 4
                            bidx = rb * 4 + g4
                            if bidx >= 4:
                                b.wait("tensor", "vector", vevac[bidx - 4])
                            for j in range(4):
                                c = g4 * 4 + j
                                fn = lambda e, rb=rb, c=c, pb=pb, j=j: e.transpose(
                                    ps_tp[pb][:, j * 128:(j + 1) * 128],
                                    hb[rb % 2][:, c * 128:(c + 1) * 128], identb[:])
                                if j == 3:
                                    tdone[bidx] = b.do("tensor", fn)
                                else:
                                    b.do_("tensor", fn)
                            b.wait("vector", "tensor", tdone[bidx])
                            for j in range(4):
                                c = g4 * 4 + j
                                fn = lambda e, c=c, pb=pb, j=j, rb=rb: e.tensor_scalar(
                                    out=hinT[:, c * S + rb * 128: c * S + (rb + 1) * 128],
                                    in0=ps_tp[pb][:, j * 128:(j + 1) * 128],
                                    scalar1=n1w_pp[:, c:c + 1], scalar2=None, op0=ALU.mult)
                                if j == 3:
                                    vevac[bidx] = b.do("vector", fn)
                                else:
                                    b.do_("vector", fn)
                        if rb % 4 == 3:
                            g = rb // 4
                            b.wait("sync", "vector", b.cnt["vector"])
                            b.dma("sync", tok_scrh, lambda e, g=g: e.dma_start(
                                out=scr_hin[:, g * 512:(g + 1) * 512].rearrange(
                                    "(c p) s -> p c s", p=128),
                                in_=hinT[:].rearrange("p (c s) -> p c s", s=S)[
                                    :, :, g * 512:(g + 1) * 512]))
                b.barrier()

            with contextlib.ExitStack() as pg:
                gA = sb("gA", [8, S], F32, pg)
                gB = sb("gB", [8, S], F32, pg)
                gC = sb("gC", [8, S], F32, pg)
                gZ = sb("gZ", [8, S], F32, pg)
                with contextlib.ExitStack() as pgp:
                    ps_ig = ps("ps_ig", [8, S], F32, pgp)
                    ps_fg = ps("ps_fg", [8, S], F32, pgp)
                    tg = 0
                    for st in range(4):
                        for c in range(NCH):
                            b.do_("tensor", lambda e, st=st, c=c: e.matmul(
                                ps_ig[:, st * 512:(st + 1) * 512],
                                wigt[:, c * H:(c + 1) * H],
                                hinT[:, c * S + st * 512: c * S + (st + 1) * 512],
                                start=(c == 0), stop=(c == NCH - 1)))
                        for c in range(NCH):
                            fn = lambda e, st=st, c=c: e.matmul(
                                ps_fg[:, st * 512:(st + 1) * 512],
                                wfgt[:, c * H:(c + 1) * H],
                                hinT[:, c * S + st * 512: c * S + (st + 1) * 512],
                                start=(c == 0), stop=(c == NCH - 1))
                            if c == NCH - 1:
                                tg = b.do("tensor", fn)
                            else:
                                b.do_("tensor", fn)
                    b.wait("scalar", "tensor", tg)
                    s1 = b.do("scalar", lambda e: e.activation(
                        out=gA[:], in_=ps_ig[:], func=AF.Tanh, scale=float(1 / CAPV),
                        bias=b15i[:]))
                    sm = b.do("scalar", lambda e: e.activation(
                        out=gB[:], in_=ps_fg[:], func=AF.Tanh, scale=float(1 / CAPV),
                        bias=b15f[:]))
                    b.wait("vector", "scalar", sm)
                    b.do_("vector", lambda e: e.memset(gZ[:], 0.0))
                    b.do_("vector", lambda e: e.tensor_scalar(
                        out=gA[:], in0=gA[:], scalar1=CAPV, scalar2=None, op0=ALU.mult))
                    vm = b.do("vector", lambda e: e.tensor_scalar(
                        out=gB[:], in0=gB[:], scalar1=CAPV, scalar2=None, op0=ALU.mult))
                if "gates" in taps:
                    t = add_tap("gates_if", [8, 2 * S])
                    b.wait("sync", "vector", vm)
                    b.dma("sync", tok_tap, lambda e, t=t: e.dma_start(out=t[:, 0:S], in_=gA[:]))
                    b.dma("sync", tok_tap, lambda e, t=t: e.dma_start(out=t[:, S:2 * S], in_=gB[:]))
                b.wait("scalar", "vector", vm)
                if "gates" in taps:
                    b.wait_tok("scalar", tok_tap)
                s2 = b.do("scalar", lambda e: e.activation(out=gC[:], in_=gB[:], func=AF.Exp, scale=-1.0))
                b.wait("scalar", "scalar", s2)
                sm = b.do("scalar", lambda e: e.activation(out=gB[:], in_=gC[:], func=AF.Ln, bias=one8[:]))
                b.wait("vector", "scalar", sm)
                b.flush("vector")
                v1 = b.do("vector", lambda e: e.tensor_tensor_scan(
                    out=gC[:], data0=gB[:], data1=gZ[:], initial=0.0,
                    op0=ALU.add, op1=ALU.add))
                b.wait("vector", "vector", v1)
                v2 = b.do("vector", lambda e: e.tensor_add(gA[:], gA[:], gC[:]))
                b.wait("vector", "vector", v2)
                v3 = b.do("vector", lambda e: e.tensor_tensor_scan(
                    out=gM[:], data0=gA[:], data1=gZ[:], initial=-1e30,
                    op0=ALU.max, op1=ALU.add))
                b.wait("vector", "vector", v3)
                vm = b.do("vector", lambda e: e.tensor_sub(gB[:], gC[:], gM[:]))
                b.wait("scalar", "vector", vm)
                sm = b.do("scalar", lambda e: e.activation(out=gEn[:], in_=gB[:], func=AF.Exp))
                if "gates" in taps:
                    t = add_tap("gates2", [8, 4 * S])
                    b.wait("sync", "vector", vm)
                    b.wait("sync", "scalar", sm)
                    b.dma("sync", tok_tap, lambda e, t=t: e.dma_start(out=t[:, 0:S], in_=gC[:]))
                    b.dma("sync", tok_tap, lambda e, t=t: e.dma_start(out=t[:, S:2 * S], in_=gA[:]))
                    b.dma("sync", tok_tap, lambda e, t=t: e.dma_start(out=t[:, 2 * S:3 * S], in_=gM[:]))
                    b.dma("sync", tok_tap, lambda e, t=t: e.dma_start(out=t[:, 3 * S:4 * S], in_=gEn[:]))
                b.wait("sync", "vector", vm)
                b.wait("sync", "scalar", sm)
                b.dma("sync", tok_tap, lambda e: e.dma_start(out=scr_gme[0:8, :], in_=gM[:]))
                b.dma("sync", tok_tap, lambda e: e.dma_start(out=scr_gme[8:16, :], in_=gEn[:]))
                with contextlib.ExitStack() as pgc:
                    ps_gc = ps("ps_gc", [128, 32], F32, pgc)
                    b.wait("tensor", "vector", vm)
                    for g4 in range(4):
                        for j in range(4):
                            tc = g4 * 4 + j
                            fn = lambda e, tc=tc, j=j: e.transpose(
                                ps_gc[:, j * 8:(j + 1) * 8], gA[:, tc * 128:(tc + 1) * 128],
                                ident32[0:8, 0:8])
                            if j == 3:
                                tm = b.do("tensor", fn)
                            else:
                                b.do_("tensor", fn)
                        b.wait("vector", "tensor", tm)
                        vmg = b.do("vector", lambda e, g4=g4: e.tensor_scalar(
                            out=Gcol[:, g4 * 32:(g4 + 1) * 32], in0=ps_gc[:],
                            scalar1=LOG_ISQ, scalar2=None, op0=ALU.add))
                        b.wait("tensor", "vector", vmg)
                a2_sig = b.cnt["scalar"]
                a2_vm = b.cnt["vector"]
                for e_ in ("tensor", "sync", "gpsimd"):
                    b.wait(e_, "scalar", a2_sig)
                    b.wait(e_, "vector", a2_vm)
                for tok in b.all_toks:
                    if tok[1] > 0:
                        b.wait_tok("sync", tok)
                        b.wait_tok("gpsimd", tok)

        scope_bm = contextlib.ExitStack()
        scope_ac.enter_context(scope_bm)
        big_b = sb("big_b", [128, NCH * S], BF16, scope_bm)
        with contextlib.ExitStack() as pb:
            wkq_cb = [sb("wkq0", [128, NCH * 128], BF16, pb),
                      sb("wkq1", [128, NCH * 128], BF16, pb)]
            wv_ct = sb("wv_ct", [128, NCH * 512], BF16, pb)
            vstage = sb("vstage", [128, NCH * 512], BF16, pb)
            tok_wb = [b.dsem(es), b.dsem(es)]
            tok_wv = b.dsem(es)
            tok_scrv = b.dsem(es)
            with contextlib.ExitStack() as pbp:
                ps_b = [ps(f"ps_b{i}", [128, 512], F32, pbp) for i in range(4)]
                t_used = {}
                v_evac = {}
                lwc = {}
                for idx, (wname, off) in enumerate([("wk", KOFF), ("wq", QOFF)]):
                    for oc in range(8):
                        gi = idx * 8 + oc
                        slot = gi % 2
                        if gi >= 2:
                            b.wait("gpsimd", "tensor", t_used[gi - 2])
                        lwc[gi] = b.dma("gpsimd", tok_wb[slot], lambda e, wname=wname, oc=oc, slot=slot: e.dma_start(
                            out=wkq_cb[slot][:].rearrange("p (c o) -> p c o", o=128),
                            in_=I[wname][:, oc * 128:(oc + 1) * 128].rearrange(
                                "(c p) o -> p c o", p=128)))
                        b.wait_tok("tensor", tok_wb[slot], lwc[gi])
                        for nt in range(4):
                            ti = gi * 4 + nt
                            pbk = nt % 4
                            if ti >= 4:
                                b.wait("tensor", "vector", v_evac[ti - 4])
                            tm = 0
                            for c in range(NCH):
                                fn = lambda e, slot=slot, c=c, pbk=pbk, nt=nt: e.matmul(
                                    ps_b[pbk][:],
                                    wkq_cb[slot][:, c * 128:(c + 1) * 128],
                                    hinT[:, c * S + nt * 512: c * S + (nt + 1) * 512],
                                    start=(c == 0), stop=(c == NCH - 1))
                                if c == NCH - 1:
                                    tm = b.do("tensor", fn)
                                else:
                                    b.do_("tensor", fn)
                            if nt == 3:
                                t_used[gi] = tm
                            b.wait("vector", "tensor", tm)
                            v_evac[ti] = b.do("vector", lambda e, off=off, oc=oc, nt=nt, pbk=pbk: e.tensor_copy(
                                big_b[:, off + oc * S + nt * 512: off + oc * S + (nt + 1) * 512],
                                ps_b[pbk][:]))
                vm_dma = 0
                for dvt in range(4):
                    if dvt > 0:
                        b.wait("gpsimd", "tensor", t_used[("v", dvt - 1)])
                    lv = b.dma("gpsimd", tok_wv, lambda e, dvt=dvt: e.dma_start(
                        out=wv_ct[:].rearrange("p (c o) -> p c o", o=512),
                        in_=I["wv"][:, dvt * 512:(dvt + 1) * 512].rearrange(
                            "(c p) o -> p c o", p=128)))
                    b.wait_tok("tensor", tok_wv, lv)
                    if dvt > 0:
                        b.wait_tok("tensor", tok_scrv, vm_dma)
                    for m in range(NRB):
                        pbk = m % 4
                        if m >= 4:
                            b.wait("tensor", "vector", v_evac[("v", dvt, m - 4)])
                        elif dvt > 0:
                            b.wait("tensor", "vector", v_evac[("v", dvt - 1, NRB - 1)])
                        else:
                            b.wait("tensor", "vector", v_evac[63])
                        tm = 0
                        for c in range(NCH):
                            fn = lambda e, c=c, m=m, pbk=pbk: e.matmul(
                                ps_b[pbk][:],
                                hinT[:, c * S + m * 128: c * S + (m + 1) * 128],
                                wv_ct[:, c * 512:(c + 1) * 512],
                                start=(c == 0), stop=(c == NCH - 1))
                            if c == NCH - 1:
                                tm = b.do("tensor", fn)
                            else:
                                b.do_("tensor", fn)
                        if m == NRB - 1:
                            t_used[("v", dvt)] = tm
                        b.wait("vector", "tensor", tm)
                        v_evac[("v", dvt, m)] = b.do("vector", lambda e, m=m, pbk=pbk: e.tensor_copy(
                            vstage[:, m * 512:(m + 1) * 512], ps_b[pbk][:]))
                    b.wait("sync", "vector", v_evac[("v", dvt, NRB - 1)])
                    vm_dma = b.dma("sync", tok_scrv, lambda e, dvt=dvt: e.dma_start(
                        out=scr_v[:, dvt * 512:(dvt + 1) * 512].rearrange(
                            "(m p) d -> p m d", p=128),
                        in_=vstage[:].rearrange("p (m d) -> p m d", d=512)))
            b.barrier()

        hT = big_a
        with contextlib.ExitStack() as pm:
            V_h = [sb("V_h0", [128, NCH * 256], BF16, pm), sb("V_h1", [128, NCH * 256], BF16, pm)]
            M_rep = sb("M_rep", [128, S], F32, pm)
            Mrow1 = sb("Mrow1", [1, S], F32, pm)
            EnRow1 = sb("EnRow1", [1, 512], F32, pm)
            E_sb = [sb(f"E_sb{i}", [128, 512], F32, pm) for i in range(3)]
            C_sb = [sb(f"C_sb{i}", [128, 512], BF16, pm) for i in range(3)]
            nrep = sb("nrep", [128, 512], F32, pm)
            nabs = sb("nabs", [1, 512], F32, pm)
            nmax = sb("nmax", [1, 512], F32, pm)
            ninv = sb("ninv", [1, 512], F32, pm)
            dmask_sb = sb("dmask_sb", [128, 4 * 512], F32, pm)
            tok_vh = [b.dsem(es), b.dsem(es)]
            tok_row = b.dsem(es)
            tok_er = b.dsem(es)
            tok_dm = b.dsem(es)
            b.dma("sync", tok_dm, lambda e: e.dma_start(out=dmask_sb[:], in_=I["c_dmask"]))
            for e in ("vector",):
                b.wait_tok(e, tok_dm)
            with contextlib.ExitStack() as pmp:
                ps_qk = [ps(f"ps_qk{i}", [128, 512], F32, pmp) for i in range(2)]
                ps_h = [ps(f"ps_h{i}", [128, 512], F32, pmp) for i in range(4)]
                ps_row = ps("ps_row", [1, 512], F32, pmp)
                ps_rep2 = ps("ps_rep2", [128, 512], F32, pmp)
                sE_prev = {}
                vC = {}
                last_vh_use = {}
                s_absmark = 0
                v_hevac = 0
                v_hevac_pair = {}
                trep = 0
                last_rep_read = 0
                ti = 0
                for h in range(H):
                    slot = h % 2
                    if h >= 2:
                        b.wait("sync", "tensor", last_vh_use[h - 2])
                    lv = b.dma("sync", tok_vh[slot], lambda e, h=h, slot=slot: e.dma_start(
                        out=V_h[slot][:].rearrange("p (tc d) -> p tc d", d=256),
                        in_=scr_v[:, h * 256:(h + 1) * 256].rearrange(
                            "(tc p) d -> p tc d", p=128)))
                    if h > 0:
                        b.wait("sync", "tensor", mrep_t)
                    lr = b.dma("sync", tok_row, lambda e, h=h: e.dma_start(
                        out=Mrow1[:], in_=scr_gme[h:h + 1, :]))
                    b.wait_tok("tensor", tok_row, lr)
                    if h > 0:
                        b.wait("tensor", "scalar", sE_prev[h - 1])
                        b.wait("tensor", "vector", last_rep_read)
                    for q4 in range(4):
                        if q4 > 0:
                            b.wait("tensor", "vector", last_rep_read)
                        mrep_t = b.do("tensor", lambda e, q4=q4: e.matmul(
                            ps_rep2[:], ones_row[:], Mrow1[0:1, q4 * 512:(q4 + 1) * 512],
                            start=True, stop=True))
                        b.wait("vector", "tensor", mrep_t)
                        v_mrep_evac = b.do("vector", lambda e, q4=q4: e.tensor_copy(
                            M_rep[:, q4 * 512:(q4 + 1) * 512], ps_rep2[:]))
                        last_rep_read = v_mrep_evac
                    for st in range(4):
                        ntc = 4 * (st + 1)
                        if not (h == 0 and st == 0):
                            b.wait("sync", "vector", v_nmax)
                        ler = b.dma("sync", tok_er, lambda e, h=h, st=st: e.dma_start(
                            out=EnRow1[:], in_=scr_gme[8 + h:9 + h, st * 512:(st + 1) * 512]))
                        hx = st % 2
                        def emit_qk(tc, qx):
                            b.wait_tok("tensor", tok_vh[slot], lv)
                            return b.do("tensor", lambda e, h=h, st=st, tc=tc, qx=qx: e.matmul(
                                ps_qk[qx][:],
                                big_b[:, KOFF + h * S + tc * 128: KOFF + h * S + (tc + 1) * 128],
                                big_b[:, QOFF + h * S + st * 512: QOFF + h * S + (st + 1) * 512],
                                start=True, stop=True))
                        if ti >= 2:
                            b.wait("tensor", "vector", vC[ti - 2])
                        tqk_pend = emit_qk(0, ti % 2)
                        for tc in range(ntc):
                            qx = ti % 2
                            tqk = tqk_pend
                            if ti >= 2:
                                b.wait("scalar", "vector", vC[ti - 2])
                            b.wait("scalar", "vector", v_mrep_evac)
                            sE = b.do("scalar", lambda e, h=h, st=st, tc=tc, qx=qx: e.activation(
                                out=E_sb[qx][:], in_=M_rep[:, st * 512:(st + 1) * 512],
                                func=AF.Exp, scale=-1.0,
                                bias=Gcol[:, tc * 8 + h: tc * 8 + h + 1]))
                            sE_prev[h] = sE
                            b.wait("vector", "scalar", sE)
                            if tc >= 4 * st:
                                k = tc - 4 * st
                                vmm = b.do("vector", lambda e, qx=qx, k=k: e.tensor_mul(
                                    E_sb[qx][:], E_sb[qx][:],
                                    dmask_sb[:, k * 512:(k + 1) * 512]))
                                b.wait("vector", "vector", vmm)
                            b.wait("vector", "tensor", tqk)
                            vC[ti] = b.do("vector", lambda e, qx=qx: e.tensor_mul(
                                C_sb[qx][:], ps_qk[qx][:], E_sb[qx][:]))
                            if tc + 1 < ntc:
                                if ti + 1 >= 2:
                                    b.wait("tensor", "vector", vC[ti - 1])
                                tqk_pend = emit_qk(tc + 1, (ti + 1) % 2)
                            b.wait("tensor", "vector", vC[ti])
                            if tc == 0 and s_absmark:
                                b.wait("tensor", "scalar", s_absmark)
                            b.do_("tensor", lambda e, tc=tc, qx=qx, ntc=ntc: e.matmul(
                                ps_row[:], ones_colb[:], C_sb[qx][:],
                                start=(tc == 0), stop=(tc == ntc - 1),
                                skip_group_check=True))
                            if tc == 0 and v_hevac_pair.get(hx):
                                b.wait("tensor", "vector", v_hevac_pair[hx])
                            for dvc in range(2):
                                fn = lambda e, slot=slot, tc=tc, dvc=dvc, qx=qx, ntc=ntc, hx=hx: e.matmul(
                                    ps_h[2 * hx + dvc][:],
                                    V_h[slot][:, tc * 256 + dvc * 128: tc * 256 + dvc * 128 + 128],
                                    C_sb[qx][:],
                                    start=(tc == 0), stop=(tc == ntc - 1),
                                    skip_group_check=True)
                                if dvc == 1 and tc == ntc - 1:
                                    th_done = b.do("tensor", fn)
                                else:
                                    b.do_("tensor", fn)
                            ti += 1
                        last_vh_use[h] = th_done
                        b.wait("scalar", "tensor", th_done)
                        s_absmark = b.do("scalar", lambda e: e.activation(
                            out=nabs[:], in_=ps_row[:], func=AF.Abs))
                        b.wait("vector", "scalar", s_absmark)
                        b.wait_tok("vector", tok_er, ler)
                        v_nmax = b.do("vector", lambda e: e.tensor_max(
                            nmax[:], nabs[:], EnRow1[0:1, :]))
                        b.wait("scalar", "vector", v_nmax)
                        if trep:
                            b.wait("scalar", "tensor", trep)
                        sl = b.do("scalar", lambda e: e.activation(out=ninv[:], in_=nmax[:], func=AF.Ln))
                        b.wait("scalar", "scalar", sl)
                        sl2 = b.do("scalar", lambda e: e.activation(
                            out=ninv[:], in_=ninv[:], func=AF.Exp, scale=-1.0))
                        sl2 = b.s_settle()
                        b.wait("tensor", "scalar", sl2)
                        b.wait("tensor", "vector", last_rep_read)
                        trep = b.do("tensor", lambda e: e.matmul(
                            ps_rep2[:], ones_row[:], ninv[:], start=True, stop=True))
                        b.wait("vector", "tensor", trep)
                        vrep = b.do("vector", lambda e: e.tensor_copy(nrep[:], ps_rep2[:]))
                        last_rep_read = vrep
                        b.wait("vector", "vector", vrep)
                        for dvc in range(2):
                            r = h * 2 + dvc
                            fn = lambda e, r=r, st=st, dvc=dvc, hx=hx: e.tensor_mul(
                                hT[:, r * S + st * 512: r * S + (st + 1) * 512],
                                ps_h[2 * hx + dvc][:], nrep[:])
                            v_hevac = b.do("vector", fn)
                        v_hevac_pair[hx] = v_hevac
            b.barrier()
        scope_bm.close()

        with contextlib.ExitStack() as pc:
            hintile = sb("hintile", [128, NCH * 512], BF16, pc)
            sig_blk = sb("sig_blk", [128, NCH * 512], BF16, pc)
            houtT = sb("houtT", [128, NCH * 512], BF16, pc)
            wog_cb = [sb("wog0", [128, NCH * 128], BF16, pc),
                      sb("wog1", [128, NCH * 128], BF16, pc)]
            wout_nt = [sb("wout_nt0", [128, NCH * 512], BF16, pc),
                       sb("wout_nt1", [128, NCH * 512], BF16, pc)]
            xc_all = sb("xc_all", [128, 4 * D], BF16, pc)
            x1st = sb("x1st", [128, 4 * D], BF16, pc)
            hsq = sb("hsq", [128, 512], BF16, pc)
            hn1 = sb("hn1", [1, 512], F32, pc)
            rstd_rep = [sb("rsr0", [128, 512], F32, pc), sb("rsr1", [128, 512], F32, pc)]
            tmp1 = [sb("tmp1a", [128, 512], BF16, pc), sb("tmp1b", [128, 512], BF16, pc)]
            xa_t = [sb("xa0", [128, 512], F32, pc), sb("xa1", [128, 512], F32, pc)]
            tok_hin = b.dsem(es)
            tok_wog = [b.dsem(es), b.dsem(es)]
            tok_wot = [b.dsem(es), b.dsem(es)]
            tok_xc = b.dsem(es)
            tok_x1w = b.dsem(es)
            with contextlib.ExitStack() as pcp:
                ps_o = [ps(f"ps_o{i}", [128, 512], F32, pcp) for i in range(4)]
                ps_hn = ps("ps_hn", [1, 512], F32, pcp)
                ps_rr = ps("ps_rr", [128, 512], F32, pcp)
                ps_x1 = [ps("ps_x10", [128, 512], F32, pcp), ps("ps_x11", [128, 512], F32, pcp)]
                t_wog_used = {}
                s_sig = {}
                v_hout = {}
                v_hn1 = 0
                v_rrevac = 0
                t_x1_last = 0
                t_x1w_use = {}
                t_hin_last = {}
                v_x1evac = {}
                v_hout_last = {}
                v_xa_last = {}
                x1w = {}
                t_rr = 0
                lh_pend = b.dma("sync", tok_hin, lambda e: e.dma_start(
                    out=hintile[:].rearrange("p (c s) -> p c s", s=512),
                    in_=scr_hin[:, 0:512].rearrange("(c p) s -> p c s", p=128)))
                for st in range(4):
                    lh = lh_pend
                    if st > 0:
                        b.wait("scalar", "vector", v_hout_last[(st - 1, H - 1)])
                    for oc in range(NCH):
                        gi = st * NCH + oc
                        slot = gi % 2
                        if gi >= 2:
                            b.wait("gpsimd", "tensor", t_wog_used[gi - 2])
                        lw = b.dma("gpsimd", tok_wog[slot], lambda e, oc=oc, slot=slot: e.dma_start(
                            out=wog_cb[slot][:].rearrange("p (c o) -> p c o", o=128),
                            in_=I["wog"][:, oc * 128:(oc + 1) * 128].rearrange(
                                "(c p) o -> p c o", p=128)))
                        b.wait_tok("tensor", tok_wog[slot], lw)
                        b.wait_tok("tensor", tok_hin, lh)
                        if gi >= 4:
                            b.wait("tensor", "scalar", s_sig[gi - 4])
                        tm = 0
                        for c in range(NCH):
                            fn = lambda e, slot=slot, gi=gi, c=c: e.matmul(
                                ps_o[gi % 4][:],
                                wog_cb[slot][:, c * 128:(c + 1) * 128],
                                hintile[:, c * 512:(c + 1) * 512],
                                start=(c == 0), stop=(c == NCH - 1))
                            if c == NCH - 1:
                                tm = b.do("tensor", fn)
                            else:
                                b.do_("tensor", fn)
                        t_wog_used[gi] = tm
                        if oc == NCH - 1:
                            t_hin_last[st] = tm
                            if st < 3:
                                b.wait("sync", "tensor", t_hin_last[st])
                                lh_pend = b.dma("sync", tok_hin, lambda e, st=st: e.dma_start(
                                    out=hintile[:].rearrange("p (c s) -> p c s", s=512),
                                    in_=scr_hin[:, (st + 1) * 512:(st + 2) * 512].rearrange(
                                        "(c p) s -> p c s", p=128)))
                        b.wait("scalar", "tensor", tm)
                        s_sig[gi] = b.do("scalar", lambda e, oc=oc, gi=gi: e.activation(
                            out=sig_blk[:, oc * 512:(oc + 1) * 512], in_=ps_o[gi % 4][:],
                            func=AF.Sigmoid))
                    for hh in range(H):
                        t_hn = 0
                        for k in range(2):
                            r = 2 * hh + k
                            if t_hn:
                                b.wait("vector", "tensor", t_hn)
                            vq = b.do("vector", lambda e, r=r, st=st: e.tensor_mul(
                                hsq[:], hT[:, r * S + st * 512: r * S + (st + 1) * 512],
                                hT[:, r * S + st * 512: r * S + (st + 1) * 512]))
                            b.wait("tensor", "vector", vq)
                            if k == 0 and v_hn1:
                                b.wait("tensor", "vector", v_hn1)
                            fn = lambda e, k=k: e.matmul(
                                ps_hn[:], ones_colb[:], hsq[:],
                                start=(k == 0), stop=(k == 1), skip_group_check=True)
                            t_hn = b.do("tensor", fn)
                        b.wait("vector", "tensor", t_hn)
                        v_hn1 = b.do("vector", lambda e: e.tensor_copy(hn1[:], ps_hn[:]))
                        b.wait("vector", "vector", v_hn1)
                        v_hn2 = b.do("vector", lambda e: e.tensor_scalar(
                            out=hn1[:], in0=hn1[:], scalar1=float(1.0 / DV), scalar2=EPS2,
                            op0=ALU.mult, op1=ALU.add))
                        b.wait("scalar", "vector", v_hn2)
                        sA = b.do("scalar", lambda e: e.activation(out=hn1[:], in_=hn1[:], func=AF.Ln))
                        b.wait("scalar", "scalar", sA)
                        b.do_("scalar", lambda e: e.activation(
                            out=hn1[:], in_=hn1[:], func=AF.Exp, scale=-0.5))
                        sB = b.s_settle()
                        b.wait("tensor", "scalar", sB)
                        if v_rrevac:
                            b.wait("tensor", "vector", v_rrevac)
                        t_rr = b.do("tensor", lambda e: e.matmul(
                            ps_rr[:], ones_row[:], hn1[:], start=True, stop=True))
                        b.wait("vector", "tensor", t_rr)
                        if hh >= 2 and v_hout_last.get((st, hh - 2)):
                            b.wait("vector", "vector", v_hout_last[(st, hh - 2)])
                        elif st > 0 and hh < 2:
                            b.wait("vector", "vector", v_hout_last[(st - 1, H - 2 + hh)])
                        v_rrevac = b.do("vector", lambda e, hh=hh: e.tensor_copy(
                            rstd_rep[hh % 2][:], ps_rr[:]))
                        b.wait("vector", "vector", v_rrevac)
                        if st == 0 and hh == 0 and t_x1_last:
                            b.wait("vector", "tensor", t_x1_last)
                        for k in range(2):
                            r = 2 * hh + k
                            if st > 0 or hh > 0 or k > 0:
                                pass
                            vv1 = b.do("vector", lambda e, r=r, st=st, k=k, hh=hh: e.scalar_tensor_tensor(
                                out=tmp1[k][:],
                                in0=hT[:, r * S + st * 512: r * S + (st + 1) * 512],
                                scalar=mhw_pp[:, r:r + 1],
                                in1=rstd_rep[hh % 2][:],
                                op0=ALU.mult, op1=ALU.mult))
                            b.wait("vector", "vector", vv1)
                            b.wait("vector", "scalar", s_sig[st * NCH + r])
                            v_hout[(st, r)] = b.do("vector", lambda e, r=r, k=k: e.tensor_mul(
                                houtT[:, r * 512:(r + 1) * 512], tmp1[k][:],
                                sig_blk[:, r * 512:(r + 1) * 512]))
                        v_hout_last[(st, hh)] = v_hout[(st, 2 * hh + 1)]
                    if st > 0:
                        b.wait("gpsimd", "vector", v_xa_last[st - 1])
                    lxc = {}
                    for mi in range(4):
                        lxc[mi] = b.dma("gpsimd", tok_xc, lambda e, st=st, mi=mi: e.dma_start(
                            out=xc_all[:, mi * D:(mi + 1) * D],
                            in_=I["x"][(st * 4 + mi) * 128:(st * 4 + mi + 1) * 128, :]))
                    if st > 0:
                        b.wait_tok("vector", tok_x1w, x1w[st - 1])
                    for nt in range(4):
                        gnt = st * 4 + nt
                        wsl = gnt % 2
                        if gnt >= 2:
                            b.wait("gpsimd", "tensor", t_x1w_use[gnt - 2])
                        lwo = b.dma("gpsimd", tok_wot[wsl], lambda e, nt=nt, wsl=wsl: e.dma_start(
                            out=wout_nt[wsl][:].rearrange("p (c o) -> p c o", o=512),
                            in_=I["wout"][:, nt * 512:(nt + 1) * 512].rearrange(
                                "(c p) o -> p c o", p=128)))
                        b.wait_tok("tensor", tok_wot[wsl], lwo)
                        for mi in range(4):
                            idx = st * 16 + nt * 4 + mi
                            sx = idx % 2
                            if idx >= 2:
                                b.wait("tensor", "vector", v_x1evac[idx - 2])
                            tm = 0
                            for c in range(NCH):
                                b.wait("tensor", "vector", v_hout[(st, c)])
                                fn = lambda e, c=c, mi=mi, sx=sx, wsl=wsl: e.matmul(
                                    ps_x1[sx][:],
                                    houtT[:, c * 512 + mi * 128: c * 512 + mi * 128 + 128],
                                    wout_nt[wsl][:, c * 512:(c + 1) * 512],
                                    start=(c == 0), stop=(c == NCH - 1))
                                if c == NCH - 1:
                                    tm = b.do("tensor", fn)
                                else:
                                    b.do_("tensor", fn)
                            t_x1_last = tm
                            t_x1w_use[gnt] = tm
                            b.wait("vector", "tensor", tm)
                            b.wait_tok("vector", tok_xc)
                            rb = st * 4 + mi
                            vv1 = b.do("vector", lambda e, mi=mi, nt=nt, rb=rb, sx=sx: e.tensor_scalar(
                                out=xa_t[sx][:],
                                in0=xc_all[:, mi * D + nt * 512: mi * D + (nt + 1) * 512],
                                scalar1=st_m[:, rb:rb + 1], scalar2=st_rstd[:, rb:rb + 1],
                                op0=ALU.subtract, op1=ALU.mult))
                            b.wait("vector", "vector", vv1)
                            v_x1evac[idx] = b.do("vector", lambda e, mi=mi, nt=nt, sx=sx: e.tensor_add(
                                x1st[:, mi * D + nt * 512: mi * D + (nt + 1) * 512],
                                xa_t[sx][:], ps_x1[sx][:]))
                    v_xa_last[st] = v_x1evac[st * 16 + 15]
                    b.wait("sync", "vector", v_xa_last[st])
                    for mi in range(4):
                        x1w[st] = b.dma("sync", tok_x1w, lambda e, st=st, mi=mi: e.dma_start(
                            out=scr_x1[(st * 4 + mi) * 128:(st * 4 + mi + 1) * 128, :],
                            in_=x1st[:, mi * D:(mi + 1) * D]))
            b.barrier()
        scope_ac.close()

        with contextlib.ExitStack() as pq:
            sup_sb = sb("sup_sb", [128, 64 * 32], F32, pq)
            sdn_sb = sb("sdn_sb", [128, 16 * 128], F32, pq)
            qtile = [sb("qt0", [128, 4096], I32, pq), sb("qt1", [128, 4096], I32, pq)]
            dq_t = [sb("dqt0", [128, 4096], BF16, pq), sb("dqt1", [128, 4096], BF16, pq)]
            stageU = sb("stageU", [128, NCH * 512], BF16, pq)
            stageD = sb("stageD", [128, 32 * 512], BF16, pq)
            tok_q = [b.dsem(es), b.dsem(es)]
            tok_sc = b.dsem(es)
            tok_stg = b.dsem(es)
            b.dma("sync", tok_sc, lambda e: e.dma_start(
                out=sup_sb[:].rearrange("p (fb c) -> p fb c", c=32),
                in_=I["s_up"].rearrange("(fb p) c -> p fb c", p=128)))
            b.dma("sync", tok_sc, lambda e: e.dma_start(
                out=sdn_sb[:].rearrange("p (db c) -> p db c", c=128),
                in_=I["s_dn"].rearrange("(db p) c -> p db c", p=128)))
            b.wait_tok("vector", tok_sc)
            with contextlib.ExitStack() as pqp:
                ps_t = [ps("ps_tq0", [128, 512], BF16, pqp), ps("ps_tq1", [128, 512], BF16, pqp)]
                state = {"seq": 0, "pkey": {0: None, 1: None}, "tpk": {0: None, 1: None},
                         "v_dq": {}, "t_tp": {}, "s_ev": {}}

                def dq_tile(qsrc, rows, cols, srep, stage, woff, nch_t):
                    st8 = state
                    slot = st8["seq"] % 2
                    width = cols[1] - cols[0]
                    if st8["pkey"][slot] is not None:
                        b.wait("sync", "vector", st8["v_dq"][st8["pkey"][slot]])
                    lq = b.dma("sync", tok_q[slot], lambda e, qsrc=qsrc, rows=rows, cols=cols, slot=slot, width=width: e.dma_start(
                        out=qtile[slot][:, 0:width],
                        in_=qsrc[rows[0]:rows[1], cols[0]:cols[1]]))
                    b.wait_tok("vector", tok_q[slot], lq)
                    if st8["pkey"][slot] is not None:
                        b.wait("vector", "tensor", st8["t_tp"][st8["pkey"][slot]])
                    key = st8["seq"]
                    st8["v_dq"][key] = b.do("vector", lambda e, slot=slot, srep=srep, width=width: e.scalar_tensor_tensor(
                        out=dq_t[slot][:, 0:width].rearrange("p (c r) -> p c r", r=64),
                        in0=qtile[slot][:, 0:width].rearrange("p (c r) -> p c r", r=64),
                        scalar=7.5, in1=srep,
                        op0=ALU.subtract, op1=ALU.mult))
                    st8["pkey"][slot] = key
                    b.wait("tensor", "vector", st8["v_dq"][key])
                    for g4 in range(nch_t // 4):
                        pbk = g4 % 2
                        if st8["tpk"][pbk] is not None:
                            b.wait("tensor", "scalar", st8["s_ev"][st8["tpk"][pbk]])
                        tm = 0
                        for jj in range(4):
                            c = g4 * 4 + jj
                            fn = lambda e, slot=slot, c=c, pbk=pbk, jj=jj: e.transpose(
                                ps_t[pbk][:, jj * 128:(jj + 1) * 128],
                                dq_t[slot][:, c * 128:(c + 1) * 128], identb[:])
                            if jj == 3:
                                tm = b.do("tensor", fn)
                            else:
                                b.do_("tensor", fn)
                        st8["t_tp"][key] = tm
                        b.wait("scalar", "tensor", tm)
                        sev = b.do("scalar", lambda e, g4=g4, woff=woff, stage=stage, pbk=pbk: e.activation(
                            out=stage[:].rearrange("p (c f) -> p c f", f=512)[
                                :, g4 * 4:(g4 + 1) * 4, woff:woff + 128],
                            in_=ps_t[pbk][:].rearrange("p (c f) -> p c f", f=128),
                            func=AF.Identity))
                        st8["s_ev"][(key, g4)] = sev
                        st8["tpk"][pbk] = (key, g4)
                    st8["seq"] += 1

                for fbg in range(16):
                    for fbi in range(4):
                        fb = fbg * 4 + fbi
                        srep = sup_sb[:, fb * 32:(fb + 1) * 32].unsqueeze(2).broadcast_to([128, 32, 64])
                        dq_tile(I["q_up"], (fb * 128, (fb + 1) * 128), (0, 2048), srep,
                                stageU, fbi * 128, 16)
                    b.wait("sync", "scalar", b.cnt["scalar"])
                    b.dma("sync", tok_stg, lambda e, fbg=fbg: e.dma_start(
                        out=scr_wup[fbg].rearrange("c p f -> p c f"),
                        in_=stageU[:].rearrange("p (c f) -> p c f", f=512)))
                    b.wait_tok("scalar", tok_stg)
                for dbg in range(4):
                    for fh in range(2):
                        for dbi in range(4):
                            db = dbg * 4 + dbi
                            srep = sdn_sb[:, db * 128 + fh * 64: db * 128 + fh * 64 + 64
                                          ].unsqueeze(2).broadcast_to([128, 64, 64])
                            dq_tile(I["q_dn"], (db * 128, (db + 1) * 128),
                                    (fh * 4096, (fh + 1) * 4096), srep,
                                    stageD, dbi * 128, 32)
                        b.wait("sync", "scalar", b.cnt["scalar"])
                        b.dma("sync", tok_stg, lambda e, dbg=dbg, fh=fh: e.dma_start(
                            out=scr_wdn[dbg, fh * 32:(fh + 1) * 32].rearrange("j p d -> p j d"),
                            in_=stageD[:].rearrange("p (j d) -> p j d", d=512)))
                        b.wait_tok("scalar", tok_stg)
            b.barrier()

        with contextlib.ExitStack() as pd:
            uT = sb("uT", [128, 64 * 1024], BF16, pd)
            st3 = sb("st3", [128, NRB], F32, pd)
            tok_x1r = [b.dsem(es), b.dsem(es)]
            tok_wg = [b.dsem(es), b.dsem(es)]
            tok_wu = [b.dsem(es), b.dsem(es)]
            tok_wd = [b.dsem(es), b.dsem(es)]
            tok_xg = [b.dsem(es), b.dsem(es)]
            tok_x1c = [b.dsem(es), b.dsem(es)]
            tok_yw = b.dsem(es)
            tok_n2 = b.dsem(es)
            def d_group(g2):
                if g2 > 0:
                    for tk in (tok_yw, tok_xg[0], tok_xg[1], tok_x1c[0], tok_x1c[1]):
                        b.wait_tok("sync", tk)
                        b.wait_tok("gpsimd", tk)
                        b.wait_tok("vector", tk)
                        b.wait_tok("scalar", tk)
                with contextlib.ExitStack() as pdu:
                    h2T = sb(f"h2T_{g2}", [128, NCH * 1024], BF16, pdu)
                    x1b = [sb(f"x1b0_{g2}", [128, D], BF16, pdu), sb(f"x1b1_{g2}", [128, D], BF16, pdu)]
                    h2tok = [sb(f"h2t0_{g2}", [128, D], BF16, pdu), sb(f"h2t1_{g2}", [128, D], BF16, pdu)]
                    pd_b = sb(f"pd_b_{g2}", [128, 512], F32, pdu)
                    pd_c = sb(f"pd_c_{g2}", [128, 512], F32, pdu)
                    wg_cb = [sb(f"wg_cba_{g2}", [128, NCH * 128], BF16, pdu),
                             sb(f"wg_cbb_{g2}", [128, NCH * 128], BF16, pdu)]
                    wu_cb = [sb(f"wu_cba_{g2}", [128, NCH * 128], BF16, pdu),
                             sb(f"wu_cbb_{g2}", [128, NCH * 128], BF16, pdu)]
                    silu_sb = [sb(f"sil0_{g2}", [128, 512], BF16, pdu), sb(f"sil1_{g2}", [128, 512], BF16, pdu)]
                    with contextlib.ExitStack() as pdp:
                        ps_tp2 = [ps(f"ps_tpa_{g2}", [128, 512], BF16, pdp), ps(f"ps_tpb_{g2}", [128, 512], BF16, pdp)]
                        ps_g = [ps(f"ps_ga_{g2}", [128, 512], F32, pdp), ps(f"ps_gb_{g2}", [128, 512], F32, pdp)]
                        ps_u = [ps(f"ps_ua_{g2}", [128, 512], F32, pdp), ps(f"ps_ub_{g2}", [128, 512], F32, pdp)]
                        vpd = b.do("vector", lambda e: e.memset(pd_b[:], 1.0))
                        b.wait("scalar", "vector", vpd)
                        s_sq = {}
                        l_x1 = {}
                        for mi in range(8):
                            rbg = g2 * 8 + mi
                            slot = mi % 2
                            if mi >= 2:
                                b.wait("sync", "scalar", s_sq[mi - 2])
                            l_x1[mi] = b.dma("sync", tok_x1r[slot], lambda e, rbg=rbg, slot=slot: e.dma_start(
                                out=x1b[slot][:], in_=scr_x1[rbg * 128:(rbg + 1) * 128, :]))
                            b.wait_tok("scalar", tok_x1r[slot], l_x1[mi])
                            s_sq[mi] = b.do("scalar", lambda e, slot=slot, mi=mi: e.activation(
                                out=h2tok[slot][:], in_=x1b[slot][:], func=AF.Square,
                                accum_out=pd_b[:, mi:mi + 1]))
                        smq = b.s_settle()
                        b.wait("vector", "scalar", smq)
                        v1 = b.do("vector", lambda e: e.tensor_scalar(
                            out=pd_c[:], in0=pd_b[:], scalar1=float(1.0 / D), scalar2=EPS2,
                            op0=ALU.mult, op1=ALU.add))
                        b.wait("scalar", "vector", v1)
                        sA = b.do("scalar", lambda e: e.activation(out=pd_c[:], in_=pd_c[:], func=AF.Ln))
                        b.wait("scalar", "scalar", sA)
                        b.do_("scalar", lambda e: e.activation(
                            out=pd_c[:], in_=pd_c[:], func=AF.Exp, scale=-0.5))
                        smq = b.s_settle()
                        b.wait("vector", "scalar", smq)
                        vst3 = b.do("vector", lambda e, g2=g2: e.tensor_copy(
                            st3[:, g2 * 8:(g2 + 1) * 8], pd_c[:, 0:8]))
                        v_h2evac = {}
                        t_tp3 = {}
                        v_h2 = {}
                        for mi in range(8):
                            rbg = g2 * 8 + mi
                            slot = mi % 2
                            if mi >= 2:
                                b.wait("sync", "vector", v_h2[mi - 2])
                            else:
                                b.wait("sync", "scalar", smq)
                            lx = b.dma("sync", tok_x1r[slot], lambda e, rbg=rbg, slot=slot: e.dma_start(
                                out=x1b[slot][:], in_=scr_x1[rbg * 128:(rbg + 1) * 128, :]))
                            b.wait_tok("vector", tok_x1r[slot], lx)
                            b.wait("vector", "vector", vst3)
                            if mi >= 2:
                                b.wait("vector", "tensor", t_tp3[(mi - 2, 3)])
                            v_h2[mi] = b.do("vector", lambda e, slot=slot, rbg=rbg: e.scalar_tensor_tensor(
                                out=h2tok[slot][:], in0=x1b[slot][:],
                                scalar=st3[:, rbg % NRB: rbg % NRB + 1] if False else st3[:, rbg:rbg + 1],
                                in1=n2w_rep[:], op0=ALU.mult, op1=ALU.mult))
                            b.wait("tensor", "vector", v_h2[mi])
                            for g4 in range(4):
                                pbk = g4 % 2
                                bidx = mi * 4 + g4
                                if bidx >= 2:
                                    b.wait("tensor", "vector", v_h2evac[bidx - 2])
                                tm = 0
                                for jj in range(4):
                                    c = g4 * 4 + jj
                                    fn = lambda e, slot=slot, c=c, pbk=pbk, jj=jj: e.transpose(
                                        ps_tp2[pbk][:, jj * 128:(jj + 1) * 128],
                                        h2tok[slot][:, c * 128:(c + 1) * 128], identb[:])
                                    if jj == 3:
                                        tm = b.do("tensor", fn)
                                    else:
                                        b.do_("tensor", fn)
                                t_tp3[(mi, g4)] = tm
                                b.wait("vector", "tensor", tm)
                                vv = b.do("vector", lambda e, g4=g4, mi=mi, pbk=pbk: e.tensor_copy(
                                    h2T[:].rearrange("p (c f) -> p c f", f=1024)[
                                        :, g4 * 4:(g4 + 1) * 4, mi * 128:(mi + 1) * 128],
                                    ps_tp2[pbk][:].rearrange("p (c f) -> p c f", f=128)))
                                v_h2evac[bidx] = vv
                        h2done = b.cnt["vector"]
                        t_g = {}
                        s_sil = {}
                        v_u = {}
                        for oc in range(64):
                            ws = oc % 2
                            if oc > 1:
                                b.wait("gpsimd", "tensor", t_g[oc - 2])
                            lg = b.dma("gpsimd", tok_wg[ws], lambda e, oc=oc, ws=ws: e.dma_start(
                                out=wg_cb[ws][:].rearrange("p (c o) -> p c o", o=128),
                                in_=I["wgate"][:, oc * 128:(oc + 1) * 128].rearrange(
                                    "(c p) o -> p c o", p=128)))
                            lu = b.dma("gpsimd", tok_wu[ws], lambda e, oc=oc, ws=ws: e.dma_start(
                                out=wu_cb[ws][:].rearrange("p (c o) -> p c o", o=128),
                                in_=I["wup"][:, oc * 128:(oc + 1) * 128].rearrange(
                                    "(c p) o -> p c o", p=128)))
                            b.wait_tok("tensor", tok_wg[ws], lg)
                            b.wait_tok("tensor", tok_wu[ws], lu)
                            b.wait("tensor", "vector", h2done)
                            for s2 in range(2):
                                gx = s2
                                gi = oc * 2 + s2
                                if gi >= 2:
                                    b.wait("tensor", "scalar", s_sil[gi - 2])
                                    b.wait("tensor", "vector", v_u[gi - 2])
                                for c in range(NCH):
                                    b.do_("tensor", lambda e, c=c, s2=s2, gx=gx, ws=ws: e.matmul(
                                        ps_g[gx][:], wg_cb[ws][:, c * 128:(c + 1) * 128],
                                        h2T[:, c * 1024 + s2 * 512: c * 1024 + (s2 + 1) * 512],
                                        start=(c == 0), stop=(c == NCH - 1)))
                                tm = 0
                                for c in range(NCH):
                                    fn = lambda e, c=c, s2=s2, gx=gx, ws=ws: e.matmul(
                                        ps_u[gx][:], wu_cb[ws][:, c * 128:(c + 1) * 128],
                                        h2T[:, c * 1024 + s2 * 512: c * 1024 + (s2 + 1) * 512],
                                        start=(c == 0), stop=(c == NCH - 1))
                                    if c == NCH - 1:
                                        tm = b.do("tensor", fn)
                                    else:
                                        b.do_("tensor", fn)
                                t_g[oc] = tm
                                b.wait("scalar", "tensor", tm)
                                s_sil[gi] = b.do("scalar", lambda e, gx=gx, gi=gi: e.activation(
                                    out=silu_sb[gi % 2][:], in_=ps_g[gx][:], func=AF.Silu))
                                b.wait("vector", "scalar", s_sil[gi])
                                v_u[gi] = b.do("vector", lambda e, oc=oc, s2=s2, gx=gx, gi=gi: e.tensor_mul(
                                    uT[:, oc * 1024 + s2 * 512: oc * 1024 + (s2 + 1) * 512],
                                    silu_sb[gi % 2][:], ps_u[gx][:]))
                        v_uT_last = v_u[127]
                with contextlib.ExitStack() as pdd:
                    wd_t = [sb(f"wd0_{g2}", [128, 4 * 512], BF16, pdd), sb(f"wd1_{g2}", [128, 4 * 512], BF16, pdd)]
                    xcol = [sb(f"xcol0_{g2}", [128, 8 * 512], F32, pdd)]
                    x1col = [sb(f"x1col0_{g2}", [128, 8 * 512], BF16, pdd)]
                    ystage = sb(f"ystage_{g2}", [128, 8 * 512], F32, pdd)
                    ytmp = [sb(f"ytmp0_{g2}", [128, 512], F32, pdd), sb(f"ytmp1_{g2}", [128, 512], F32, pdd)]
                    with contextlib.ExitStack() as pdq:
                        ps_d = [ps(f"ps_d{i}_{g2}", [128, 512], F32, pdq) for i in range(8)]
                        v_yevac = {}
                        t_wd_use = {}
                        yw = 0
                        for nt in range(4):
                            slot = 0
                            if nt >= 1:
                                b.wait("sync", "vector", v_yevac[(nt - 1, 7)])
                            else:
                                b.wait("sync", "vector", v_uT_last)
                            lxc = b.dma("sync", tok_xg[slot], lambda e, g2=g2, nt=nt, slot=slot: e.dma_start(
                                out=xcol[slot][:].rearrange("p (mi d) -> p mi d", d=512),
                                in_=I["x"][g2 * 1024:(g2 + 1) * 1024, nt * 512:(nt + 1) * 512
                                           ].rearrange("(mi p) d -> p mi d", p=128)))
                            l1c = b.dma("sync", tok_x1c[slot], lambda e, g2=g2, nt=nt, slot=slot: e.dma_start(
                                out=x1col[slot][:].rearrange("p (mi d) -> p mi d", d=512),
                                in_=scr_x1[g2 * 1024:(g2 + 1) * 1024, nt * 512:(nt + 1) * 512
                                           ].rearrange("(mi p) d -> p mi d", p=128)))
                            for jb in range(16):
                                js = jb % 2
                                prevkey = (nt, jb - 2) if jb >= 2 else (nt - 1, 14 + jb)
                                if prevkey in t_wd_use:
                                    b.wait("gpsimd", "tensor", t_wd_use[prevkey])
                                else:
                                    b.wait("gpsimd", "tensor", t_g[63])
                                lwd = b.dma("gpsimd", tok_wd[js], lambda e, jb=jb, nt=nt, js=js: e.dma_start(
                                    out=wd_t[js][:].rearrange("p (j d) -> p j d", d=512),
                                    in_=I["wdown"][jb * 512:(jb + 1) * 512, nt * 512:(nt + 1) * 512
                                                   ].rearrange("(j p) d -> p j d", p=128)))
                                b.wait_tok("tensor", tok_wd[js], lwd)
                                if jb == 0:
                                    b.wait("tensor", "vector", v_uT_last)
                                    if nt >= 1:
                                        b.wait("tensor", "vector", v_yevac[(nt - 1, 7)])
                                tm = 0
                                for jj in range(4):
                                    j = jb * 4 + jj
                                    for mi in range(8):
                                        fn = lambda e, j=j, jj=jj, mi=mi, js=js: e.matmul(
                                            ps_d[mi][:],
                                            uT[:, j * 1024 + mi * 128: j * 1024 + (mi + 1) * 128],
                                            wd_t[js][:, jj * 512:(jj + 1) * 512],
                                            start=(j == 0), stop=(j == 63), skip_group_check=True)
                                        if mi == 7 and jj == 3:
                                            tm = b.do("tensor", fn)
                                        else:
                                            b.do_("tensor", fn)
                                t_wd_use[(nt, jb)] = tm
                            b.wait("vector", "tensor", t_wd_use[(nt, 15)])
                            b.wait_tok("vector", tok_xg[slot], lxc)
                            b.wait_tok("vector", tok_x1c[slot], l1c)
                            if nt > 0:
                                b.wait_tok("vector", tok_yw, yw)
                            for mi in range(8):
                                vv1 = b.do("vector", lambda e, mi=mi, slot=slot: e.tensor_add(
                                    ytmp[mi % 2][:], ps_d[mi][:], xcol[slot][:, mi * 512:(mi + 1) * 512]))
                                b.wait("vector", "vector", vv1)
                                v_yevac[(nt, mi)] = b.do("vector", lambda e, mi=mi, slot=slot: e.tensor_add(
                                    ystage[:, mi * 512:(mi + 1) * 512], ytmp[mi % 2][:],
                                    x1col[slot][:, mi * 512:(mi + 1) * 512]))
                            b.wait("sync", "vector", v_yevac[(nt, 7)])
                            yw = b.dma("sync", tok_yw, lambda e, g2=g2, nt=nt: e.dma_start(
                                out=scr_y[g2 * 1024:(g2 + 1) * 1024, nt * 512:(nt + 1) * 512
                                          ].rearrange("(mi p) d -> p mi d", p=128),
                                in_=ystage[:].rearrange("p (mi d) -> p mi d", d=512)))
                return v_uT_last

            v_uT_last = 0
            for g2 in range(2):
                v_uT_last = d_group(g2)
            b.barrier()

        with contextlib.ExitStack() as pe:
            u2T = sb("u2T", [128, 64 * 512], BF16, pe)
            lua_sb = sb("lua_sb", [128, NCH * 16], BF16, pe)
            lda_sb = sb("lda_sb", [128, 64 * 16], BF16, pe)
            ldb_sb = sb("ldb_sb", [16, D], BF16, pe)
            bdn_rep = sb("bdn_rep", [128, D], F32, pe)
            pe_a = sb("pe_a", [128, 512], F32, pe)
            pe_b = sb("pe_b", [128, 512], F32, pe)
            pe_m = sb("pe_m", [128, 512], F32, pe)
            pe_c = sb("pe_c", [128, 512], F32, pe)
            t2T = sb("t2T", [16, 512], BF16, pe)
            tok_el = b.dsem(es)
            tok_y2 = [b.dsem(es), b.dsem(es)]
            tok_wupt = [b.dsem(es), b.dsem(es)]
            tok_lub = [b.dsem(es), b.dsem(es)]
            tok_wdt = [b.dsem(es), b.dsem(es), b.dsem(es)]
            tok_yc = [b.dsem(es), b.dsem(es)]
            tok_out = b.dsem(es)
            for c in range(NCH):
                b.dma("gpsimd", tok_el, lambda e, c=c: e.dma_start(
                    out=lua_sb[:, c * 16:(c + 1) * 16], in_=I["lua"][c * 128:(c + 1) * 128, :]))
            for c in range(64):
                b.dma("gpsimd", tok_el, lambda e, c=c: e.dma_start(
                    out=lda_sb[:, c * 16:(c + 1) * 16], in_=I["lda"][c * 128:(c + 1) * 128, :]))
            b.dma("gpsimd", tok_el, lambda e: e.dma_start(out=ldb_sb[:], in_=I["ldb"]))
            with contextlib.ExitStack() as peb:
                bdn_row = sb("bdn_row", [1, D], F32, peb)
                ps_br = ps("ps_br", [128, 512], F32, peb)
                tok_br = b.dsem(es)
                lbr = b.dma("sync", tok_br, lambda e: e.dma_start(
                    out=bdn_row[:], in_=I["b_dn"].unsqueeze(0)))
                b.wait_tok("tensor", tok_br, lbr)
                vbr = 0
                for q4 in range(4):
                    if q4 > 0:
                        b.wait("tensor", "vector", vbr)
                    tbr = b.do("tensor", lambda e, q4=q4: e.matmul(
                        ps_br[:], ones_row[:], bdn_row[0:1, q4 * 512:(q4 + 1) * 512],
                        start=True, stop=True))
                    b.wait("vector", "tensor", tbr)
                    vbr = b.do("vector", lambda e, q4=q4: e.tensor_copy(
                        bdn_rep[:, q4 * 512:(q4 + 1) * 512], ps_br[:]))
                b.wait("tensor", "vector", vbr)

            t_dn_last_box = [0]
            yblk = [sb("ybkg0", [128, D], F32, pe), sb("ybkg1", [128, D], F32, pe)]
            x1ptok = [sb("x1pg0", [128, D], BF16, pe), sb("x1pg1", [128, D], BF16, pe)]
            est_done = {}

            def e_stats(ge):
                b.do_("vector", lambda e: e.memset(pe_a[:], 1.0))
                vpd = b.do("vector", lambda e: e.memset(pe_b[:], 1.0))
                b.wait("scalar", "vector", vpd)
                b.flush("vector")
                s_sq = {}
                for mi in range(4):
                    rge = ge * 4 + mi
                    slot = mi % 2
                    if mi >= 2:
                        b.wait("sync", "scalar", s_sq[mi - 2])
                        b.wait("sync", "vector", b.cnt["vector"])
                    b.wait_tok("sync", tok_y2[slot])
                    ly = b.dma("sync", tok_y2[slot], lambda e, rge=rge, slot=slot: e.dma_start(
                        out=yblk[slot][:], in_=scr_y[rge * 128:(rge + 1) * 128, :]))
                    b.wait_tok("vector", tok_y2[slot], ly)
                    b.do("vector", lambda e, mi=mi, slot=slot: e.reduce_sum(
                        pe_a[:, mi:mi + 1], yblk[slot][:], axis=mybir.AxisListType.X))
                    b.wait_tok("scalar", tok_y2[slot], ly)
                    s_sq[mi] = b.do("scalar", lambda e, mi=mi, slot=slot: e.activation(
                        out=x1ptok[slot][:], in_=yblk[slot][:], func=AF.Square,
                        accum_out=pe_b[:, mi:mi + 1]))
                smq = b.s_settle()
                b.wait("vector", "scalar", smq)
                b.flush("vector")
                v1 = b.do("vector", lambda e: e.tensor_scalar(
                    out=pe_m[:], in0=pe_a[:], scalar1=float(1.0 / D), scalar2=None, op0=ALU.mult))
                b.do_("vector", lambda e: e.tensor_scalar(
                    out=pe_c[:], in0=pe_b[:], scalar1=float(1.0 / D), scalar2=None, op0=ALU.mult))
                b.wait("vector", "vector", v1)
                v2 = b.do("vector", lambda e: e.tensor_mul(pe_a[:], pe_m[:], pe_m[:]))
                b.wait("vector", "vector", v2)
                v3 = b.do("vector", lambda e: e.tensor_sub(pe_c[:], pe_c[:], pe_a[:]))
                b.wait("vector", "vector", v3)
                vm = b.do("vector", lambda e: e.tensor_scalar(
                    out=pe_c[:], in0=pe_c[:], scalar1=EPS1, scalar2=None, op0=ALU.add))
                b.wait("scalar", "vector", vm)
                sA = b.do("scalar", lambda e: e.activation(out=pe_c[:], in_=pe_c[:], func=AF.Ln))
                b.wait("scalar", "scalar", sA)
                b.do_("scalar", lambda e: e.activation(
                    out=pe_c[:], in_=pe_c[:], func=AF.Exp, scale=-0.5))
                smq = b.s_settle()
                b.wait("vector", "scalar", smq)
                vsm = b.do("vector", lambda e, ge=ge: e.tensor_copy(
                    sy_m[:, ge * 4:(ge + 1) * 4], pe_m[:, 0:4]))
                b.wait("vector", "vector", vsm)
                vsr = b.do("vector", lambda e, ge=ge: e.tensor_copy(
                    sy_rstd[:, ge * 4:(ge + 1) * 4], pe_c[:, 0:4]))
                b.wait("vector", "vector", vsr)
                est_done[ge] = (smq, vsr)

            def e_group(ge):
                for tk in (tok_out, tok_yc[0], tok_yc[1], tok_wdt[0], tok_wdt[1]):
                    b.wait_tok("sync", tk)
                    b.wait_tok("gpsimd", tk)
                    b.wait_tok("vector", tk)
                    b.wait_tok("scalar", tk)
                with contextlib.ExitStack() as peu:
                    x1pT = sb(f"x1pT_{ge}", [128, NCH * 512], BF16, peu)
                    wupt = [sb(f"wupta_{ge}", [128, NCH * 512], BF16, peu),
                            sb(f"wuptb_{ge}", [128, NCH * 512], BF16, peu)]
                    lub_oc = [sb(f"lub0_{ge}", [16, 128], BF16, peu), sb(f"lub1_{ge}", [16, 128], BF16, peu)]
                    tupT = sb(f"tupT_{ge}", [16, 512], BF16, peu)
                    with contextlib.ExitStack() as pep:
                        ps_tp4 = [ps(f"ps_tc0_{ge}", [128, 512], BF16, pep),
                                  ps(f"ps_tc1_{ge}", [128, 512], BF16, pep)]
                        ps_l = ps(f"ps_l_{ge}", [16, 512], F32, pep)
                        ps_y1 = [ps(f"ps_y1a_{ge}", [128, 512], F32, pep),
                                 ps(f"ps_y1b_{ge}", [128, 512], F32, pep)]
                        smq, vsr = est_done[ge]
                        v_xp = {}
                        t_tp5 = {}
                        v_xpev = {}
                        b.wait("vector", "scalar", smq)
                        for mi in range(4):
                            rge = ge * 4 + mi
                            slot = mi % 2
                            if mi >= 2:
                                b.wait("sync", "vector", v_xp[mi - 2])
                            else:
                                b.wait("sync", "scalar", smq)
                                b.wait("sync", "vector", vsr)
                                b.wait_tok("sync", tok_y2[slot])
                            ly = b.dma("sync", tok_y2[slot], lambda e, rge=rge, slot=slot: e.dma_start(
                                out=yblk[slot][:], in_=scr_y[rge * 128:(rge + 1) * 128, :]))
                            b.wait_tok("vector", tok_y2[slot], ly)
                            if mi >= 2:
                                b.wait("vector", "tensor", t_tp5[(mi - 2, 3)])
                            v_xp[mi] = b.do("vector", lambda e, slot=slot, rge=rge: e.tensor_scalar(
                                out=x1ptok[slot][:], in0=yblk[slot][:],
                                scalar1=sy_m[:, rge:rge + 1], scalar2=sy_rstd[:, rge:rge + 1],
                                op0=ALU.subtract, op1=ALU.mult))
                            b.wait("tensor", "vector", v_xp[mi])
                            for g4 in range(4):
                                pbk = g4 % 2
                                bidx = mi * 4 + g4
                                if bidx >= 2:
                                    b.wait("tensor", "vector", v_xpev[bidx - 2])
                                tm = 0
                                for jj in range(4):
                                    c = g4 * 4 + jj
                                    fn = lambda e, slot=slot, c=c, pbk=pbk, jj=jj: e.transpose(
                                        ps_tp4[pbk][:, jj * 128:(jj + 1) * 128],
                                        x1ptok[slot][:, c * 128:(c + 1) * 128], identb[:])
                                    if jj == 3:
                                        tm = b.do("tensor", fn)
                                    else:
                                        b.do_("tensor", fn)
                                t_tp5[(mi, g4)] = tm
                                b.wait("vector", "tensor", tm)
                                vv = b.do("vector", lambda e, g4=g4, mi=mi, pbk=pbk: e.tensor_copy(
                                    x1pT[:].rearrange("p (c f) -> p c f", f=512)[
                                        :, g4 * 4:(g4 + 1) * 4, mi * 128:(mi + 1) * 128],
                                    ps_tp4[pbk][:].rearrange("p (c f) -> p c f", f=128)))
                                v_xpev[bidx] = vv
                        xp_done = b.cnt["vector"]
                        b.wait("tensor", "vector", xp_done)
                        b.wait_tok("tensor", tok_el)
                        tm = 0
                        for c in range(NCH):
                            fn = lambda e, c=c: e.matmul(
                                ps_l[:], lua_sb[:, c * 16:(c + 1) * 16],
                                x1pT[:, c * 512:(c + 1) * 512],
                                start=(c == 0), stop=(c == NCH - 1))
                            if c == NCH - 1:
                                tm = b.do("tensor", fn)
                            else:
                                b.do_("tensor", fn)
                        b.wait("vector", "tensor", tm)
                        v_tup = b.do("vector", lambda e: e.tensor_copy(tupT[:], ps_l[:]))
                        t_fb_use = {}
                        t_oc = {}
                        s_relu = {}
                        lwu = 0
                        for oc in range(64):
                            fb = oc // 4
                            wslot = fb % 2
                            if oc % 4 == 0:
                                if fb > 1:
                                    b.wait("sync", "tensor", t_fb_use[fb - 2])
                                lwu = b.dma("sync", tok_wupt[wslot], lambda e, fb=fb, wslot=wslot: e.dma_start(
                                    out=wupt[wslot][:].rearrange("p (c f) -> p c f", f=512),
                                    in_=scr_wup[fb].rearrange("c p f -> p c f")))
                            slot = oc % 2
                            if oc >= 2:
                                b.wait("gpsimd", "tensor", t_oc[oc - 2])

                            llu = b.dma("gpsimd", tok_lub[slot], lambda e, oc=oc, slot=slot: e.dma_start(
                                out=lub_oc[slot][:], in_=I["lub"][:, oc * 128:(oc + 1) * 128]))
                            b.wait_tok("tensor", tok_wupt[wslot], lwu)
                            b.wait_tok("tensor", tok_lub[slot], llu)
                            if oc == 0:
                                b.wait("tensor", "vector", v_tup)
                                if t_dn_last_box[0]:
                                    b.wait("tensor", "tensor", t_dn_last_box[0]) if False else None
                            if oc >= 2:
                                b.wait("tensor", "scalar", s_relu[oc - 2])
                            for c in range(NCH):
                                b.do_("tensor", lambda e, c=c, oc=oc, slot=slot, wslot=wslot: e.matmul(
                                    ps_y1[slot][:],
                                    wupt[wslot][:, c * 512 + (oc % 4) * 128: c * 512 + (oc % 4) * 128 + 128],
                                    x1pT[:, c * 512:(c + 1) * 512],
                                    start=(c == 0), stop=False))
                            tm = b.do("tensor", lambda e, oc=oc, slot=slot: e.matmul(
                                ps_y1[slot][:], lub_oc[slot][:], tupT[:],
                                start=False, stop=True))
                            if oc % 4 == 3:
                                t_fb_use[fb] = tm
                            t_oc[oc] = tm
                            b.wait("scalar", "tensor", tm)
                            s_relu[oc] = b.do("scalar", lambda e, oc=oc, slot=slot: e.activation(
                                out=u2T[:, oc * 512:(oc + 1) * 512], in_=ps_y1[slot][:],
                                func=AF.Relu, bias=bup_pp[:, oc:oc + 1]))
                        sm_relu = b.s_settle()
                        b.wait("tensor", "scalar", sm_relu)
                        b.wait("tensor", "vector", v_tup)
                        tm = 0
                        for j in range(64):
                            fn = lambda e, j=j: e.matmul(
                                ps_l[:], lda_sb[:, j * 16:(j + 1) * 16],
                                u2T[:, j * 512:(j + 1) * 512],
                                start=(j == 0), stop=(j == 63))
                            if j == 63:
                                tm = b.do("tensor", fn)
                            else:
                                b.do_("tensor", fn)
                        b.wait("vector", "tensor", tm)
                        v_t2 = b.do("vector", lambda e: e.tensor_copy(t2T[:], ps_l[:]))
                        b.wait("tensor", "vector", v_t2)
                        if ge < 3:
                            b.wait("sync", "vector", xp_done)
                            e_stats(ge + 1)
                b.wait("sync", "vector", v_t2)
                b.wait("gpsimd", "vector", v_t2)
                with contextlib.ExitStack() as ped:
                    wdn_t = [sb(f"wdt{i}_{ge}", [128, 4 * 512], BF16, ped) for i in range(3)]
                    ycol = [sb(f"ycl0_{ge}", [128, 4 * 512], F32, ped),
                            sb(f"ycl1_{ge}", [128, 4 * 512], F32, ped)]
                    etmp = [sb(f"etp0_{ge}", [128, 512], F32, ped),
                            sb(f"etp1_{ge}", [128, 512], F32, ped)]
                    outst = sb(f"outst_{ge}", [128, 4 * D], F32, ped)
                    with contextlib.ExitStack() as peq:
                        ps_d = [ps(f"ps_e{i}_{ge}", [128, 512], F32, peq) for i in range(4)]
                        v_oev = {}
                        t_wd_use = {}
                        t_dn = 0
                        for nt in range(4):
                            slot = nt % 2
                            if nt >= 2:
                                b.wait("sync", "vector", v_oev[(nt - 2, 3)])
                            lyc = b.dma("sync", tok_yc[slot], lambda e, ge=ge, nt=nt, slot=slot: e.dma_start(
                                out=ycol[slot][:].rearrange("p (mi d) -> p mi d", d=512),
                                in_=scr_y[ge * 512:(ge + 1) * 512, nt * 512:(nt + 1) * 512
                                          ].rearrange("(mi p) d -> p mi d", p=128)))
                            for jb in range(16):
                                js = jb % 3
                                prevkey = (nt, jb - 3) if jb >= 3 else (nt - 1, {0: 15, 1: 13, 2: 14}[jb])
                                if prevkey in t_wd_use:
                                    b.wait("sync", "tensor", t_wd_use[prevkey])
                                lwd = b.dma("sync", tok_wdt[js], lambda e, jb=jb, nt=nt, js=js: e.dma_start(
                                    out=wdn_t[js][:].rearrange("p (j d) -> p j d", d=512),
                                    in_=scr_wdn[nt, jb * 4:(jb + 1) * 4].rearrange("j p d -> p j d")))
                                b.wait_tok("tensor", tok_wdt[js], lwd)
                                if jb == 0 and nt >= 1:
                                    b.wait("tensor", "vector", v_oev[(nt - 1, 3)])
                                tm = 0
                                for jj in range(4):
                                    j = jb * 4 + jj
                                    for mi in range(4):
                                        fn = lambda e, j=j, jj=jj, mi=mi, js=js: e.matmul(
                                            ps_d[mi][:],
                                            u2T[:, j * 512 + mi * 128: j * 512 + (mi + 1) * 128],
                                            wdn_t[js][:, jj * 512:(jj + 1) * 512],
                                            start=(j == 0), stop=False, skip_group_check=True)
                                        if mi == 3 and jj == 3:
                                            tm = b.do("tensor", fn)
                                        else:
                                            b.do_("tensor", fn)
                                t_wd_use[(nt, jb)] = tm
                            tm = 0
                            for mi in range(4):
                                fn = lambda e, mi=mi, nt=nt: e.matmul(
                                    ps_d[mi][:], t2T[:, mi * 128:(mi + 1) * 128],
                                    ldb_sb[:, nt * 512:(nt + 1) * 512],
                                    start=False, stop=True, skip_group_check=True)
                                if mi == 3:
                                    tm = b.do("tensor", fn)
                                else:
                                    b.do_("tensor", fn)
                            t_dn = tm
                            b.wait("vector", "tensor", tm)
                            b.wait_tok("vector", tok_yc[slot], lyc)
                            for mi in range(4):
                                vv1 = b.do("vector", lambda e, mi=mi, nt=nt: e.tensor_add(
                                    etmp[mi % 2][:], ps_d[mi][:],
                                    bdn_rep[:, nt * 512:(nt + 1) * 512]))
                                b.wait("vector", "vector", vv1)
                                v_oev[(nt, mi)] = b.do("vector", lambda e, mi=mi, nt=nt, slot=slot: e.tensor_add(
                                    outst[:, mi * D + nt * 512: mi * D + (nt + 1) * 512],
                                    etmp[mi % 2][:], ycol[slot][:, mi * 512:(mi + 1) * 512]))
                        b.wait("sync", "vector", v_oev[(3, 3)])
                        for mi in range(4):
                            b.dma("sync", tok_out, lambda e, ge=ge, mi=mi: e.dma_start(
                                out=out[(ge * 4 + mi) * 128:(ge * 4 + mi + 1) * 128, :],
                                in_=outst[:, mi * D:(mi + 1) * D]))
                        t_dn_last_box[0] = t_dn

            b.wait("sync", "tensor", b.cnt["tensor"])
            e_stats(0)
            for ge in range(4):
                e_group(ge)
            b.barrier()

        if "kq" in taps:
            t = add_tap("kq", [128, 2 * NCH * S // 2 * 2], BF16) if False else add_tap("kq", [128, NCH * S], BF16)
            b.dma("sync", tok_tap, lambda e, t=t: e.dma_start(out=t[:], in_=big_b[:]))
        if "hT" in taps:
            t = add_tap("hT", [128, NCH * S], BF16)
            b.dma("sync", tok_tap, lambda e, t=t: e.dma_start(out=t[:], in_=big_a[:]))
        if "x1" in taps:
            t = add_tap("x1", [S, D], BF16)
            b.dma("sync", tok_tap, lambda e, t=t: e.dma_start(out=t[:], in_=scr_x1[:]))
        if "scrv" in taps:
            t = add_tap("scrv", [S, D], BF16)
            b.dma("sync", tok_tap, lambda e, t=t: e.dma_start(out=t[:], in_=scr_v[:]))

        if "hinT" in taps:
            t = add_tap("hinT", [128, NCH * S])
            with contextlib.ExitStack() as tp:
                tmp = sb("tap_tmp", [128, 2048], F32, tp)
                for c in range(NCH):
                    vmt = b.do("vector", lambda e, c=c: e.tensor_copy(
                        tmp[:], hinT[:, c * S:(c + 1) * S]))
                    b.wait("sync", "vector", vmt)
                    smt = b.dma("sync", tok_tap, lambda e, c=c, t=t: e.dma_start(
                        out=t[:, c * S:(c + 1) * S], in_=tmp[:]))
                    b.wait_tok("vector", tok_tap, smt)
            b.barrier()
        if "stats" in taps:
            t = add_tap("stats", [128, 3 * NRB])
            b.dma("sync", tok_tap, lambda e, t=t: e.dma_start(out=t[:, 0:16], in_=st_m[:]))
            b.dma("sync", tok_tap, lambda e, t=t: e.dma_start(out=t[:, 16:32], in_=st_rstd[:]))
            b.dma("sync", tok_tap, lambda e, t=t: e.dma_start(out=t[:, 32:48], in_=st_c1[:]))
        if "gcol" in taps:
            t = add_tap("gcol", [128, 128])
            b.dma("sync", tok_tap, lambda e, t=t: e.dma_start(out=t[:], in_=Gcol[:]))

        b.barrier()
        for tok in b.all_toks:
            b.wait_tok("vector", tok)
        b.mark("vector")

        with nc.Block() as block:
            b.emit(block)
    return nc, tap_outs


def make_consts():
    ident = np.eye(128, dtype=np.float32)
    dm = np.zeros((128, 4 * 512), np.float32)
    tl = np.arange(128)[:, None]
    sl = np.arange(512)[None, :]
    for k in range(4):
        dm[:, k * 512:(k + 1) * 512] = (128 * k + tl <= sl).astype(np.float32)
    return {
        "c_ident": ident, "c_dmask": dm,
        "c_ones_row": np.ones((1, 128), np.float32),
        "c_ones_col": np.ones((128, 1), np.float32),
    }


def make_in_maps(inputs):
    x = np.ascontiguousarray(np.asarray(inputs["L0_attn_x"], dtype=np.float32))
    w = {
        "n1w": inputs["L0_attn_norm1_w"], "wq": inputs["L0_attn_wq"],
        "wk": inputs["L0_attn_wk"], "wv": inputs["L0_attn_wv"],
        "wig": inputs["L0_attn_w_ig"], "big": inputs["L0_attn_b_ig"],
        "wfg": inputs["L0_attn_w_fg"], "bfg": inputs["L0_attn_b_fg"],
        "wog": inputs["L0_attn_w_og"], "mhw": inputs["L0_attn_mh_w"],
        "wout": inputs["L0_attn_w_out"], "n2w": inputs["L0_attn_norm2_w"],
        "wgate": inputs["L0_attn_w_gate"], "wup": inputs["L0_attn_w_up"],
        "wdown": inputs["L0_attn_w_down"],
        "q_up": inputs["L0_ffn_w_up_q"], "s_up": inputs["L0_ffn_w_up_scale"],
        "b_up": inputs["L0_ffn_b_up"],
        "lua": inputs["L0_ffn_w_up_lora_a"], "lub": inputs["L0_ffn_w_up_lora_b"],
        "q_dn": inputs["L0_ffn_w_down_q"], "s_dn": inputs["L0_ffn_w_down_scale"],
        "b_dn": inputs["L0_ffn_b_down"],
        "lda": inputs["L0_ffn_w_down_lora_a"], "ldb": inputs["L0_ffn_w_down_lora_b"],
    }
    w = {k: np.ascontiguousarray(np.asarray(v)) for k, v in w.items()}
    w.update(make_consts())
    maps = []
    for core in range(8):
        m = dict(w)
        m["x"] = np.ascontiguousarray(x[core])
        maps.append(m)
    return maps


def kernel(**inputs):
    maps = make_in_maps(inputs)
    nc, _ = build_core()
    res = run_bass_kernel_spmd(nc, maps, core_ids=list(range(8)))
    return np.stack([r["out"] for r in res.results], axis=0)
